# version 13
# baseline (speedup 1.0000x reference)
from contextlib import ExitStack
import numpy as np
import ml_dtypes
import concourse.bass as bass
import concourse.mybir as mybir

F32 = mybir.dt.float32
BF16 = mybir.dt.bfloat16
AF = mybir.ActivationFunctionType
ALU = mybir.AluOpType
AX = mybir.AxisListType
BF = ml_dtypes.bfloat16

NCORES = 8
TC = 2048
HALO = 4096
TE = TC + HALO
NT = TC // 128
NTH = TE // 128
DM = 1024
NH, HD = 16, 64
BANDS = [0, 1, 2, 3, 4, 8, 16, 32, 64, 128, 256, 512, 1024, 2048, 4096]
NBANDS = 15
BGROUPS = [(0, [0, 1, 2, 3, 4]), (5, [8, 16]), (7, [32, 64]),
           (9, [128, 256]), (11, [512, 1024]), (13, [2048, 4096])]
PV_CUM = []
_c = 0
for _gi, (_pos, _ds) in enumerate(BGROUPS):
    _c += (1 + 2 * (len(_ds) - 1)) if _gi == 0 else 2 * len(_ds)
    PV_CUM.append(_c)
PV_OPS_I = PV_CUM[-1] + 1

DECAY = 0.9
LN_EPS = 1e-5
NEXP, PSLOT = 64, 32
NP = NEXP * PSLOT
DIN = 2 * DM
DH = 4096


def prep_inputs(inputs):
    f = lambda a: np.ascontiguousarray(np.asarray(a), dtype=np.float32)
    sh = {}
    x = f(inputs["L0_attn_x"]).reshape(2 * 8192, DM)
    bias_cat = []
    for p in ("q", "k", "v", "o"):
        Wv = f(inputs[f"L0_attn_{p}_Wv"])
        Wr = f(inputs[f"L0_attn_{p}_Wr"])
        Wl = f(inputs[f"L0_attn_{p}_Wl"])
        sh[f"u_{p}"] = np.concatenate([Wv, Wr, Wl], 0).reshape(22 * 128, 128)
        bias_cat.append(f(inputs[f"L0_attn_{p}_b"]))
    sh["ubias"] = np.concatenate(bias_cat).reshape(1, 4 * DM)
    sh["w1T"] = f(inputs["L0_ffn_w1_w"].T)
    sh["w2T"] = f(inputs["L0_ffn_w2_w"].T)
    sh["h1T"] = f(inputs["L0_ffn_h1_w"].T)
    sh["h2T"] = f(inputs["L0_ffn_h2_w"].T)
    sh["w3T"] = f(inputs["L0_ffn_w3_w"].T)
    cm = lambda s: f(np.asarray(s).reshape(-1, 128).T)
    sh["s1cm"] = cm(inputs["L0_ffn_w1_s"])
    sh["s2cm"] = cm(inputs["L0_ffn_w2_s"])
    sh["h1scm"] = cm(inputs["L0_ffn_h1_s"])
    sh["h2scm"] = cm(inputs["L0_ffn_h2_s"])
    sh["s3row"] = f(inputs["L0_ffn_w3_s"]).reshape(1, DM)
    sh["s3cm"] = cm(inputs["L0_ffn_w3_s"])
    sh["Win"] = f(inputs["L1_attn_Win"])
    sh["bincm"] = cm(inputs["L1_attn_bin_"])
    sh["Bp"] = f(inputs["L1_attn_Bp"])
    sh["CpT"] = f(inputs["L1_attn_Cp"].T)
    sh["Dpcm"] = cm(inputs["L1_attn_Dp"])
    sh["Wout"] = f(inputs["L1_attn_Wout"])
    sh["boutrow"] = f(inputs["L1_attn_bout"]).reshape(1, DM)
    sh["phiT"] = f(inputs["L1_ffn_phi_w"].T)
    sh["phibcm"] = cm(inputs["L1_ffn_phi_b"])
    sh["phibrow"] = f(inputs["L1_ffn_phi_b"]).reshape(1, NP)
    ew = f(inputs["L1_ffn_expert_w"])
    eb = f(inputs["L1_ffn_expert_b"])
    sh["ident"] = np.eye(128, dtype=BF)
    sh["ones1x128"] = np.ones((1, 128), dtype=BF)
    sh["oav"] = np.full((128, 128), 1.0 / DM, dtype=BF)
    sh["sel32"] = (np.arange(128)[:, None] % 32 == np.arange(32)[None, :]).astype(BF)
    sh["rep32"] = (np.arange(32)[:, None] == np.arange(128)[None, :] % 32).astype(BF)
    t_loc = np.arange(TC)
    sh["decaycm"] = (DECAY ** (t_loc + 1.0)).astype(np.float32).reshape(NT, 128).T

    per_core = []
    for c in range(NCORES):
        d = {}
        r0 = c * TC
        xe = np.zeros((TE, DM), dtype=np.float32)
        lo = r0 - HALO
        b0 = (c // 4) * 8192
        lo_c = max(lo, b0)
        xe[lo_c - lo:] = x[lo_c:r0 + TC]
        d["x_ext"] = xe
        tb = (c % 4) * TC + np.arange(TC)
        m = np.where(tb[:, None] - np.asarray(BANDS)[None, :] >= 0, 0.0, -1e30)
        d["masks"] = np.ascontiguousarray(m.astype(np.float32).reshape(NT, 128, NBANDS))
        selr = np.zeros((128, 4), dtype=np.float32)
        if c % 4 > 0:
            selr[:, (c % 4) - 1] = 1.0
        d["selread"] = selr
        d["ew_shard"] = np.ascontiguousarray(ew[8 * c:8 * c + 8])
        d["eb_shard"] = np.ascontiguousarray(eb[8 * c:8 * c + 8].reshape(1, 8 * DM))
        per_core.append(d)
    return sh, per_core


def V(handle_or_ap, off, dims):
    a = handle_or_ap if isinstance(handle_or_ap, bass.AP) else handle_or_ap.ap()
    return bass.AP(a.tensor, off, [list(d) for d in dims])


CHAINED = {
    "tensor_copy", "tensor_tensor", "tensor_add", "tensor_sub", "tensor_mul",
    "tensor_max", "tensor_scalar", "tensor_single_scalar", "tensor_scalar_add",
    "tensor_scalar_sub", "tensor_scalar_mul", "tensor_scalar_max",
    "tensor_scalar_min", "tensor_relu", "scalar_tensor_tensor", "tensor_reduce",
    "reduce_sum", "reduce_max", "reciprocal", "memset", "memzero",
    "tensor_tensor_scan", "select", "transpose", "bn_stats", "bn_aggr",
    "activation", "copy", "mul", "add", "square", "sqrt", "sign", "iota",
    "affine_select",
}
ENG_SEM = {"vector": "dve", "scalar": "act", "gpsimd": "pool"}


class ChainEng:
    def __init__(self, ctx, eng, sem):
        self._ctx, self._eng, self._sem = ctx, eng, sem

    def __getattr__(self, name):
        f = getattr(self._eng, name)
        if name not in CHAINED:
            return f
        ctx, eng, sem = self._ctx, self._eng, self._sem

        def wrap(*a, **k):
            if ctx.cnt[sem] > 0:
                eng.wait_ge(ctx.sems[sem], ctx.cnt[sem])
            inst = f(*a, **k)
            ctx.inc(inst, sem)
            return inst
        return wrap


class Ctx:
    def __init__(self, nc, block, sems):
        self.nc, self.block, self.sems = nc, block, sems
        self.cnt = {k: 0 for k in sems}
        self.nbar = 0

    def inc(self, inst, s, n=1):
        self.cnt[s] += n
        inst.then_inc(self.sems[s], n)
        return self.cnt[s]

    def w(self, eng, s, val):
        if val > 0:
            eng.wait_ge(self.sems[s], val)

    def wp(self, eng, sv):
        if sv is not None:
            s, v = sv
            if v > 0:
                eng.wait_ge(self.sems[s], v)

    def dma(self, eng, out, in_, sem):
        inst = eng.dma_start(out=out, in_=in_)
        return (sem, self.inc(inst, sem, 16))

    def sec(self, name, fn):
        if name in ENG_SEM:
            sem = ENG_SEM[name]
            getattr(self.block, name)(lambda e: fn(ChainEng(self, e, sem)))
        else:
            getattr(self.block, name)(fn)

    def barrier(self, drains=()):
        self.nbar += 1
        target = 5 * self.nbar
        for name, own in (("sync", None), ("tensor", "pe"), ("vector", "dve"),
                          ("scalar", "act"), ("gpsimd", "pool")):
            def _f(eng, t=target, own=own, first=(name == "sync")):
                if first:
                    for sv in drains:
                        self.wp(eng, sv)
                if own is not None and self.cnt[own] > 0:
                    eng.wait_ge(self.sems[own], self.cnt[own])
                eng.nop().then_inc(self.sems["bar"], 1)
                eng.wait_ge(self.sems["bar"], t)
            getattr(self.block, name)(_f)
        self.cnt["bar"] = target


def sal(ctx, es, name, shape, dt):
    free_bytes = int(np.prod(shape[1:])) * (2 if dt == BF16 else 4)
    ab = 2048 if free_bytes >= 8192 else 32
    return es.enter_context(ctx.nc.sbuf_tensor(name, shape, dt, align_bytes=ab))


def pal(ctx, es, name, shape, dt):
    return es.enter_context(ctx.nc.psum_tensor(name, shape, dt))


class LNState:
    def __init__(self, ctx, es, tag):
        self.scr = sal(ctx, es, f"lnscr_{tag}", [128, 1024], F32)
        self.stat = sal(ctx, es, f"lnstat_{tag}", [128, 8], F32)
        self.last_var_dve = 0
        self.last_sq_act = 0


def emit_ln(ctx, ln, src_ap, dst_ap, ready, res_add_ap=None, dst2=None,
            norm_waits=()):
    st = ln.stat
    inv = 1.0 / DM

    def _act1(e, ready=tuple(ready), war=ln.last_var_dve):
        for s_, v_ in ready:
            ctx.w(e, s_, v_)
        ctx.w(e, "dve", war)
        e.activation(out=ln.scr[:, :], in_=src_ap, func=AF.Square,
                     accum_out=st[:, 1:2])
    ctx.sec("scalar", _act1)
    ln.last_sq_act = a_sq = ctx.cnt["act"]

    def _vec1(e, ready=tuple(ready), a=a_sq):
        for s_, v_ in ready:
            ctx.w(e, s_, v_)
        e.reduce_sum(st[:, 0:1], src_ap, AX.X)
        e.tensor_scalar(st[:, 2:3], st[:, 0:1], inv, None, ALU.mult)
        ctx.w(e, "act", a)
        e.tensor_mul(st[:, 3:4], st[:, 2:3], st[:, 2:3])
        e.scalar_tensor_tensor(st[:, 4:5], st[:, 1:2], inv, st[:, 3:4],
                               ALU.mult, ALU.subtract)
    ctx.sec("vector", _vec1)
    d_var = ctx.cnt["dve"]
    ln.last_var_dve = d_var

    def _act2(e, d=d_var):
        ctx.w(e, "dve", d)
        e.activation(out=st[:, 6:7], in_=st[:, 4:5], func=AF.Sqrt,
                     bias=ctx.eps[:, 0:1])
    ctx.sec("scalar", _act2)
    a_rstd = ctx.cnt["act"]

    def _vec2(e, a=a_rstd, nw=tuple(norm_waits)):
        ctx.w(e, "act", a)
        for s_, v_ in nw:
            ctx.w(e, s_, v_)
        e.reciprocal(st[:, 5:6], st[:, 6:7])
        e.tensor_scalar(dst_ap, src_ap, st[:, 2:3], st[:, 5:6],
                        ALU.subtract, ALU.mult)
        if res_add_ap is not None:
            e.tensor_add(dst2, dst_ap, res_add_ap)
    ctx.sec("vector", _vec2)
    return ctx.cnt["dve"]


def build(phases=("l0a", "l0f", "l1a", "l1f")):
    nc = bass.Bass("TRN2", target_bir_lowering=False)

    x_ext = nc.dram_tensor("x_ext", [TE, DM], F32, kind="ExternalInput").ap()
    y_out = nc.dram_tensor("y_out", [TC, DM], F32, kind="ExternalOutput").ap()
    xdr = [nc.dram_tensor(f"xdr{i}", [TC, DM], F32).ap() for i in range(2)]
    kv_dr = nc.dram_tensor("kv_dr", [TE, 2 * DM], BF16).ap()
    uag_in = nc.dram_tensor("uag_in", [128, 16], F32).ap()
    uag_out = nc.dram_tensor("uag_out", [512, 16], F32).ap()
    xp_in = nc.dram_tensor("xp_in", [NP, DM], F32).ap()
    xp_out = nc.dram_tensor("xp_out", [NP // NCORES, DM], F32).ap()
    c_dr = nc.dram_tensor("c_dr", [NP, TC], BF16).ap()
    yt_in = nc.dram_tensor("yt_in", [8 * PSLOT, DM], BF16).ap()
    yt_out = nc.dram_tensor("yt_out", [NP, DM], BF16, addr_space="Shared" if "l1f" in phases else "Local").ap()

    WSPECS = {"ubias": ([1, 4 * DM], F32), "w1T": ([DM, DH], F32),
              "w2T": ([DM, DH], F32), "h1T": ([DH, DH], F32),
              "h2T": ([DH, DH], F32), "w3T": ([DH, DM], F32),
              "s1cm": ([128, 32], F32), "s2cm": ([128, 32], F32),
              "h1scm": ([128, 32], F32), "h2scm": ([128, 32], F32),
              "s3row": ([1, DM], F32), "s3cm": ([128, 8], F32),
              "Win": ([DM, DIN], F32), "bincm": ([128, 16], F32),
              "Bp": ([DIN, 16], F32), "CpT": ([DIN, 16], F32),
              "Dpcm": ([128, 16], F32), "Wout": ([DIN, DM], F32),
              "boutrow": ([1, DM], F32), "phiT": ([DM, NP], F32),
              "phibcm": ([128, 16], F32), "phibrow": ([1, NP], F32),
              "decaycm": ([128, NT], F32), "ew_shard": ([8, DM, DM], F32),
              "eb_shard": ([1, 8 * DM], F32), "masks": ([NT, 128, NBANDS], F32),
              "selread": ([128, 4], F32), "ident": ([128, 128], BF16),
              "ones1x128": ([1, 128], BF16), "oav": ([128, 128], BF16),
              "sel32": ([128, 32], BF16), "rep32": ([32, 128], BF16)}
    for p in ("q", "k", "v", "o"):
        WSPECS[f"u_{p}"] = ([22 * 128, 128], F32)

    class LazyW(dict):
        def __missing__(self, name):
            shp, dt = WSPECS[name]
            ap = nc.dram_tensor(name, list(shp), dt, kind="ExternalInput").ap()
            self[name] = ap
            return ap
    W = LazyW()
    nc._mk_W = W

    sems, semcm = {}, []
    for s in ("d0", "d1", "d2", "d3", "d4", "d5", "d6", "d7",
              "pe", "dve", "act", "pool", "cc", "bar"):
        cm_ = nc.semaphore(s)
        sems[s] = cm_.__enter__()
        semcm.append(cm_)
    block_cm = nc.Block()
    block = block_cm.__enter__()
    ctx = Ctx(nc, block, sems)

    ident = nc.alloc_sbuf_tensor("sb_ident", [128, 128], BF16)
    ones1 = nc.alloc_sbuf_tensor("sb_ones1", [1, 128], BF16)

    ctx.eps = nc.alloc_sbuf_tensor("sb_eps", [128, 1], F32)

    def _vprolog(e):
        e.memset(ctx.eps[:, :], LN_EPS)
    ctx.sec("vector", _vprolog)

    def _prolog(e):
        ctx.dma(e, ident[:, :], W["ident"][:, :], "d7")
        ctx.dma(e, ones1[:, :], W["ones1x128"][:, :], "d7")
        e.wait_ge(sems["d7"], ctx.cnt["d7"])
    ctx.sec("gpsimd", _prolog)
    ctx.prolog_d7 = ctx.cnt["d7"]
    ctx.barrier()

    cur = None
    if "l0a" in phases:
        emit_l0a(ctx, W, x_ext, xdr[0], kv_dr, ident, ones1)
        cur = xdr[0]
    if "l0f" in phases:
        src = cur if cur is not None else V(x_ext, HALO * DM, [[DM, TC], [1, DM]])
        emit_l0f(ctx, W, src, xdr[1], ident)
        cur = xdr[1]
    if "l1a" in phases:
        src = cur if cur is not None else V(x_ext, HALO * DM, [[DM, TC], [1, DM]])
        emit_l1a(ctx, W, src, xdr[0], ident, ones1, uag_in, uag_out)
        cur = xdr[0]
    if "l1f" in phases:
        src = cur if cur is not None else V(x_ext, HALO * DM, [[DM, TC], [1, DM]])
        emit_l1f(ctx, W, src, y_out, ident, ones1, xp_in, xp_out, c_dr,
                 yt_in, yt_out)
        cur = None
    if cur is not None:
        emit_copyout(ctx, cur, y_out)

    ctx.barrier()
    block_cm.__exit__(None, None, None)
    for cm_ in semcm:
        cm_.__exit__(None, None, None)
    return nc


def emit_copyout(ctx, src_dr, y_out):
    nc = ctx.nc
    es = ExitStack()
    t = sal(ctx, es, "cp_t", [128, DM], F32)
    first = [True]
    for s in range(NT):
        def _m(e, s=s):
            if first[0]:
                for sv in getattr(ctx, "pending", []):
                    ctx.wp(e, sv)
                first[0] = False
            sv = ctx.dma(e, t[:, :], src_dr[128 * s:128 * (s + 1), :], "d0")
            ctx.wp(e, sv)
            sv2 = ctx.dma(e, y_out[128 * s:128 * (s + 1), :], t[:, :], "d0")
            ctx.wp(e, sv2)
        ctx.sec("sync", _m)
    es.close()


def emit_l0a(ctx, W, x_ext, x1_dr, kv_dr, ident, ones1):
    nc = ctx.nc
    es0 = ExitStack()
    esA = ExitStack()
    qsb = sal(ctx, es0, "a_qsb", [128, NT * DM], BF16)
    wu = {"o": sal(ctx, es0, "a_wu_o", [128, 22 * 128], BF16)}
    obias = sal(ctx, es0, "a_obias", [1, DM], BF16)
    xt = [sal(ctx, esA, f"a_xt{i}", [128, DM], F32) for i in range(3)]
    lnt = [sal(ctx, esA, f"a_lnt{i}", [128, DM], BF16) for i in range(3)]
    lnf = [sal(ctx, esA, f"a_lnf{i}", [128, DM], BF16) for i in range(2)]
    kvst = [sal(ctx, esA, f"a_kvst{i}", [128, 2 * DM], BF16) for i in range(3)]
    for p in ("q", "k", "v"):
        wu[p] = sal(ctx, esA, f"a_wu_{p}", [128, 22 * 128], BF16)
    bbc = sal(ctx, esA, "a_bbc_qkv", [128, 3 * DM], BF16)
    ubias = sal(ctx, esA, "sb_ubias", [1, 4 * DM], BF16)
    ln = LNState(ctx, esA, "a")
    ptr = [pal(ctx, esA, f"a_ptr{i}", [128, DM], BF16) for i in range(2)]
    pj = {p: pal(ctx, esA, f"a_p{p}", [128, DM], F32) for p in ("q", "k", "v")}

    def _pl(e):
        ctx.dma(e, ubias[:, :], W["ubias"][:, :], "d7")
        for p in ("q", "k", "v", "o"):
            src = V(W[f"u_{p}"], 0, [[128, 128], [128 * 128, 22], [1, 128]])
            dst = wu[p].ap().rearrange("p (j c) -> p j c", j=22, c=128)
            ctx.dma(e, dst, src, "d7")
        e.wait_ge(ctx.sems["d7"], ctx.cnt["d7"])
    ctx.sec("gpsimd", _pl)
    d7_w = ctx.cnt["d7"]

    prev_evac = 0
    for j in range(3):
        def _pe(e, j=j, pv=prev_evac):
            if j == 0:
                ctx.w(e, "d7", d7_w)
            ctx.w(e, "dve", pv)
            last = None
            for h in range(2):
                last = e.matmul(pj["v"][:, 512 * h:512 * (h + 1)], ones1[0:1, :],
                                ubias[0:1, 1024 * j + 512 * h:1024 * j + 512 * (h + 1)],
                                start=True, stop=True)
            ctx.inc(last, "pe")
        ctx.sec("tensor", _pe)
        pe_now = ctx.cnt["pe"]

        def _ve(e, j=j, pe_now=pe_now):
            ctx.w(e, "pe", pe_now)
            if j == 0:
                e.tensor_copy(obias[0:1, :], ubias[0:1, 3 * DM:4 * DM])
            e.tensor_copy(bbc[:, 1024 * j:1024 * (j + 1)], pj["v"][:, :])
        ctx.sec("vector", _ve)
        prev_evac = ctx.cnt["dve"]

    ln_done = [0] * NTH
    sq_act = [0] * NTH
    pe_tr = [0] * NTH
    dve_lnf = [0] * NTH
    pe_proj = [0] * NTH
    dve_evac = [0] * NTH
    st_kv = [None] * NTH
    ld_x = [None] * NTH

    for s in range(NTH + 1):
        own = s >= (NTH - NT)
        par = s % 3
        J = s - 1

        if s < NTH:
            def _load(e, s=s, par=par):
                if s >= 3:
                    ctx.w(e, "dve", ln_done[s - 3])
                    ctx.w(e, "act", sq_act[s - 3])
                ld_x[s] = ctx.dma(e, xt[par][:, :],
                                  x_ext[128 * s:128 * (s + 1), :], f"d{s % 2}")
            ctx.sec("sync" if s % 2 == 0 else "gpsimd", _load)

        if s >= 1:
            jown = J >= (NTH - NT)

            def _evac(e, J=J, jown=jown):
                ctx.w(e, "pe", pe_proj[J])
                if J >= 3:
                    ctx.wp(e, st_kv[J - 3])
                e.tensor_add(kvst[J % 3][:, 0:DM], pj["k"][:, :],
                             bbc[:, DM:2 * DM])
                e.tensor_add(kvst[J % 3][:, DM:2 * DM], pj["v"][:, :],
                             bbc[:, 2 * DM:3 * DM])
                if jown:
                    so = J - (NTH - NT)
                    e.tensor_add(qsb[:, DM * so:DM * (so + 1)], pj["q"][:, :],
                                 bbc[:, 0:DM])
            ctx.sec("vector", _evac)
            dve_evac[J] = ctx.cnt["dve"]

            def _st(e, J=J):
                ctx.w(e, "dve", dve_evac[J])
                st_kv[J] = ctx.dma(e, kv_dr[128 * J:128 * (J + 1), :],
                                   kvst[J % 3][:, :], f"d{2 + J % 2}")
            ctx.sec("scalar" if J % 2 == 0 else "gpsimd", _st)

        if s < NTH:
            d = emit_ln(ctx, ln, xt[par][:, :], lnt[par][:, :], [ld_x[s]],
                        norm_waits=([("pe", pe_tr[s - 3])] if s >= 3 else ()))
            ln_done[s] = d
            sq_act[s] = ln.last_sq_act

            def _tr(e, s=s, par=par, d=d):
                ctx.w(e, "dve", d)
                if s >= 2:
                    ctx.w(e, "dve", dve_lnf[s - 2])
                last = None
                for b in range(8):
                    last = e.transpose(ptr[s % 2][:, 128 * b:128 * (b + 1)],
                                       lnt[par][:, 128 * b:128 * (b + 1)],
                                       ident[:, :])
                ctx.inc(last, "pe")
            ctx.sec("tensor", _tr)
            pe_tr[s] = ctx.cnt["pe"]

            def _lnf(e, s=s):
                ctx.w(e, "pe", pe_tr[s])
                if s >= 2:
                    ctx.w(e, "pe", pe_proj[s - 2])
                e.tensor_copy(lnf[s % 2][:, :], ptr[s % 2][:, :])
            ctx.sec("vector", _lnf)
            dve_lnf[s] = ctx.cnt["dve"]

            projs = ("q", "k", "v") if own else ("k", "v")

            def _proj(e, s=s, projs=projs):
                ctx.w(e, "dve", dve_lnf[s])
                if s >= 1:
                    ctx.w(e, "dve", dve_evac[s - 1])
                started = {p: [False, False] for p in projs}
                last = None
                for bp in range(8):
                    lhsT = lnf[s % 2][:, 128 * bp:128 * (bp + 1)]
                    for p in projs:
                        tri = [(bp, bp)]
                        if bp + 1 < 8:
                            tri.append((bp + 1, 8 + bp))
                        if bp - 1 >= 0:
                            tri.append((bp - 1, 15 + bp - 1))
                        for (b, jj) in tri:
                            bank = b // 4
                            st = not started[p][bank]
                            started[p][bank] = True
                            last = e.matmul(pj[p][:, 128 * b:128 * (b + 1)],
                                            lhsT,
                                            wu[p][:, 128 * jj:128 * (jj + 1)],
                                            start=st, stop=False,
                                            skip_group_check=True)
                ctx.inc(last, "pe")
            ctx.sec("tensor", _proj)
            pe_proj[s] = ctx.cnt["pe"]

    d2_tot, d3_tot = ctx.cnt["d2"], ctx.cnt["d3"]
    ctx.barrier(drains=st_kv[-2:])
    esA.close()

    esB = ExitStack()
    CHG = [[(0, [0, 1, 2, 3, 4])],
           [(5, [8, 16]), (7, [32, 64]), (9, [128])],
           [(10, [256]), (11, [512, 1024]), (13, [2048, 4096])]]
    CW = 5 * DM
    PVROW = NBANDS * DM
    kcb = [sal(ctx, esB, f"b_kcb{i}", [128, CW], BF16) for i in range(3)]
    vcb = [sal(ctx, esB, f"b_vcb{i}", [128, CW], BF16) for i in range(3)]
    qkprod = sal(ctx, esB, "b_qkp", [128, CW], BF16)
    pvcP = sal(ctx, esB, "b_pvcP", [128, CW], BF16)
    pvcD = sal(ctx, esB, "b_pvcD", [128, CW], BF16)
    scor = [sal(ctx, esB, f"b_scor{i}", [128, NBANDS * NH], F32) for i in range(2)]
    escr = [sal(ctx, esB, f"b_escr{i}", [128, NBANDS * NH], F32) for i in range(2)]
    Pb = [sal(ctx, esB, f"b_P{i}", [128, NBANDS * NH], BF16) for i in range(2)]
    sm = sal(ctx, esB, "b_sm", [128, 3 * NH], F32)
    accA = [sal(ctx, esB, f"b_accA{i}", [128, DM], F32) for i in range(2)]
    accB = [sal(ctx, esB, f"b_accB{i}", [128, DM], F32) for i in range(2)]
    accb = [sal(ctx, esB, f"b_accb{i}", [128, DM], BF16) for i in range(2)]
    msk = sal(ctx, esB, "b_msk", [128, NT * NBANDS], F32)
    afm = [sal(ctx, esB, f"c_afm{i}", [128, DM], BF16) for i in range(2)]
    x0t = [sal(ctx, esB, f"c_x0{i}", [128, DM], F32) for i in range(2)]
    ot = [sal(ctx, esB, f"c_ot{i}", [128, DM], F32) for i in range(2)]
    lnn = sal(ctx, esB, "c_lnn", [128, DM], F32)
    x1t = [sal(ctx, esB, f"c_x1{i}", [128, DM], F32) for i in range(2)]
    lnC = LNState(ctx, esB, "c")
    ptrc = [pal(ctx, esB, f"c_ptr{i}", [128, DM], BF16) for i in range(2)]
    po = [pal(ctx, esB, f"c_po{i}", [128, DM], F32) for i in range(2)]

    def _mload(e):
        src = V(W["masks"], 0, [[NBANDS, 128], [128 * NBANDS, NT], [1, NBANDS]])
        dst = msk.ap().rearrange("p (i j) -> p i j", i=NT, j=NBANDS)
        ctx.dma(e, dst, src, "d7")
        e.wait_ge(ctx.sems["d7"], ctx.cnt["d7"])
    ctx.sec("gpsimd", _mload)
    d7_m = ctx.cnt["d7"]

    NC3 = 3 * NT
    ld_k = [None] * NC3
    ld_v = [None] * NC3
    dve_qkc = [0] * NC3
    pool_pvm = [0] * NC3
    dve_pvm = [0] * NC3
    dve_qk = [0] * NT
    act_exp = [0] * NT
    dve_P = [0] * NT
    pool_pvA = [0] * NT
    dve_fin = [0] * NT
    pe_tr = [0] * NT
    a_afm = [0] * NT
    pe_o = [0] * NT
    a_ot = [0] * NT
    dve_x1 = [0] * NT
    ld_x0 = [None] * NT
    st_x1 = [None] * NT
    kvrow = 2 * DM

    for I in range(NT + 1):
        par = I % 2
        J = I - 1
        jpar = J % 2
        base_row = HALO + 128 * I

        if I < NT:
            def _kload(e, I=I, base_row=base_row):
                if I == 0:
                    ctx.w(e, "d2", d2_tot)
                    ctx.w(e, "d3", d3_tot)
                for c in range(3):
                    g = 3 * I + c
                    if g >= 3:
                        ctx.w(e, "dve", dve_qkc[g - 3])
                    last = None
                    for (pos, ds) in CHG[c]:
                        step = ds[1] - ds[0] if len(ds) > 1 else 1
                        src = V(kv_dr, (base_row - ds[0]) * kvrow,
                                [[kvrow, 128], [-step * kvrow, len(ds)], [1, DM]])
                        dst = V(kcb[g % 3], DM * (pos - 5 * c),
                                [[CW, 128], [DM, len(ds)], [1, DM]])
                        last = ctx.dma(e, dst, src, f"d{g % 2}")
                    ld_k[g] = last
            ctx.sec("sync", _kload)

            def _vload(e, I=I, base_row=base_row):
                for c in range(3):
                    g = 3 * I + c
                    if g >= 3:
                        if c == 2:
                            ctx.w(e, "dve", dve_pvm[g - 3])
                        else:
                            ctx.w(e, "pool", pool_pvm[g - 3])
                    last = None
                    for (pos, ds) in CHG[c]:
                        step = ds[1] - ds[0] if len(ds) > 1 else 1
                        src = V(kv_dr, (base_row - ds[0]) * kvrow + DM,
                                [[kvrow, 128], [-step * kvrow, len(ds)], [1, DM]])
                        dst = V(vcb[g % 3], DM * (pos - 5 * c),
                                [[CW, 128], [DM, len(ds)], [1, DM]])
                        last = ctx.dma(e, dst, src, f"d{2 + g % 2}")
                    ld_v[g] = last
            ctx.sec("scalar", _vload)

            def _qk(e, I=I, par=par):
                if I == 0:
                    ctx.w(e, "d7", d7_m)
                if I >= 2:
                    ctx.w(e, "act", act_exp[I - 2])
                for c in range(3):
                    g = 3 * I + c
                    ctx.wp(e, ld_k[g])
                    qv = V(qsb, DM * I,
                           [[NT * DM, 128], [0, 5], [64, NH], [1, HD]])
                    kv_ = V(kcb[g % 3], 0,
                            [[CW, 128], [DM, 5], [64, NH], [1, HD]])
                    pv = V(qkprod, 0, [[CW, 128], [DM, 5], [64, NH], [1, HD]])
                    e.tensor_mul(pv, qv, kv_)
                    rin = V(qkprod, 0, [[CW, 128], [64, 5 * NH], [1, HD]])
                    e.reduce_sum(scor[par][:, 5 * NH * c:5 * NH * (c + 1)],
                                 rin, AX.X)
                    dve_qkc[g] = ctx.cnt["dve"]
                mv = V(msk, NBANDS * I, [[NT * NBANDS, 128], [1, NBANDS], [0, NH]])
                sv = V(scor[par], 0, [[NBANDS * NH, 128], [NH, NBANDS], [1, NH]])
                e.tensor_add(sv, sv, mv)
                svh = V(scor[par], 0, [[NBANDS * NH, 128], [1, NH], [NH, NBANDS]])
                e.reduce_max(sm[:, 0:NH], svh, AX.X)
                mxv = V(sm, 0, [[3 * NH, 128], [0, NBANDS], [1, NH]])
                e.tensor_sub(scor[par][:, :], scor[par][:, :], mxv)
            ctx.sec("vector", _qk)
            dve_qk[I] = ctx.cnt["dve"]

            def _exp(e, I=I, par=par):
                ctx.w(e, "dve", dve_qk[I])
                if I >= 2:
                    ctx.w(e, "dve", dve_P[I - 2])
                e.activation(out=escr[par][:, :], in_=scor[par][:, :],
                             func=AF.Exp, scale=0.125)
            ctx.sec("scalar", _exp)
            act_exp[I] = ctx.cnt["act"]

            def _P(e, I=I, par=par):
                ctx.w(e, "act", act_exp[I])
                if I >= 2:
                    ctx.w(e, "pool", pool_pvm[3 * (I - 2) + 1])
                ev = V(escr[par], 0, [[NBANDS * NH, 128], [1, NH], [NH, NBANDS]])
                e.reduce_sum(sm[:, NH:2 * NH], ev, AX.X)
                e.reciprocal(sm[:, 2 * NH:3 * NH], sm[:, NH:2 * NH])
                rzv = V(sm, 2 * NH, [[3 * NH, 128], [0, NBANDS], [1, NH]])
                e.tensor_mul(Pb[par][:, :], escr[par][:, :], rzv)
            ctx.sec("vector", _P)
            dve_P[I] = ctx.cnt["dve"]

            def _pvB(e, I=I, par=par):
                g = 3 * I + 2
                ctx.wp(e, ld_v[g])
                pvw = V(Pb[par], 10 * NH,
                        [[NBANDS * NH, 128], [NH, 5], [1, NH], [0, HD]])
                vbw = V(vcb[g % 3], 0, [[CW, 128], [DM, 5], [64, NH], [1, HD]])
                e.tensor_mul(pvcD[:, :], vbw, pvw)
                dve_pvm[g] = ctx.cnt["dve"]
                e.tensor_add(accB[par][:, :], pvcD[:, 0:DM], pvcD[:, DM:2 * DM])
                for d in range(2, 5):
                    e.tensor_add(accB[par][:, :], accB[par][:, :],
                                 pvcD[:, DM * d:DM * (d + 1)])
            ctx.sec("vector", _pvB)

            def _pvA(e, I=I, par=par):
                ctx.w(e, "dve", dve_P[I])
                if I >= 2:
                    ctx.w(e, "dve", dve_fin[I - 2])
                for c in range(2):
                    g = 3 * I + c
                    ctx.wp(e, ld_v[g])
                    pvw = V(Pb[par], 5 * c * NH,
                            [[NBANDS * NH, 128], [NH, 5], [1, NH], [0, HD]])
                    vbw = V(vcb[g % 3], 0, [[CW, 128], [DM, 5], [64, NH], [1, HD]])
                    e.tensor_mul(pvcP[:, :], vbw, pvw)
                    pool_pvm[g] = ctx.cnt["pool"]
                    if c == 0:
                        e.tensor_add(accA[par][:, :], pvcP[:, 0:DM],
                                     pvcP[:, DM:2 * DM])
                    else:
                        e.tensor_add(accA[par][:, :], accA[par][:, :],
                                     pvcP[:, 0:DM])
                        e.tensor_add(accA[par][:, :], accA[par][:, :],
                                     pvcP[:, DM:2 * DM])
                    for d in range(2, 5):
                        e.tensor_add(accA[par][:, :], accA[par][:, :],
                                     pvcP[:, DM * d:DM * (d + 1)])
            ctx.sec("gpsimd", _pvA)
            pool_pvA[I] = ctx.cnt["pool"]

            def _x0(e, I=I, par=par):
                if I >= 2:
                    ctx.w(e, "dve", dve_x1[I - 2])
                ld_x0[I] = ctx.dma(e, x0t[par][:, :],
                                   x_ext[HALO + 128 * I:HALO + 128 * (I + 1), :],
                                   "d6")
            ctx.sec("gpsimd", _x0)

        if I >= 1:
            def _fin(e, J=J, jpar=jpar):
                ctx.w(e, "pool", pool_pvA[J])
                if J >= 2:
                    ctx.w(e, "pe", pe_tr[J - 2])
                e.tensor_add(accb[jpar][:, :], accA[jpar][:, :],
                             accB[jpar][:, :])
            ctx.sec("vector", _fin)
            dve_fin[J] = ctx.cnt["dve"]

            def _tr(e, J=J, jpar=jpar):
                ctx.w(e, "dve", dve_fin[J])
                if J >= 2:
                    ctx.w(e, "act", a_afm[J - 2])
                last = None
                for b in range(8):
                    last = e.transpose(ptrc[jpar][:, 128 * b:128 * (b + 1)],
                                       accb[jpar][:, 128 * b:128 * (b + 1)],
                                       ident[:, :])
                ctx.inc(last, "pe")
            ctx.sec("tensor", _tr)
            pe_tr[J] = ctx.cnt["pe"]

            def _afm(e, J=J, jpar=jpar):
                ctx.w(e, "pe", pe_tr[J])
                if J >= 2:
                    ctx.w(e, "pe", pe_o[J - 2])
                e.activation(out=afm[jpar][:, :], in_=ptrc[jpar][:, :],
                             func=AF.Copy)
            ctx.sec("scalar", _afm)
            a_afm[J] = ctx.cnt["act"]

            def _oproj(e, J=J, jpar=jpar):
                ctx.w(e, "act", a_afm[J])
                if J >= 2:
                    ctx.w(e, "act", a_ot[J - 2])
                started = [False, False]
                for bp in range(8):
                    lhsT = afm[jpar][:, 128 * bp:128 * (bp + 1)]
                    tri = [(bp, bp)]
                    if bp + 1 < 8:
                        tri.append((bp + 1, 8 + bp))
                    if bp - 1 >= 0:
                        tri.append((bp - 1, 15 + bp - 1))
                    for (b, jj) in tri:
                        bank = b // 4
                        st = not started[bank]
                        started[bank] = True
                        last = e.matmul(po[jpar][:, 128 * b:128 * (b + 1)], lhsT,
                                        wu["o"][:, 128 * jj:128 * (jj + 1)],
                                        start=st, stop=False,
                                        skip_group_check=True)
                for h in range(2):
                    last = e.matmul(po[jpar][:, 512 * h:512 * (h + 1)],
                                    ones1[0:1, :],
                                    obias[0:1, 512 * h:512 * (h + 1)],
                                    start=False, stop=False,
                                    skip_group_check=True)
                ctx.inc(last, "pe")
            ctx.sec("tensor", _oproj)
            pe_o[J] = ctx.cnt["pe"]

            def _ot(e, J=J, jpar=jpar):
                ctx.w(e, "pe", pe_o[J])
                if J >= 2:
                    ctx.w(e, "dve", dve_x1[J - 2])
                e.activation(out=ot[jpar][:, :], in_=po[jpar][:, :],
                             func=AF.Copy)
            ctx.sec("scalar", _ot)
            a_ot[J] = ctx.cnt["act"]

            d = emit_ln(ctx, lnC, ot[jpar][:, :], lnn[:, :],
                        [("act", a_ot[J]), ld_x0[J]],
                        res_add_ap=x0t[jpar][:, :], dst2=x1t[jpar][:, :],
                        norm_waits=([st_x1[J - 2]] if J >= 2 else ()))
            dve_x1[J] = d

            def _stx(e, J=J, jpar=jpar):
                ctx.w(e, "dve", dve_x1[J])
                st_x1[J] = ctx.dma(e, x1_dr[128 * J:128 * (J + 1), :],
                                   x1t[jpar][:, :], f"d{4 + jpar}")
            ctx.sec("scalar", _stx)

    ctx.pending = [sv for sv in st_x1 if sv is not None][-2:]
    ctx.barrier(drains=st_x1[-2:])
    esB.close()
    es0.close()


def emit_l0f(ctx, W, x_in, x_out, ident):
    nc = ctx.nc
    NO1, NOH, NOC = 32, 32, 8
    drains = getattr(ctx, "pending", [])
    ctx.pending = []

    esS = ExitStack()
    for nm, w_ in (("s1cm", 32), ("s2cm", 32), ("h1scm", 32), ("h2scm", 32),
                   ("s3cm", 8)):
        W[nm + "_sb"] = sal(ctx, esS, "sb_" + nm, [128, w_], F32)

    def _sprolog(e):
        for nm in ("s1cm", "s2cm", "h1scm", "h2scm", "s3cm"):
            ctx.dma(e, W[nm + "_sb"][:, :], W[nm][:, :], "d7")
        e.wait_ge(ctx.sems["d7"], ctx.cnt["d7"])
    ctx.sec("gpsimd", _sprolog)
    scale_tot = ("d7", ctx.cnt["d7"])

    for tg in range(2):
        es = ExitStack()
        esL = ExitStack()
        g_sb = sal(ctx, es, f"f_g{tg}", [128, NO1 * 1024], BF16)
        xt = [sal(ctx, es, f"f_xt{tg}{i}", [128, DM], F32) for i in range(2)]
        lnt = [sal(ctx, es, f"f_lnt{tg}{i}", [128, DM], BF16) for i in range(2)]
        wh = [sal(ctx, es, f"f_wh{tg}{i}", [128, 32 * 128], BF16) for i in range(2)]
        dt_scr = sal(ctx, es, f"f_dt{tg}", [128, DM], BF16)
        x2t = [sal(ctx, es, f"f_x2{tg}{i}", [128, DM], F32) for i in range(2)]
        lnF = LNState(ctx, es, f"f{tg}")
        lnO = LNState(ctx, es, f"fo{tg}")
        lnx = sal(ctx, esL, f"f_lnx{tg}", [128, 8 * 1024], BF16)
        u_scr = sal(ctx, esL, f"f_u{tg}", [128, 1024], BF16)
        v_scr = sal(ctx, esL, f"f_v{tg}", [128, 1024], BF16)
        w12 = [sal(ctx, esL, f"f_w12{tg}{i}", [128, 16 * 128], BF16) for i in range(2)]
        pu = pal(ctx, es, f"f_pu{tg}", [128, 1024], F32)
        pv2 = pal(ctx, es, f"f_pv2{tg}", [128, 1024], F32)
        esP = ExitStack()
        ptr = pal(ctx, esP, f"f_ptr{tg}", [128, DM], BF16)

        ln_done = [0] * 8
        sq_act = [0] * 8
        pe_tr = [0] * 8
        dve_lnx = [0] * 8
        ld_x = [None] * 8
        for j in range(8):
            par = j % 2
            row0 = tg * 1024 + 128 * j

            def _ld(e, j=j, par=par, row0=row0, tg=tg):
                if tg == 0 and j == 0:
                    for sv in drains:
                        ctx.wp(e, sv)
                if j >= 2:
                    ctx.w(e, "dve", ln_done[j - 2])
                    ctx.w(e, "act", sq_act[j - 2])
                ld_x[j] = ctx.dma(e, xt[par][:, :],
                                  x_in[row0:row0 + 128, :], f"d{par}")
            ctx.sec("sync", _ld)

            d = emit_ln(ctx, lnF, xt[par][:, :], lnt[par][:, :], [ld_x[j]],
                        norm_waits=([("pe", pe_tr[j - 2])] if j >= 2 else ()))
            ln_done[j] = d
            sq_act[j] = lnF.last_sq_act

            def _tr(e, j=j, par=par, d=d):
                ctx.w(e, "dve", d)
                if j >= 1:
                    ctx.w(e, "dve", dve_lnx[j - 1])
                last = None
                for b in range(8):
                    last = e.transpose(ptr[:, 128 * b:128 * (b + 1)],
                                       lnt[par][:, 128 * b:128 * (b + 1)],
                                       ident[:, :])
                ctx.inc(last, "pe")
            ctx.sec("tensor", _tr)
            pe_tr[j] = ctx.cnt["pe"]

            def _ev(e, j=j):
                ctx.w(e, "pe", pe_tr[j])
                dst = V(lnx, 128 * j, [[8 * 1024, 128], [1024, 8], [1, 128]])
                e.tensor_copy(dst, ptr[:, :])
            ctx.sec("vector", _ev)
            dve_lnx[j] = ctx.cnt["dve"]

        lnx_done = dve_lnx[7]
        esP.close()
        esH = ExitStack()
        phw2 = [pal(ctx, esH, f"f_phw{tg}{i}", [128, 1024], F32) for i in range(2)]

        ld_w12 = [None] * NO1
        pe_uv = [0] * NO1
        act_u = [0] * NO1
        dve_g = [0] * NO1
        for o in range(NO1):
            par = o % 2

            def _wl(e, o=o, par=par):
                if o >= 2:
                    ctx.w(e, "pe", pe_uv[o - 2])
                else:
                    ctx.w(e, "pe", getattr(ctx, "wgate", 0))
                for wnm in ("w1T", "w2T"):
                    off = 0 if wnm == "w1T" else 8 * 128
                    src = V(W[wnm], 128 * o,
                            [[DH, 128], [128 * DH, 8], [1, 128]])
                    dst = V(w12[par], off, [[16 * 128, 128], [128, 8], [1, 128]])
                    ld_w12[o] = ctx.dma(e, dst, src, f"d{6 + par}")
            ctx.sec("gpsimd", _wl)

            def _mm(e, o=o, par=par):
                if o == 0:
                    ctx.w(e, "dve", lnx_done)
                ctx.wp(e, ld_w12[o])
                last = None
                for wi, ps in ((0, pu), (1, pv2)):
                    if wi == 0 and o >= 1:
                        ctx.w(e, "act", act_u[o - 1])
                    if wi == 1 and o >= 1:
                        ctx.w(e, "dve", dve_g[o - 1])
                    for ki in range(8):
                        lhsT = w12[par][:, (8 * wi + ki) * 128:(8 * wi + ki + 1) * 128]
                        for h in range(2):
                            last = e.matmul(ps[:, 512 * h:512 * (h + 1)], lhsT,
                                            V(lnx, ki * 1024 + 512 * h,
                                              [[8 * 1024, 128], [1, 512]]),
                                            start=(ki == 0), stop=(ki == 7),
                                            skip_group_check=True)
                    if wi == 0:
                        ctx.inc(last, "pe")
                ctx.inc(last, "pe")
            ctx.sec("tensor", _mm)
            pe_uv[o] = ctx.cnt["pe"]
            pe_u_part = pe_uv[o] - 1

            def _ue(e, o=o, tg=tg, pw=pe_u_part):
                ctx.w(e, "pe", pw)
                if tg == 0 and o == 0:
                    ctx.wp(e, scale_tot)
                if o >= 1:
                    ctx.w(e, "dve", dve_g[o - 1])
                e.activation(out=u_scr[:, :], in_=pu[:, :], func=AF.Silu,
                             scale=W["s1cm_sb"][:, o:o + 1])
            ctx.sec("scalar", _ue)
            act_u[o] = ctx.cnt["act"]

            def _ge(e, o=o, tg=tg):
                ctx.w(e, "pe", pe_uv[o])
                if tg == 0 and o == 0:
                    ctx.wp(e, scale_tot)
                e.tensor_scalar(v_scr[:, :], pv2[:, :],
                                W["s2cm_sb"][:, o:o + 1], None, ALU.mult)
                ctx.w(e, "act", act_u[o])
                e.tensor_mul(g_sb[:, 1024 * o:1024 * (o + 1)], u_scr[:, :],
                             v_scr[:, :])
            ctx.sec("vector", _ge)
            dve_g[o] = ctx.cnt["dve"]

        esL.close()
        ctx.wgate = pe_uv[NO1 - 1]
        es_b = ExitStack()
        b_sb = sal(ctx, es_b, f"f_b{tg}", [128, NOH * 1024], BF16)

        def hpass(src_sb, dst_write, wname, scname, first_wait_dve, first_wait_act,
                  war_pe_total):
            ld_wh = [None] * NOH
            pe_h = [0] * NOH
            act_h = [0] * NOH
            for o in range(NOH):
                par = o % 2

                def _wl(e, o=o, par=par):
                    if o >= 2:
                        ctx.w(e, "pe", pe_h[o - 2])
                    else:
                        ctx.w(e, "pe", getattr(ctx, "wgate", 0))
                    src = V(W[wname], 128 * o,
                            [[DH, 128], [128 * DH, 32], [1, 128]])
                    dst = wh[par].ap().rearrange("p (k c) -> p k c", k=32, c=128)
                    ld_wh[o] = ctx.dma(e, dst, src, f"d{6 + par}")
                ctx.sec("gpsimd", _wl)

                def _mm(e, o=o, par=par):
                    if o == 0:
                        if first_wait_dve:
                            ctx.w(e, "dve", first_wait_dve)
                        if first_wait_act:
                            ctx.w(e, "act", first_wait_act)
                    ctx.wp(e, ld_wh[o])
                    if o >= 2:
                        ctx.w(e, "act", act_h[o - 2])
                    ph = phw2[o % 2]
                    last = None
                    for ki in range(32):
                        lhsT = wh[par][:, ki * 128:(ki + 1) * 128]
                        for h in range(2):
                            last = e.matmul(ph[:, 512 * h:512 * (h + 1)], lhsT,
                                            V(src_sb, ki * 1024 + 512 * h,
                                              [[NOH * 1024, 128], [1, 512]]),
                                            start=(ki == 0), stop=(ki == 31),
                                            skip_group_check=True)
                    ctx.inc(last, "pe")
                ctx.sec("tensor", _mm)
                pe_h[o] = ctx.cnt["pe"]

                def _be(e, o=o, tg=tg):
                    ctx.w(e, "pe", pe_h[o])
                    if tg == 0 and o == 0:
                        ctx.wp(e, scale_tot)
                    if war_pe_total and o == 0:
                        ctx.w(e, "pe", war_pe_total[0])
                    e.activation(out=dst_write(o), in_=phw2[o % 2][:, :],
                                 func=AF.Gelu, scale=W[scname][:, o:o + 1])
                ctx.sec("scalar", _be)
                act_h[o] = ctx.cnt["act"]
            return pe_h, act_h

        pe_h1, act_b = hpass(g_sb, lambda o: b_sb[:, 1024 * o:1024 * (o + 1)],
                             "h1T", "h1scm_sb", dve_g[NO1 - 1], 0, None)
        ctx.wgate = pe_h1[NOH - 1]
        pe_h2, act_c = hpass(b_sb, lambda o: g_sb[:, 1024 * o:1024 * (o + 1)],
                             "h2T", "h2scm_sb", 0, act_b[NOH - 1], [pe_h1[NOH - 1]])
        ctx.wgate = pe_h2[NOH - 1]
        es_b.close()
        es_d = ExitStack()
        dfm = sal(ctx, es_d, f"f_dfm{tg}", [128, 8 * 1024], BF16)

        ld_w3 = [None] * NOC
        pe_w3 = [0] * NOC
        dve_dfm = [0] * NOC
        for oc in range(NOC):
            par = oc % 2

            def _wl(e, oc=oc, par=par):
                if oc >= 2:
                    ctx.w(e, "pe", pe_w3[oc - 2])
                else:
                    ctx.w(e, "pe", getattr(ctx, "wgate", 0))
                src = V(W["w3T"], 128 * oc,
                        [[DM, 128], [128 * DM, 32], [1, 128]])
                dst = wh[par].ap().rearrange("p (k c) -> p k c", k=32, c=128)
                ld_w3[oc] = ctx.dma(e, dst, src, f"d{6 + par}")
            ctx.sec("gpsimd", _wl)

            def _mm(e, oc=oc, par=par):
                if oc == 0:
                    ctx.w(e, "act", act_c[NOH - 1])
                ctx.wp(e, ld_w3[oc])
                if oc >= 2:
                    ctx.w(e, "dve", dve_dfm[oc - 2])
                last = None
                for ki in range(32):
                    lhsT = wh[par][:, ki * 128:(ki + 1) * 128]
                    for h in range(2):
                        last = e.matmul(phw2[oc % 2][:, 512 * h:512 * (h + 1)], lhsT,
                                        V(g_sb, ki * 1024 + 512 * h,
                                          [[NO1 * 1024, 128], [1, 512]]),
                                        start=(ki == 0), stop=(ki == 31),
                                        skip_group_check=True)
                ctx.inc(last, "pe")
            ctx.sec("tensor", _mm)
            pe_w3[oc] = ctx.cnt["pe"]

            def _de(e, oc=oc, tg=tg):
                ctx.w(e, "pe", pe_w3[oc])
                if tg == 0 and oc == 0:
                    ctx.wp(e, scale_tot)
                e.tensor_scalar(dfm[:, 1024 * oc:1024 * (oc + 1)],
                                phw2[oc % 2][:, :],
                                W["s3cm_sb"][:, oc:oc + 1], None, ALU.mult)
            ctx.sec("vector", _de)
            dve_dfm[oc] = ctx.cnt["dve"]

        ctx.wgate = pe_w3[NOC - 1]
        esH.close()
        esP2 = ExitStack()
        ptrF = pal(ctx, esP2, f"f_ptrF{tg}", [128, DM], BF16)
        pe_trF = [0] * 8
        dve_out = [0] * 8
        st_x2 = [None] * 8
        ld_x1 = [None] * 8
        for j in range(8):
            par = j % 2
            row0 = tg * 1024 + 128 * j

            def _ld(e, j=j, par=par, row0=row0):
                if j >= 2:
                    ctx.w(e, "dve", dve_out[j - 2])
                else:
                    ctx.w(e, "dve", ln_done[6 + j])
                    ctx.w(e, "act", sq_act[6 + j])
                ld_x1[j] = ctx.dma(e, xt[par][:, :], x_in[row0:row0 + 128, :],
                                   f"d{par}")
            ctx.sec("sync", _ld)

            def _tr(e, j=j):
                if j == 0:
                    ctx.w(e, "dve", dve_dfm[NOC - 1])
                if j >= 1:
                    ctx.w(e, "dve", dve_out[j - 1])
                last = None
                for oc in range(8):
                    last = e.transpose(
                        ptrF[:, 128 * oc:128 * (oc + 1)],
                        dfm[:, 1024 * oc + 128 * j:1024 * oc + 128 * (j + 1)],
                        ident[:, :])
                ctx.inc(last, "pe")
            ctx.sec("tensor", _tr)
            pe_trF[j] = ctx.cnt["pe"]

            def _ev(e, j=j):
                ctx.w(e, "pe", pe_trF[j])
                if j >= 1:
                    ctx.w(e, "act", lnO.last_sq_act)
                e.tensor_copy(dt_scr[:, :], ptrF[:, :])
            ctx.sec("vector", _ev)

            d = emit_ln(ctx, lnO, dt_scr[:, :], lnF.scr[:, :],
                        [("dve", ctx.cnt["dve"]), ld_x1[j]],
                        res_add_ap=xt[par][:, :], dst2=x2t[par][:, :],
                        norm_waits=([st_x2[j - 2]] if j >= 2 else ()))
            dve_out[j] = d

            def _st(e, j=j, par=par, row0=row0):
                ctx.w(e, "dve", dve_out[j])
                st_x2[j] = ctx.dma(e, x_out[row0:row0 + 128, :], x2t[par][:, :],
                                   f"d{4 + par}")
            ctx.sec("scalar", _st)

        ctx.pending.extend([sv for sv in st_x2 if sv is not None][-2:])
        ctx.barrier(drains=st_x2[-2:])
        esP2.close()
        es_d.close()
        es.close()
    esS.close()


def emit_l1a(ctx, W, x_in, x_out, ident, ones1, uag_in, uag_out):
    nc = ctx.nc
    drains = getattr(ctx, "pending", [])
    ctx.pending = []
    NG = 16
    es = ExitStack()
    y_sb = sal(ctx, es, "s_y", [128, NG * 2048], BF16)
    xt = [sal(ctx, es, f"s_xt{i}", [128, DM], F32) for i in range(2)]
    lnt = [sal(ctx, es, f"s_lnt{i}", [128, DM], BF16) for i in range(2)]
    u_buf = sal(ctx, es, "s_u", [128, 16], F32)
    gcol = sal(ctx, es, "s_gcol", [128, 16], F32)
    cvec = sal(ctx, es, "s_cvec", [128, 6 * 16], F32)
    bp_sb = sal(ctx, es, "s_bp", [128, 16 * 16], F32)
    cpt_sb = sal(ctx, es, "s_cpt", [128, 16 * 16], F32)
    uago = sal(ctx, es, "s_uago", [128, 4 * 16], F32)
    ugb = sal(ctx, es, "s_ugb", [128, 16], BF16)
    p9 = sal(ctx, es, "s_p9", [128, 1], F32)
    t16 = sal(ctx, es, "s_t16", [128, 16], F32)
    boutr = sal(ctx, es, "s_bout", [1, DM], BF16)
    wrow = sal(ctx, es, "s_wrow", [1, DM], BF16)
    wb_sb = sal(ctx, es, "s_wb", [128, DM], BF16)
    lnS = LNState(ctx, es, "s")
    lnSo = LNState(ctx, es, "so")
    BIN, DP, DEC, SEL, UPV, UG = range(6)

    def _pl(e):
        ctx.dma(e, cvec[:, BIN * 16:BIN * 16 + 16], W["bincm"][:, :], "d7")
        ctx.dma(e, cvec[:, DP * 16:DP * 16 + 16], W["Dpcm"][:, :], "d7")
        ctx.dma(e, cvec[:, DEC * 16:DEC * 16 + 16], W["decaycm"][:, :], "d7")
        ctx.dma(e, cvec[:, SEL * 16:SEL * 16 + 4], W["selread"][:, :], "d7")
        src = V(W["Bp"], 0, [[16, 128], [128 * 16, 16], [1, 16]])
        ctx.dma(e, bp_sb.ap().rearrange("p (g j) -> p g j", g=16, j=16), src, "d7")
        src = V(W["CpT"], 0, [[16, 128], [128 * 16, 16], [1, 16]])
        ctx.dma(e, cpt_sb.ap().rearrange("p (g j) -> p g j", g=16, j=16), src, "d7")
        ctx.dma(e, boutr[:, :], W["boutrow"][:, :], "d7")
        e.wait_ge(ctx.sems["d7"], ctx.cnt["d7"])
    ctx.sec("gpsimd", _pl)
    pl_tot = ("d7", ctx.cnt["d7"])

    def _gc(e):
        ctx.wp(e, pl_tot)
        e.memset(p9[:, :], DECAY)
        for g in range(NG):
            e.tensor_mul(t16[:, :], bp_sb[:, 16 * g:16 * (g + 1)],
                         cpt_sb[:, 16 * g:16 * (g + 1)])
            e.reduce_sum(gcol[:, g:g + 1], t16[:, :], AX.X)
    ctx.sec("vector", _gc)
    gcol_done = ctx.cnt["dve"]

    esL = ExitStack()
    lnx2 = sal(ctx, esL, "s_lnx2", [128, 8 * 2048], BF16)
    esA = ExitStack()
    ptr = pal(ctx, esA, "s_ptr", [128, DM], BF16)
    ln_done = [0] * NT
    sq_act = [0] * NT
    pe_tr = [0] * NT
    dve_lnx = [0] * NT
    ld_x = [None] * NT
    for j in range(NT):
        par = j % 2

        def _ld(e, j=j, par=par):
            if j == 0:
                for sv in drains:
                    ctx.wp(e, sv)
            if j >= 2:
                ctx.w(e, "dve", ln_done[j - 2])
                ctx.w(e, "act", sq_act[j - 2])
            ld_x[j] = ctx.dma(e, xt[par][:, :],
                              x_in[128 * j:128 * (j + 1), :], f"d{par}")
        ctx.sec("sync", _ld)

        d = emit_ln(ctx, lnS, xt[par][:, :], lnt[par][:, :], [ld_x[j]],
                    norm_waits=([("pe", pe_tr[j - 2])] if j >= 2 else ()))
        ln_done[j] = d
        sq_act[j] = lnS.last_sq_act

        def _tr(e, j=j, par=par, d=d):
            ctx.w(e, "dve", d)
            if j >= 1:
                ctx.w(e, "dve", dve_lnx[j - 1])
            last = None
            for b in range(8):
                last = e.transpose(ptr[:, 128 * b:128 * (b + 1)],
                                   lnt[par][:, 128 * b:128 * (b + 1)],
                                   ident[:, :])
            ctx.inc(last, "pe")
        ctx.sec("tensor", _tr)
        pe_tr[j] = ctx.cnt["pe"]

        def _ev(e, j=j):
            ctx.w(e, "pe", pe_tr[j])
            dst = V(lnx2, 128 * j, [[8 * 2048, 128], [2048, 8], [1, 128]])
            e.tensor_copy(dst, ptr[:, :])
        ctx.sec("vector", _ev)
        dve_lnx[j] = ctx.cnt["dve"]
    lnx_done = dve_lnx[NT - 1]
    esA.close()

    esB = ExitStack()
    xi_g = sal(ctx, esB, "s_xi", [128, 2048], F32)
    a_g = sal(ctx, esB, "s_a", [128, 2048], F32)
    t1 = sal(ctx, esB, "s_t1", [128, 2048], F32)
    wbuf = [sal(ctx, esB, f"s_wbuf{i}", [128, 8 * 128], BF16) for i in range(2)]
    pxi = [pal(ctx, esB, f"s_pxi{i}", [128, 2048], F32) for i in range(2)]
    ld_w = [None] * NG
    pe_xi = [0] * NG
    act_xi = [0] * NG
    dve_y = [0] * NG
    for g in range(NG):
        par = g % 2

        def _wl(e, g=g, par=par):
            if g >= 2:
                ctx.w(e, "pe", pe_xi[g - 2])
            else:
                ctx.w(e, "pe", getattr(ctx, "wgate", 0))
            src = V(W["Win"], 128 * g, [[DIN, 128], [128 * DIN, 8], [1, 128]])
            dst = wbuf[par].ap().rearrange("p (k c) -> p k c", k=8, c=128)
            ld_w[g] = ctx.dma(e, dst, src, f"d{6 + par}")
        ctx.sec("gpsimd", _wl)

        def _mm(e, g=g, par=par):
            if g == 0:
                ctx.w(e, "dve", lnx_done)
            ctx.wp(e, ld_w[g])
            if g >= 2:
                ctx.w(e, "act", act_xi[g - 2])
            last = None
            for ki in range(8):
                lhsT = wbuf[par][:, ki * 128:(ki + 1) * 128]
                for h in range(4):
                    last = e.matmul(pxi[g % 2][:, 512 * h:512 * (h + 1)], lhsT,
                                    V(lnx2, ki * 2048 + 512 * h,
                                      [[8 * 2048, 128], [1, 512]]),
                                    start=(ki == 0), stop=(ki == 7),
                                    skip_group_check=True)
            ctx.inc(last, "pe")
        ctx.sec("tensor", _mm)
        pe_xi[g] = ctx.cnt["pe"]

        def _xe(e, g=g):
            ctx.w(e, "pe", pe_xi[g])
            if g == 0:
                ctx.wp(e, pl_tot)
            if g >= 1:
                ctx.w(e, "dve", dve_y[g - 1])
            e.activation(out=xi_g[:, :], in_=pxi[g % 2][:, :], func=AF.Silu,
                         bias=cvec[:, BIN * 16 + g:BIN * 16 + g + 1])
        ctx.sec("scalar", _xe)
        act_xi[g] = ctx.cnt["act"]

        def _sy(e, g=g):
            ctx.w(e, "act", act_xi[g])
            d0 = V(p9, 0, [[1, 128], [0, 2048]])
            e.tensor_tensor_scan(a_g[:, :], d0, xi_g[:, :], 0.0,
                                 ALU.mult, ALU.add)
            e.tensor_copy(u_buf[:, g:g + 1], a_g[:, 2047:2048])
            e.tensor_scalar(t1[:, :], xi_g[:, :],
                            cvec[:, DP * 16 + g:DP * 16 + g + 1], None, ALU.mult)
            e.scalar_tensor_tensor(y_sb[:, 2048 * g:2048 * (g + 1)], a_g[:, :],
                                   gcol[:, g:g + 1], t1[:, :], ALU.mult, ALU.add)
        ctx.sec("vector", _sy)
        dve_y[g] = ctx.cnt["dve"]
    ctx.wgate = pe_xi[NG - 1]
    esB.close()
    esL.close()

    def _ust(e):
        ctx.w(e, "dve", dve_y[NG - 1])
        sv = ctx.dma(e, uag_in[:, :], u_buf[:, :], "d2")
        ctx.wp(e, sv)
        ctx.inc(e.nop(), "act")
    ctx.sec("scalar", _ust)
    act_ust = ctx.cnt["act"]

    def _ag(e):
        ctx.w(e, "act", act_ust)
        ctx.inc(e.collective_compute(
            "AllGather", ALU.bypass,
            replica_groups=[[0, 1, 2, 3], [4, 5, 6, 7]],
            ins=[uag_in[:, :]], outs=[uag_out[:, :]]), "cc")
        e.wait_ge(ctx.sems["cc"], ctx.cnt["cc"])
        src = V(uag_out, 0, [[16, 128], [128 * 16, 4], [1, 16]])
        sv = ctx.dma(e, uago.ap().rearrange("p (r j) -> p r j", r=4, j=16),
                     src, "d7")
        ctx.wp(e, sv)
        ctx.inc(e.engine_nop(), "pool")
    ctx.sec("gpsimd", _ag)
    pool_ag = ctx.cnt["pool"]

    def _sel(e):
        ctx.w(e, "pool", pool_ag)
        selv = V(cvec, SEL * 16, [[6 * 16, 128], [1, 4], [0, 16]])
        e.tensor_mul(uago[:, :], uago[:, :], selv)
        rin = V(uago, 0, [[4 * 16, 128], [1, 16], [16, 4]])
        e.reduce_sum(cvec[:, UPV * 16:UPV * 16 + 16], rin, AX.X)
        e.tensor_mul(cvec[:, UG * 16:UG * 16 + 16],
                     cvec[:, UPV * 16:UPV * 16 + 16], gcol[:, :])
        e.tensor_copy(ugb[:, :], cvec[:, UG * 16:UG * 16 + 16])
    ctx.sec("vector", _sel)
    dve_ug = ctx.cnt["dve"]

    esD = ExitStack()
    wo_sb = sal(ctx, esD, "s_wo", [128, NG * 1024], BF16)
    o_tm = sal(ctx, esD, "s_otm", [128, DM], F32)
    lnn = sal(ctx, esD, "s_lnn", [128, DM], F32)
    x3t = [sal(ctx, esD, f"s_x3{i}", [128, DM], F32) for i in range(2)]
    pso = [pal(ctx, esD, f"s_pso{i}", [128, DM], F32) for i in range(2)]
    pw1 = pal(ctx, esD, "s_pw1", [1, DM], F32)

    def _wol(e):
        ctx.w(e, "pe", ctx.wgate)
        src = V(W["Wout"], 0, [[DM, 128], [128 * DM, NG], [1, DM]])
        sv = ctx.dma(e, wo_sb.ap().rearrange("p (g c) -> p g c", g=NG, c=DM),
                     src, "d6")
        ld = sv
        ctx.pending_wo = ld
    ctx.sec("gpsimd", _wol)
    ld_wo = ctx.pending_wo

    def _wr(e):
        ctx.wp(e, ld_wo)
        ctx.w(e, "dve", dve_ug)
        last = None
        for g in range(NG):
            for h in range(2):
                last = e.matmul(pw1[0:1, 512 * h:512 * (h + 1)],
                                ugb[:, g:g + 1],
                                wo_sb[:, 1024 * g + 512 * h:1024 * g + 512 * (h + 1)],
                                start=(g == 0), stop=(g == NG - 1),
                                skip_group_check=True)
        ctx.inc(last, "pe")
    ctx.sec("tensor", _wr)
    pe_wr = ctx.cnt["pe"]

    def _wre(e):
        ctx.w(e, "pe", pe_wr)
        e.tensor_copy(wrow[0:1, :], pw1[0:1, :])
    ctx.sec("vector", _wre)
    dve_wre = ctx.cnt["dve"]

    def _wrb(e):
        ctx.w(e, "dve", dve_wre)
        last = None
        for h in range(2):
            last = e.matmul(pso[1][:, 512 * h:512 * (h + 1)], ones1[0:1, :],
                            wrow[0:1, 512 * h:512 * (h + 1)],
                            start=True, stop=True, skip_group_check=True)
        ctx.inc(last, "pe")
    ctx.sec("tensor", _wrb)
    pe_wrb = ctx.cnt["pe"]

    def _wbe(e):
        ctx.w(e, "pe", pe_wrb)
        e.tensor_copy(wb_sb[:, :], pso[1][:, :])
    ctx.sec("vector", _wbe)
    dve_wb = ctx.cnt["dve"]

    pe_sl = [0] * NT
    dve_out = [0] * NT
    ld_x2 = [None] * NT
    st_x3 = [None] * NT
    for sl in range(NT):
        par = sl % 2

        def _ld(e, sl=sl, par=par):
            if sl >= 2:
                ctx.w(e, "dve", dve_out[sl - 2])
            else:
                ctx.w(e, "dve", ln_done[NT - 2 + sl])
                ctx.w(e, "act", sq_act[NT - 2 + sl])
            ld_x2[sl] = ctx.dma(e, xt[par][:, :],
                                x_in[128 * sl:128 * (sl + 1), :], f"d{par}")
        ctx.sec("sync", _ld)

        def _mm(e, sl=sl):
            if sl <= 1:
                ctx.w(e, "dve", dve_wb)
            if sl >= 2:
                ctx.w(e, "dve", dve_out[sl - 2])
            last = None
            for g in range(NG):
                lhsT = y_sb[:, 2048 * g + 128 * sl:2048 * g + 128 * (sl + 1)]
                for h in range(2):
                    last = e.matmul(pso[sl % 2][:, 512 * h:512 * (h + 1)], lhsT,
                                    wo_sb[:, 1024 * g + 512 * h:1024 * g + 512 * (h + 1)],
                                    start=(g == 0), stop=False,
                                    skip_group_check=True)
            for h in range(2):
                last = e.matmul(pso[sl % 2][:, 512 * h:512 * (h + 1)], ones1[0:1, :],
                                boutr[0:1, 512 * h:512 * (h + 1)],
                                start=False, stop=(h == 1),
                                skip_group_check=True)
            ctx.inc(last, "pe")
        ctx.sec("tensor", _mm)
        pe_sl[sl] = ctx.cnt["pe"]

        def _oe(e, sl=sl):
            ctx.w(e, "pe", pe_sl[sl])
            ctx.w(e, "act", lnSo.last_sq_act)
            e.scalar_tensor_tensor(o_tm[:, :], wb_sb[:, :],
                                   cvec[:, DEC * 16 + sl:DEC * 16 + sl + 1],
                                   pso[sl % 2][:, :], ALU.mult, ALU.add)
        ctx.sec("vector", _oe)
        dve_oe = ctx.cnt["dve"]

        d = emit_ln(ctx, lnSo, o_tm[:, :], lnn[:, :],
                    [("dve", dve_oe), ld_x2[sl]],
                    res_add_ap=xt[par][:, :], dst2=x3t[par][:, :],
                    norm_waits=([st_x3[sl - 2]] if sl >= 2 else ()))
        dve_out[sl] = d

        def _st(e, sl=sl, par=par):
            ctx.w(e, "dve", dve_out[sl])
            st_x3[sl] = ctx.dma(e, x_out[128 * sl:128 * (sl + 1), :],
                                x3t[par][:, :], f"d{4 + par}")
        ctx.sec("scalar", _st)

    ctx.pending = st_x3[-2:]
    ctx.wgate = pe_sl[NT - 1]
    ctx.barrier(drains=st_x3[-2:])
    esD.close()
    es.close()


def emit_l1f(ctx, W, x_in, y_out, ident, ones1, xp_in, xp_out, c_dr, yt_in,
             yt_out):
    nc = ctx.nc
    drains = getattr(ctx, "pending", [])
    ctx.pending = []
    es = ExitStack()
    xt = [sal(ctx, es, f"m_xt{i}", [128, DM], F32) for i in range(2)]
    lnt = sal(ctx, es, "m_lnt", [128, DM], BF16)
    rS = sal(ctx, es, "m_rS", [128, TC], F32)
    phib = sal(ctx, es, "m_phib", [1, NP], BF16)
    phibc = sal(ctx, es, "m_phibc", [128, 16], F32)
    sel32 = sal(ctx, es, "m_sel32", [128, 32], BF16)
    rep32 = sal(ctx, es, "m_rep32", [32, 128], BF16)
    oav = sal(ctx, es, "m_oav", [128, 128], BF16)
    lnM = LNState(ctx, es, "m")
    lnMo = LNState(ctx, es, "mo")

    def _pl(e):
        ctx.dma(e, phib[:, :], W["phibrow"][:, :], "d7")
        ctx.dma(e, phibc[:, :], W["phibcm"][:, :], "d7")
        ctx.dma(e, sel32[:, :], W["sel32"][:, :], "d7")
        ctx.dma(e, rep32[:, :], W["rep32"][:, :], "d7")
        ctx.dma(e, oav[:, :], W["oav"][:, :], "d7")
        e.wait_ge(ctx.sems["d7"], ctx.cnt["d7"])
    ctx.sec("gpsimd", _pl)
    pl_tot = ("d7", ctx.cnt["d7"])

    es1 = ExitStack()
    phit = sal(ctx, es1, "m_phit", [128, 8 * NP], BF16)
    lnx3fm = sal(ctx, es1, "m_lnx3fm", [128, 8 * TC], BF16)

    def _pwl(e):
        ctx.w(e, "pe", getattr(ctx, "wgate", 0))
        src = V(W["phiT"], 0, [[NP, 128], [128 * NP, 8], [1, NP]])
        ctx.phit_ld = ctx.dma(
            e, phit.ap().rearrange("p (k c) -> p k c", k=8, c=NP), src, "d6")
    ctx.sec("gpsimd", _pwl)
    phit_ld = ctx.phit_ld

    es2 = ExitStack()
    lnx3tm = sal(ctx, es2, "m_lnx3tm", [128, NT * DM], BF16)
    esA = ExitStack()
    ptr = pal(ctx, esA, "m_ptr", [128, DM], BF16)
    ln_done = [0] * NT
    sq_act = [0] * NT
    pe_tr = [0] * NT
    dve_fm = [0] * NT
    ld_x = [None] * NT
    for j in range(NT):
        par = j % 2

        def _ld(e, j=j, par=par):
            if j == 0:
                for sv in drains:
                    ctx.wp(e, sv)
            if j >= 2:
                ctx.w(e, "dve", ln_done[j - 2])
                ctx.w(e, "act", sq_act[j - 2])
            ld_x[j] = ctx.dma(e, xt[par][:, :],
                              x_in[128 * j:128 * (j + 1), :], f"d{par}")
        ctx.sec("sync", _ld)

        d = emit_ln(ctx, lnM, xt[par][:, :], lnx3tm[:, DM * j:DM * (j + 1)],
                    [ld_x[j]], norm_waits=([("pe", pe_tr[j - 2])] if j >= 2 else ()))
        ln_done[j] = d
        sq_act[j] = lnM.last_sq_act

        def _tr(e, j=j, d=d):
            ctx.w(e, "dve", d)
            if j >= 1:
                ctx.w(e, "dve", dve_fm[j - 1])
            last = None
            for b in range(8):
                last = e.transpose(ptr[:, 128 * b:128 * (b + 1)],
                                   lnx3tm[:, DM * j + 128 * b:DM * j + 128 * (b + 1)],
                                   ident[:, :])
            ctx.inc(last, "pe")
        ctx.sec("tensor", _tr)
        pe_tr[j] = ctx.cnt["pe"]

        def _ev(e, j=j):
            ctx.w(e, "pe", pe_tr[j])
            dst = V(lnx3fm, 128 * j, [[8 * TC, 128], [TC, 8], [1, 128]])
            e.tensor_copy(dst, ptr[:, :])
        ctx.sec("vector", _ev)
        dve_fm[j] = ctx.cnt["dve"]
    lnx_done = dve_fm[NT - 1]
    esA.close()

    es3 = ExitStack()
    D_sb = sal(ctx, es3, "m_D", [128, NT * NP], BF16)
    e_d = sal(ctx, es3, "m_ed", [128, NP], BF16)
    sd = sal(ctx, es3, "m_sd", [128, 2 * NEXP], F32)
    plog = pal(ctx, es3, "m_plog", [128, NP], F32)
    pe_lg = [0] * NT
    act_ex = [0] * NT
    dve_D = [0] * NT
    for I in range(NT):
        def _mm(e, I=I):
            if I == 0:
                ctx.w(e, "dve", lnx_done)
                ctx.wp(e, phit_ld)
                ctx.wp(e, pl_tot)
            if I >= 1:
                ctx.w(e, "act", act_ex[I - 1])
            last = None
            for ct in range(8):
                lhsT = V(lnx3fm, ct * TC + 128 * I, [[8 * TC, 128], [1, 128]])
                for h in range(4):
                    last = e.matmul(plog[:, 512 * h:512 * (h + 1)], lhsT,
                                    phit[:, ct * NP + 512 * h:ct * NP + 512 * (h + 1)],
                                    start=(ct == 0), stop=False,
                                    skip_group_check=True)
            for h in range(4):
                last = e.matmul(plog[:, 512 * h:512 * (h + 1)], ones1[0:1, :],
                                phib[0:1, 512 * h:512 * (h + 1)],
                                start=False, stop=(h == 3), skip_group_check=True)
            ctx.inc(last, "pe")
        ctx.sec("tensor", _mm)
        pe_lg[I] = ctx.cnt["pe"]

        def _ex(e, I=I):
            ctx.w(e, "pe", pe_lg[I])
            if I >= 1:
                ctx.w(e, "dve", dve_D[I - 1])
            e.activation(out=e_d[:, :], in_=plog[:, :], func=AF.Exp)
        ctx.sec("scalar", _ex)
        act_ex[I] = ctx.cnt["act"]

        def _D(e, I=I):
            ctx.w(e, "act", act_ex[I])
            rin = V(e_d, 0, [[NP, 128], [PSLOT, NEXP], [1, PSLOT]])
            e.reduce_sum(sd[:, 0:NEXP], rin, AX.X)
            e.reciprocal(sd[:, NEXP:2 * NEXP], sd[:, 0:NEXP])
            rv = V(sd, NEXP, [[2 * NEXP, 128], [1, NEXP], [0, PSLOT]])
            e.tensor_mul(D_sb[:, NP * I:NP * (I + 1)], e_d[:, :], rv)
        ctx.sec("vector", _D)
        dve_D[I] = ctx.cnt["dve"]

    es4 = ExitStack()
    xps = sal(ctx, es4, "m_xps", [128, DM], F32)
    pxp = [pal(ctx, es4, f"m_pxp{i}", [128, DM], F32) for i in range(2)]
    pe_xp = [0] * 16
    dve_xp = [0] * 16
    st_xp = [None] * 16
    for sn in range(16):
        def _mm(e, sn=sn):
            if sn == 0:
                ctx.w(e, "dve", dve_D[NT - 1])
            if sn >= 2:
                ctx.w(e, "dve", dve_xp[sn - 2])
            last = None
            for I in range(NT):
                lhsT = D_sb[:, NP * I + 128 * sn:NP * I + 128 * (sn + 1)]
                for h in range(2):
                    last = e.matmul(pxp[sn % 2][:, 512 * h:512 * (h + 1)], lhsT,
                                    lnx3tm[:, DM * I + 512 * h:DM * I + 512 * (h + 1)],
                                    start=(I == 0), stop=(I == NT - 1),
                                    skip_group_check=True)
            ctx.inc(last, "pe")
        ctx.sec("tensor", _mm)
        pe_xp[sn] = ctx.cnt["pe"]

        def _ev(e, sn=sn):
            ctx.w(e, "pe", pe_xp[sn])
            if sn >= 1:
                ctx.wp(e, st_xp[sn - 1])
            e.tensor_copy(xps[:, :], pxp[sn % 2][:, :])
        ctx.sec("vector", _ev)
        dve_xp[sn] = ctx.cnt["dve"]

        def _st(e, sn=sn):
            ctx.w(e, "dve", dve_xp[sn])
            st_xp[sn] = ctx.dma(e, xp_in[128 * sn:128 * (sn + 1), :],
                                xps[:, :], f"d{2 + sn % 2}")
        ctx.sec("scalar", _st)
    ctx.wgate = pe_xp[15]

    def _rs(e):
        for sv in st_xp[-2:]:
            ctx.wp(e, sv)
        ctx.inc(e.collective_compute(
            "ReduceScatter", ALU.add, replica_groups=[[0, 1, 2, 3, 4, 5, 6, 7]],
            ins=[xp_in[:, :]], outs=[xp_out[:, :]]), "cc")
    ctx.sec("gpsimd", _rs)
    cc_rs = ctx.cnt["cc"]
    es4.close()
    es3.close()
    es2.close()

    es5 = ExitStack()
    e_fm = sal(ctx, es5, "m_efm", [128, TC], F32)
    e_bf = [sal(ctx, es5, f"m_ebf{i}", [128, TC], BF16) for i in range(2)]
    sC = sal(ctx, es5, "m_sC", [32, TC], BF16)
    pfm = pal(ctx, es5, "m_pfm", [128, TC], F32)
    psC = pal(ctx, es5, "m_psC", [32, TC], F32)
    pe_fm = [0] * 16
    act_fx = [0] * 16
    dve_eb = [0] * 16
    pe_sC = [0] * 16
    st_eb = [None] * 16
    for sn in range(16):
        par = sn % 2

        def _mm(e, sn=sn):
            if sn >= 1:
                ctx.w(e, "act", act_fx[sn - 1])
            last = None
            for ct in range(8):
                lhsT = V(phit, ct * NP + 128 * sn, [[8 * NP, 128], [1, 128]])
                for h in range(4):
                    last = e.matmul(pfm[:, 512 * h:512 * (h + 1)], lhsT,
                                    V(lnx3fm, ct * TC + 512 * h,
                                      [[8 * TC, 128], [1, 512]]),
                                    start=(ct == 0), stop=(ct == 7),
                                    skip_group_check=True)
            ctx.inc(last, "pe")
        ctx.sec("tensor", _mm)
        pe_fm[sn] = ctx.cnt["pe"]

        def _ex(e, sn=sn):
            ctx.w(e, "pe", pe_fm[sn])
            if sn >= 1:
                ctx.w(e, "dve", dve_eb[sn - 1])
            e.activation(out=e_fm[:, :], in_=pfm[:, :], func=AF.Exp,
                         bias=phibc[:, sn:sn + 1])
        ctx.sec("scalar", _ex)
        act_fx[sn] = ctx.cnt["act"]

        def _eb(e, sn=sn, par=par):
            ctx.w(e, "act", act_fx[sn])
            if sn >= 2:
                ctx.wp(e, st_eb[sn - 2])
                ctx.w(e, "pe", pe_sC[sn - 2])
            e.tensor_copy(e_bf[par][:, :], e_fm[:, :])
        ctx.sec("vector", _eb)
        dve_eb[sn] = ctx.cnt["dve"]

        def _st(e, sn=sn, par=par):
            ctx.w(e, "dve", dve_eb[sn])
            if sn < 2:
                ctx.wp(e, st_xp[14 + sn])
            st_eb[sn] = ctx.dma(e, c_dr[128 * sn:128 * (sn + 1), :],
                                e_bf[par][:, :], f"d{2 + par}")
        ctx.sec("scalar", _st)

        def _sc(e, sn=sn, par=par):
            ctx.w(e, "dve", dve_eb[sn])
            last = None
            for h in range(4):
                last = e.matmul(psC[0:32, 512 * h:512 * (h + 1)], sel32[:, :],
                                e_bf[par][:, 512 * h:512 * (h + 1)],
                                start=(sn == 0), stop=(sn == 15),
                                skip_group_check=True)
            ctx.inc(last, "pe")
        ctx.sec("tensor", _sc)
        pe_sC[sn] = ctx.cnt["pe"]

    def _sce(e):
        ctx.w(e, "pe", pe_sC[15])
        e.tensor_copy(sC[0:32, :], psC[0:32, :])
    ctx.sec("vector", _sce)
    dve_sC = ctx.cnt["dve"]

    def _scb(e):
        ctx.w(e, "dve", dve_sC)
        last = None
        for h in range(4):
            last = e.matmul(pfm[:, 512 * h:512 * (h + 1)], rep32[0:32, :],
                            sC[0:32, 512 * h:512 * (h + 1)],
                            start=True, stop=True, skip_group_check=True)
        ctx.inc(last, "pe")
    ctx.sec("tensor", _scb)
    pe_scb = ctx.cnt["pe"]

    def _rse(e):
        ctx.w(e, "pe", pe_scb)
        e.reciprocal(rS[:, :], pfm[:, :])
    ctx.sec("vector", _rse)
    dve_rS = ctx.cnt["dve"]
    ctx.barrier()
    es5.close()
    es1.close()

    es6 = ExitStack()
    xrs = sal(ctx, es6, "m_xrs", [128, DM], F32)
    xcl = sal(ctx, es6, "m_xcl", [128, DM], F32)
    xln = sal(ctx, es6, "m_xln", [128, DM], F32)
    xtfm = sal(ctx, es6, "m_xtfm", [128, 8 * 256], BF16)
    xth = sal(ctx, es6, "m_xth", [128, DM], BF16)
    ptr2 = pal(ctx, es6, "m_ptr2", [128, DM], BF16)
    pe_trx = [0] * 2
    dcl_h = [0] * 2
    for half in range(2):
        def _ld(e, half=half):
            if half == 0:
                e.wait_ge(ctx.sems["cc"], cc_rs)
            else:
                ctx.w(e, "dve", dcl_h[0])
            ld = ctx.dma(e, xrs[:, :], xp_out[128 * half:128 * (half + 1), :],
                         f"d{half}")
            ctx.ld_xrs = ld
        ctx.sec("sync", _ld)
        ld_xrs = ctx.ld_xrs

        def _cl(e, half=half, ld=ld_xrs):
            ctx.wp(e, ld)
            if half == 1:
                ctx.w(e, "pe", pe_trx[0])
                ctx.w(e, "act", lnM.last_sq_act)
            e.tensor_scalar(xcl[:, :], xrs[:, :], -33000.0, 65000.0,
                            ALU.max, ALU.min)
        ctx.sec("vector", _cl)
        d_cl = ctx.cnt["dve"]
        dcl_h[half] = d_cl

        d = emit_ln(ctx, lnM, xcl[:, :], xln[:, :], [("dve", d_cl)])

        def _th(e, half=half, d=d):
            ctx.w(e, "dve", d)
            e.activation(out=xth[:, :], in_=xln[:, :], func=AF.Tanh)
        ctx.sec("scalar", _th)
        a_th = ctx.cnt["act"]

        def _tr(e, half=half, a=a_th):
            ctx.w(e, "act", a)
            last = None
            for b in range(8):
                last = e.transpose(ptr2[:, 128 * b:128 * (b + 1)],
                                   xth[:, 128 * b:128 * (b + 1)], ident[:, :])
            ctx.inc(last, "pe")
        ctx.sec("tensor", _tr)
        pe_trx[half] = ctx.cnt["pe"]

        def _ev(e, half=half):
            ctx.w(e, "pe", pe_trx[half])
            dst = V(xtfm, 128 * half, [[8 * 256, 128], [256, 8], [1, 128]])
            e.tensor_copy(dst, ptr2[:, :])
        ctx.sec("vector", _ev)
    dve_xt = ctx.cnt["dve"]

    ebias = sal(ctx, es6, "m_ebias", [1, 8 * DM], BF16)
    wex = [sal(ctx, es6, f"m_wex{i}", [128, 8 * DM], BF16) for i in range(2)]
    yrow = sal(ctx, es6, "m_yrow", [32, DM], BF16)
    pY = pal(ctx, es6, "m_pY", [32, DM], F32)

    def _ebl(e):
        ctx.dma(e, ebias[:, :], W["eb_shard"][:, :], "d7")
        e.wait_ge(ctx.sems["d7"], ctx.cnt["d7"])
    ctx.sec("gpsimd", _ebl)
    eb_tot = ("d7", ctx.cnt["d7"])

    ld_we = [None] * 8
    pe_Y = [0] * 8
    dve_Y = [0] * 8
    st_Y = [None] * 8
    for eL in range(8):
        par = eL % 2

        def _wl(e, eL=eL, par=par):
            if eL >= 2:
                ctx.w(e, "pe", pe_Y[eL - 2])
            else:
                ctx.w(e, "pe", ctx.wgate)
            src = V(W["ew_shard"], eL * DM * DM,
                    [[DM, 128], [128 * DM, 8], [1, DM]])
            ld_we[eL] = ctx.dma(
                e, wex[par].ap().rearrange("p (k c) -> p k c", k=8, c=DM),
                src, f"d{6 + par}")
        ctx.sec("gpsimd", _wl)

        def _mm(e, eL=eL, par=par):
            if eL == 0:
                ctx.w(e, "dve", dve_xt)
                ctx.wp(e, eb_tot)
            ctx.wp(e, ld_we[eL])
            if eL >= 1:
                ctx.w(e, "dve", dve_Y[eL - 1])
            last = None
            for ds in range(8):
                lhsT = V(xtfm, ds * 256 + 32 * eL, [[8 * 256, 128], [1, 32]])
                for h in range(2):
                    last = e.matmul(pY[0:32, 512 * h:512 * (h + 1)], lhsT,
                                    wex[par][:, ds * DM + 512 * h:ds * DM + 512 * (h + 1)],
                                    start=(ds == 0), stop=False,
                                    skip_group_check=True)
            for h in range(2):
                last = e.matmul(pY[0:32, 512 * h:512 * (h + 1)], ones1[0:1, 0:32],
                                ebias[0:1, eL * DM + 512 * h:eL * DM + 512 * (h + 1)],
                                start=False, stop=(h == 1), skip_group_check=True)
            ctx.inc(last, "pe")
        ctx.sec("tensor", _mm)
        pe_Y[eL] = ctx.cnt["pe"]

        def _ev(e, eL=eL):
            ctx.w(e, "pe", pe_Y[eL])
            if eL >= 1:
                ctx.wp(e, st_Y[eL - 1])
            e.tensor_copy(yrow[0:32, :], pY[0:32, :])
        ctx.sec("vector", _ev)
        dve_Y[eL] = ctx.cnt["dve"]

        def _st(e, eL=eL):
            ctx.w(e, "dve", dve_Y[eL])
            if eL < 2:
                ctx.wp(e, st_eb[14 + eL])
            st_Y[eL] = ctx.dma(e, yt_in[32 * eL:32 * (eL + 1), :],
                               yrow[0:32, :], f"d{2 + eL % 2}")
        ctx.sec("scalar", _st)

    def _ag(e):
        for sv in st_Y[-2:]:
            ctx.wp(e, sv)
        ctx.inc(e.collective_compute(
            "AllGather", ALU.bypass, replica_groups=[[0, 1, 2, 3, 4, 5, 6, 7]],
            ins=[yt_in[:, :]], outs=[yt_out[:, :]]), "cc")
    ctx.sec("gpsimd", _ag)
    cc_ag = ctx.cnt["cc"]
    es6.close()

    es7 = ExitStack()
    C_all = sal(ctx, es7, "m_C", [128, 16 * TC], BF16)
    ytk = sal(ctx, es7, "m_ytk", [128, NT * DM], BF16)
    ceb = [sal(ctx, es7, f"m_ceb{i}", [128, TC], BF16) for i in range(2)]
    o_tm = sal(ctx, es7, "m_otm", [128, DM], F32)
    lnn = sal(ctx, es7, "m_lnn", [128, DM], F32)
    x4t = [sal(ctx, es7, f"m_x4{i}", [128, DM], F32) for i in range(2)]
    pY2 = [pal(ctx, es7, f"m_pY2{i}", [128, DM], F32) for i in range(2)]

    def _ytl(e):
        e.wait_ge(ctx.sems["cc"], cc_ag)
        src = V(yt_out, 0, [[DM, 128], [128 * DM, NT], [1, DM]])
        ctx.ld_ytk = ctx.dma(
            e, ytk.ap().rearrange("p (g c) -> p g c", g=NT, c=DM), src, "d6")
    ctx.sec("gpsimd", _ytl)
    ld_ytk = ctx.ld_ytk

    ld_eb2 = [None] * 16
    dve_C = [0] * 16
    for sn in range(16):
        par = sn % 2

        def _ld(e, sn=sn, par=par):
            if sn >= 2:
                ctx.w(e, "dve", dve_C[sn - 2])
            else:
                ctx.w(e, "dve", dcl_h[1])
                ctx.wp(e, st_eb[14 + sn])
            ld_eb2[sn] = ctx.dma(e, ceb[par][:, :],
                                 c_dr[128 * sn:128 * (sn + 1), :], f"d{par}")
        ctx.sec("sync", _ld)

        def _C(e, sn=sn, par=par):
            ctx.wp(e, ld_eb2[sn])
            if sn == 0:
                ctx.w(e, "dve", dve_rS)
            e.tensor_mul(C_all[:, TC * sn:TC * (sn + 1)], ceb[par][:, :],
                         rS[:, :])
        ctx.sec("vector", _C)
        dve_C[sn] = ctx.cnt["dve"]

    pe_Y2 = [0] * NT
    dve_o = [0] * NT
    ld_x3 = [None] * NT
    st_x4 = [None] * NT
    for sl in range(NT):
        par = sl % 2

        def _ld(e, sl=sl, par=par):
            if sl >= 2:
                ctx.w(e, "dve", dve_o[sl - 2])
            else:
                ctx.w(e, "dve", ln_done[NT - 2 + sl])
                ctx.w(e, "act", sq_act[NT - 2 + sl])
                e.wait_ge(ctx.sems["cc"], cc_ag)
            ld_x3[sl] = ctx.dma(e, xt[par][:, :],
                                x_in[128 * sl:128 * (sl + 1), :], f"d{2 + par}")
        ctx.sec("sync", _ld)

        def _mm(e, sl=sl):
            if sl == 0:
                ctx.w(e, "dve", dve_C[15])
                ctx.wp(e, ld_ytk)
            if sl >= 2:
                ctx.w(e, "dve", dve_o[sl - 2])
            last = None
            for sn in range(16):
                lhsT = C_all[:, TC * sn + 128 * sl:TC * sn + 128 * (sl + 1)]
                for h in range(2):
                    last = e.matmul(pY2[sl % 2][:, 512 * h:512 * (h + 1)], lhsT,
                                    ytk[:, DM * sn + 512 * h:DM * sn + 512 * (h + 1)],
                                    start=(sn == 0), stop=(sn == 15),
                                    skip_group_check=True)
            ctx.inc(last, "pe")
        ctx.sec("tensor", _mm)
        pe_Y2[sl] = ctx.cnt["pe"]

        def _oe(e, sl=sl):
            ctx.w(e, "pe", pe_Y2[sl])
            ctx.w(e, "act", lnMo.last_sq_act)
            e.tensor_copy(o_tm[:, :], pY2[sl % 2][:, :])
        ctx.sec("vector", _oe)
        dve_oe = ctx.cnt["dve"]

        d = emit_ln(ctx, lnMo, o_tm[:, :], lnn[:, :],
                    [("dve", dve_oe), ld_x3[sl]],
                    res_add_ap=xt[par][:, :], dst2=x4t[par][:, :],
                    norm_waits=([st_x4[sl - 2]] if sl >= 2 else ()))
        dve_o[sl] = d

        def _st(e, sl=sl, par=par):
            ctx.w(e, "dve", dve_o[sl])
            st_x4[sl] = ctx.dma(e, y_out[128 * sl:128 * (sl + 1), :],
                                x4t[par][:, :], f"d{4 + par}")
        ctx.sec("scalar", _st)

    ctx.barrier(drains=st_x4[-2:])
    es7.close()
    es.close()


from concourse.bass_utils import run_bass_kernel_spmd

_CACHE = {}


def kernel(**inputs):
    sh, pc = prep_inputs(inputs)
    if "nc" not in _CACHE:
        _CACHE["nc"] = build(("l0a", "l0f", "l1a", "l1f"))
    nc = _CACHE["nc"]
    declared = set(nc._mk_W.keys()) | {"x_ext"}
    in_maps = [{k: v for k, v in dict(sh, **pc[c]).items() if k in declared}
               for c in range(NCORES)]
    res = run_bass_kernel_spmd(nc, in_maps, core_ids=list(range(NCORES)))
    out = np.concatenate([np.asarray(res.results[c]["y_out"])
                          for c in range(NCORES)], axis=0)
    return out.reshape(2, 8192, DM).astype(np.float32)


# version 16
# speedup vs baseline: 1.0123x; 1.0123x over previous
from contextlib import ExitStack
import numpy as np
import ml_dtypes
import concourse.bass as bass
import concourse.mybir as mybir

F32 = mybir.dt.float32
BF16 = mybir.dt.bfloat16
AF = mybir.ActivationFunctionType
ALU = mybir.AluOpType
AX = mybir.AxisListType
BF = ml_dtypes.bfloat16

NCORES = 8
TC = 2048
HALO = 4096
TE = TC + HALO
NT = TC // 128
NTH = TE // 128
DM = 1024
NH, HD = 16, 64
BANDS = [0, 1, 2, 3, 4, 8, 16, 32, 64, 128, 256, 512, 1024, 2048, 4096]
NBANDS = 15
BGROUPS = [(0, [0, 1, 2, 3, 4]), (5, [8, 16]), (7, [32, 64]),
           (9, [128, 256]), (11, [512, 1024]), (13, [2048, 4096])]
PV_CUM = []
_c = 0
for _gi, (_pos, _ds) in enumerate(BGROUPS):
    _c += (1 + 2 * (len(_ds) - 1)) if _gi == 0 else 2 * len(_ds)
    PV_CUM.append(_c)
PV_OPS_I = PV_CUM[-1] + 1

DECAY = 0.9
LN_EPS = 1e-5
NEXP, PSLOT = 64, 32
NP = NEXP * PSLOT
DIN = 2 * DM
DH = 4096


def prep_inputs(inputs):
    f = lambda a: np.ascontiguousarray(np.asarray(a), dtype=np.float32)
    sh = {}
    x = f(inputs["L0_attn_x"]).reshape(2 * 8192, DM)
    bias_cat = []
    for p in ("q", "k", "v", "o"):
        Wv = f(inputs[f"L0_attn_{p}_Wv"])
        Wr = f(inputs[f"L0_attn_{p}_Wr"])
        Wl = f(inputs[f"L0_attn_{p}_Wl"])
        sh[f"u_{p}"] = np.concatenate([Wv, Wr, Wl], 0).reshape(22 * 128, 128)
        bias_cat.append(f(inputs[f"L0_attn_{p}_b"]))
    sh["ubias"] = np.concatenate(bias_cat).reshape(1, 4 * DM)
    sh["w1T"] = f(inputs["L0_ffn_w1_w"].T)
    sh["w2T"] = f(inputs["L0_ffn_w2_w"].T)
    sh["h1T"] = f(inputs["L0_ffn_h1_w"].T)
    sh["h2T"] = f(inputs["L0_ffn_h2_w"].T)
    sh["w3T"] = f(inputs["L0_ffn_w3_w"].T)
    cm = lambda s: f(np.asarray(s).reshape(-1, 128).T)
    sh["s1cm"] = cm(inputs["L0_ffn_w1_s"])
    sh["s2cm"] = cm(inputs["L0_ffn_w2_s"])
    sh["h1scm"] = cm(inputs["L0_ffn_h1_s"])
    sh["h2scm"] = cm(inputs["L0_ffn_h2_s"])
    sh["s3row"] = f(inputs["L0_ffn_w3_s"]).reshape(1, DM)
    sh["s3cm"] = cm(inputs["L0_ffn_w3_s"])
    sh["Win"] = f(inputs["L1_attn_Win"])
    sh["bincm"] = cm(inputs["L1_attn_bin_"])
    sh["Bp"] = f(inputs["L1_attn_Bp"])
    sh["CpT"] = f(inputs["L1_attn_Cp"].T)
    sh["Dpcm"] = cm(inputs["L1_attn_Dp"])
    sh["Wout"] = f(inputs["L1_attn_Wout"])
    sh["boutrow"] = f(inputs["L1_attn_bout"]).reshape(1, DM)
    sh["phiT"] = f(inputs["L1_ffn_phi_w"].T)
    sh["phibcm"] = cm(inputs["L1_ffn_phi_b"])
    sh["phibrow"] = f(inputs["L1_ffn_phi_b"]).reshape(1, NP)
    ew = f(inputs["L1_ffn_expert_w"])
    eb = f(inputs["L1_ffn_expert_b"])
    sh["ident"] = np.eye(128, dtype=BF)
    sh["ones1x128"] = np.ones((1, 128), dtype=BF)
    sh["oav"] = np.full((128, 128), 1.0 / DM, dtype=BF)
    sh["sel32"] = (np.arange(128)[:, None] % 32 == np.arange(32)[None, :]).astype(BF)
    sh["rep32"] = (np.arange(32)[:, None] == np.arange(128)[None, :] % 32).astype(BF)
    t_loc = np.arange(TC)
    sh["decaycm"] = (DECAY ** (t_loc + 1.0)).astype(np.float32).reshape(NT, 128).T

    per_core = []
    for c in range(NCORES):
        d = {}
        r0 = c * TC
        xe = np.zeros((TE, DM), dtype=np.float32)
        lo = r0 - HALO
        b0 = (c // 4) * 8192
        lo_c = max(lo, b0)
        xe[lo_c - lo:] = x[lo_c:r0 + TC]
        d["x_ext"] = xe
        tb = (c % 4) * TC + np.arange(TC)
        m = np.where(tb[:, None] - np.asarray(BANDS)[None, :] >= 0, 0.0, -1e30)
        d["masks"] = np.ascontiguousarray(m.astype(np.float32).reshape(NT, 128, NBANDS))
        selr = np.zeros((128, 4), dtype=np.float32)
        if c % 4 > 0:
            selr[:, (c % 4) - 1] = 1.0
        d["selread"] = selr
        d["ew_shard"] = np.ascontiguousarray(ew[8 * c:8 * c + 8])
        d["eb_shard"] = np.ascontiguousarray(eb[8 * c:8 * c + 8].reshape(1, 8 * DM))
        per_core.append(d)
    return sh, per_core


def V(handle_or_ap, off, dims):
    a = handle_or_ap if isinstance(handle_or_ap, bass.AP) else handle_or_ap.ap()
    return bass.AP(a.tensor, off, [list(d) for d in dims])


CHAINED = {
    "tensor_copy", "tensor_tensor", "tensor_add", "tensor_sub", "tensor_mul",
    "tensor_max", "tensor_scalar", "tensor_single_scalar", "tensor_scalar_add",
    "tensor_scalar_sub", "tensor_scalar_mul", "tensor_scalar_max",
    "tensor_scalar_min", "tensor_relu", "scalar_tensor_tensor", "tensor_reduce",
    "reduce_sum", "reduce_max", "reciprocal", "memset", "memzero",
    "tensor_tensor_scan", "select", "transpose", "bn_stats", "bn_aggr",
    "activation", "copy", "mul", "add", "square", "sqrt", "sign", "iota",
    "affine_select",
}
ENG_SEM = {"vector": "dve", "scalar": "act", "gpsimd": "pool"}


class ChainEng:
    def __init__(self, ctx, eng, sem):
        self._ctx, self._eng, self._sem = ctx, eng, sem

    def __getattr__(self, name):
        f = getattr(self._eng, name)
        if name not in CHAINED:
            return f
        ctx, eng, sem = self._ctx, self._eng, self._sem

        def wrap(*a, **k):
            if ctx.cnt[sem] > 0:
                eng.wait_ge(ctx.sems[sem], ctx.cnt[sem])
            inst = f(*a, **k)
            ctx.inc(inst, sem)
            return inst
        return wrap


class Ctx:
    def __init__(self, nc, block, sems):
        self.nc, self.block, self.sems = nc, block, sems
        self.cnt = {k: 0 for k in sems}
        self.nbar = 0

    def inc(self, inst, s, n=1):
        self.cnt[s] += n
        inst.then_inc(self.sems[s], n)
        return self.cnt[s]

    def w(self, eng, s, val):
        if val > 0:
            eng.wait_ge(self.sems[s], val)

    def wp(self, eng, sv):
        if sv is not None:
            s, v = sv
            if v > 0:
                eng.wait_ge(self.sems[s], v)

    def dma(self, eng, out, in_, sem):
        inst = eng.dma_start(out=out, in_=in_)
        return (sem, self.inc(inst, sem, 16))

    def sec(self, name, fn):
        if name in ENG_SEM:
            sem = ENG_SEM[name]
            getattr(self.block, name)(lambda e: fn(ChainEng(self, e, sem)))
        else:
            getattr(self.block, name)(fn)

    def barrier(self, drains=()):
        self.nbar += 1
        target = 5 * self.nbar
        for name, own in (("sync", None), ("tensor", "pe"), ("vector", "dve"),
                          ("scalar", "act"), ("gpsimd", "pool")):
            def _f(eng, t=target, own=own, first=(name == "sync")):
                if first:
                    for sv in drains:
                        self.wp(eng, sv)
                if own is not None and self.cnt[own] > 0:
                    eng.wait_ge(self.sems[own], self.cnt[own])
                eng.nop().then_inc(self.sems["bar"], 1)
                eng.wait_ge(self.sems["bar"], t)
            getattr(self.block, name)(_f)
        self.cnt["bar"] = target


def sal(ctx, es, name, shape, dt):
    free_bytes = int(np.prod(shape[1:])) * (2 if dt == BF16 else 4)
    ab = 2048 if free_bytes >= 8192 else 32
    return es.enter_context(ctx.nc.sbuf_tensor(name, shape, dt, align_bytes=ab))


def pal(ctx, es, name, shape, dt):
    return es.enter_context(ctx.nc.psum_tensor(name, shape, dt))


class LNState:
    def __init__(self, ctx, es, tag):
        self.scr = sal(ctx, es, f"lnscr_{tag}", [128, 1024], F32)
        self.stat = sal(ctx, es, f"lnstat_{tag}", [128, 8], F32)
        self.last_var_dve = 0
        self.last_sq_act = 0


def emit_ln(ctx, ln, src_ap, dst_ap, ready, res_add_ap=None, dst2=None,
            norm_waits=()):
    st = ln.stat
    inv = 1.0 / DM

    def _act1(e, ready=tuple(ready), war=ln.last_var_dve):
        for s_, v_ in ready:
            ctx.w(e, s_, v_)
        ctx.w(e, "dve", war)
        e.activation(out=ln.scr[:, :], in_=src_ap, func=AF.Square,
                     accum_out=st[:, 1:2])
    ctx.sec("scalar", _act1)
    ln.last_sq_act = a_sq = ctx.cnt["act"]

    def _vec1(e, ready=tuple(ready), a=a_sq):
        for s_, v_ in ready:
            ctx.w(e, s_, v_)
        e.reduce_sum(st[:, 0:1], src_ap, AX.X)
        e.tensor_scalar(st[:, 2:3], st[:, 0:1], inv, None, ALU.mult)
        ctx.w(e, "act", a)
        e.tensor_mul(st[:, 3:4], st[:, 2:3], st[:, 2:3])
        e.scalar_tensor_tensor(st[:, 4:5], st[:, 1:2], inv, st[:, 3:4],
                               ALU.mult, ALU.subtract)
    ctx.sec("vector", _vec1)
    d_var = ctx.cnt["dve"]
    ln.last_var_dve = d_var

    def _act2(e, d=d_var):
        ctx.w(e, "dve", d)
        e.activation(out=st[:, 6:7], in_=st[:, 4:5], func=AF.Sqrt,
                     bias=ctx.eps[:, 0:1])
    ctx.sec("scalar", _act2)
    a_rstd = ctx.cnt["act"]

    def _vec2(e, a=a_rstd, nw=tuple(norm_waits)):
        ctx.w(e, "act", a)
        for s_, v_ in nw:
            ctx.w(e, s_, v_)
        e.reciprocal(st[:, 5:6], st[:, 6:7])
        e.tensor_scalar(dst_ap, src_ap, st[:, 2:3], st[:, 5:6],
                        ALU.subtract, ALU.mult)
        if res_add_ap is not None:
            e.tensor_add(dst2, dst_ap, res_add_ap)
    ctx.sec("vector", _vec2)
    return ctx.cnt["dve"]


def build(phases=("l0a", "l0f", "l1a", "l1f")):
    nc = bass.Bass("TRN2", target_bir_lowering=False)

    x_ext = nc.dram_tensor("x_ext", [TE, DM], F32, kind="ExternalInput").ap()
    y_out = nc.dram_tensor("y_out", [TC, DM], F32, kind="ExternalOutput").ap()
    xdr = [nc.dram_tensor(f"xdr{i}", [TC, DM], F32).ap() for i in range(2)]
    kv_dr = nc.dram_tensor("kv_dr", [TE, 2 * DM], BF16).ap()
    uag_in = nc.dram_tensor("uag_in", [128, 16], F32).ap()
    uag_out = nc.dram_tensor("uag_out", [512, 16], F32).ap()
    xp_in = nc.dram_tensor("xp_in", [NP, DM], F32).ap()
    xp_out = nc.dram_tensor("xp_out", [NP // NCORES, DM], F32).ap()
    c_dr = nc.dram_tensor("c_dr", [NP, TC], BF16).ap()
    yt_in = nc.dram_tensor("yt_in", [8 * PSLOT, DM], BF16).ap()
    yt_out = nc.dram_tensor("yt_out", [NP, DM], BF16, addr_space="Shared" if "l1f" in phases else "Local").ap()

    WSPECS = {"ubias": ([1, 4 * DM], F32), "w1T": ([DM, DH], F32),
              "w2T": ([DM, DH], F32), "h1T": ([DH, DH], F32),
              "h2T": ([DH, DH], F32), "w3T": ([DH, DM], F32),
              "s1cm": ([128, 32], F32), "s2cm": ([128, 32], F32),
              "h1scm": ([128, 32], F32), "h2scm": ([128, 32], F32),
              "s3row": ([1, DM], F32), "s3cm": ([128, 8], F32),
              "Win": ([DM, DIN], F32), "bincm": ([128, 16], F32),
              "Bp": ([DIN, 16], F32), "CpT": ([DIN, 16], F32),
              "Dpcm": ([128, 16], F32), "Wout": ([DIN, DM], F32),
              "boutrow": ([1, DM], F32), "phiT": ([DM, NP], F32),
              "phibcm": ([128, 16], F32), "phibrow": ([1, NP], F32),
              "decaycm": ([128, NT], F32), "ew_shard": ([8, DM, DM], F32),
              "eb_shard": ([1, 8 * DM], F32), "masks": ([NT, 128, NBANDS], F32),
              "selread": ([128, 4], F32), "ident": ([128, 128], BF16),
              "ones1x128": ([1, 128], BF16), "oav": ([128, 128], BF16),
              "sel32": ([128, 32], BF16), "rep32": ([32, 128], BF16)}
    for p in ("q", "k", "v", "o"):
        WSPECS[f"u_{p}"] = ([22 * 128, 128], F32)

    class LazyW(dict):
        def __missing__(self, name):
            shp, dt = WSPECS[name]
            ap = nc.dram_tensor(name, list(shp), dt, kind="ExternalInput").ap()
            self[name] = ap
            return ap
    W = LazyW()
    nc._mk_W = W

    sems, semcm = {}, []
    for s in ("d0", "d1", "d2", "d3", "d4", "d5", "d6", "d7",
              "pe", "dve", "act", "pool", "cc", "bar"):
        cm_ = nc.semaphore(s)
        sems[s] = cm_.__enter__()
        semcm.append(cm_)
    block_cm = nc.Block()
    block = block_cm.__enter__()
    ctx = Ctx(nc, block, sems)

    ident = nc.alloc_sbuf_tensor("sb_ident", [128, 128], BF16)
    ones1 = nc.alloc_sbuf_tensor("sb_ones1", [1, 128], BF16)

    ctx.eps = nc.alloc_sbuf_tensor("sb_eps", [128, 1], F32)

    def _vprolog(e):
        e.memset(ctx.eps[:, :], LN_EPS)
    ctx.sec("vector", _vprolog)

    def _prolog(e):
        ctx.dma(e, ident[:, :], W["ident"][:, :], "d7")
        ctx.dma(e, ones1[:, :], W["ones1x128"][:, :], "d7")
        e.wait_ge(sems["d7"], ctx.cnt["d7"])
    ctx.sec("gpsimd", _prolog)
    ctx.prolog_d7 = ctx.cnt["d7"]
    ctx.barrier()

    cur = None
    if "l0a" in phases:
        emit_l0a(ctx, W, x_ext, xdr[0], kv_dr, ident, ones1)
        cur = xdr[0]
    if "l0f" in phases:
        src = cur if cur is not None else V(x_ext, HALO * DM, [[DM, TC], [1, DM]])
        emit_l0f(ctx, W, src, xdr[1], ident)
        cur = xdr[1]
    if "l1a" in phases:
        src = cur if cur is not None else V(x_ext, HALO * DM, [[DM, TC], [1, DM]])
        emit_l1a(ctx, W, src, xdr[0], ident, ones1, uag_in, uag_out)
        cur = xdr[0]
    if "l1f" in phases:
        src = cur if cur is not None else V(x_ext, HALO * DM, [[DM, TC], [1, DM]])
        emit_l1f(ctx, W, src, y_out, ident, ones1, xp_in, xp_out, c_dr,
                 yt_in, yt_out)
        cur = None
    if cur is not None:
        emit_copyout(ctx, cur, y_out)

    ctx.barrier()
    block_cm.__exit__(None, None, None)
    for cm_ in semcm:
        cm_.__exit__(None, None, None)
    return nc


def emit_copyout(ctx, src_dr, y_out):
    nc = ctx.nc
    es = ExitStack()
    t = sal(ctx, es, "cp_t", [128, DM], F32)
    first = [True]
    for s in range(NT):
        def _m(e, s=s):
            if first[0]:
                for sv in getattr(ctx, "pending", []):
                    ctx.wp(e, sv)
                first[0] = False
            sv = ctx.dma(e, t[:, :], src_dr[128 * s:128 * (s + 1), :], "d0")
            ctx.wp(e, sv)
            sv2 = ctx.dma(e, y_out[128 * s:128 * (s + 1), :], t[:, :], "d0")
            ctx.wp(e, sv2)
        ctx.sec("sync", _m)
    es.close()


def emit_l0a(ctx, W, x_ext, x1_dr, kv_dr, ident, ones1):
    nc = ctx.nc
    es0 = ExitStack()
    esA = ExitStack()
    qsb = sal(ctx, es0, "a_qsb", [128, NT * DM], BF16)
    wu = {"o": sal(ctx, es0, "a_wu_o", [128, 22 * 128], BF16)}
    obias = sal(ctx, es0, "a_obias", [1, DM], BF16)
    xt = [sal(ctx, esA, f"a_xt{i}", [128, DM], F32) for i in range(3)]
    lnt = [sal(ctx, esA, f"a_lnt{i}", [128, DM], BF16) for i in range(3)]
    lnf = [sal(ctx, esA, f"a_lnf{i}", [128, DM], BF16) for i in range(2)]
    kvst = [sal(ctx, esA, f"a_kvst{i}", [128, 2 * DM], BF16) for i in range(3)]
    for p in ("q", "k", "v"):
        wu[p] = sal(ctx, esA, f"a_wu_{p}", [128, 22 * 128], BF16)
    bbc = sal(ctx, esA, "a_bbc_qkv", [128, 3 * DM], BF16)
    ubias = sal(ctx, esA, "sb_ubias", [1, 4 * DM], BF16)
    ln = LNState(ctx, esA, "a")
    ptr = [pal(ctx, esA, f"a_ptr{i}", [128, DM], BF16) for i in range(2)]
    pj = {p: pal(ctx, esA, f"a_p{p}", [128, DM], F32) for p in ("q", "k", "v")}

    def _pl(e):
        ctx.dma(e, ubias[:, :], W["ubias"][:, :], "d7")
        for p in ("q", "k", "v", "o"):
            src = V(W[f"u_{p}"], 0, [[128, 128], [128 * 128, 22], [1, 128]])
            dst = wu[p].ap().rearrange("p (j c) -> p j c", j=22, c=128)
            ctx.dma(e, dst, src, "d7")
        e.wait_ge(ctx.sems["d7"], ctx.cnt["d7"])
    ctx.sec("gpsimd", _pl)
    d7_w = ctx.cnt["d7"]

    prev_evac = 0
    for j in range(3):
        def _pe(e, j=j, pv=prev_evac):
            if j == 0:
                ctx.w(e, "d7", d7_w)
            ctx.w(e, "dve", pv)
            last = None
            for h in range(2):
                last = e.matmul(pj["v"][:, 512 * h:512 * (h + 1)], ones1[0:1, :],
                                ubias[0:1, 1024 * j + 512 * h:1024 * j + 512 * (h + 1)],
                                start=True, stop=True)
            ctx.inc(last, "pe")
        ctx.sec("tensor", _pe)
        pe_now = ctx.cnt["pe"]

        def _ve(e, j=j, pe_now=pe_now):
            ctx.w(e, "pe", pe_now)
            if j == 0:
                e.tensor_copy(obias[0:1, :], ubias[0:1, 3 * DM:4 * DM])
            e.tensor_copy(bbc[:, 1024 * j:1024 * (j + 1)], pj["v"][:, :])
        ctx.sec("vector", _ve)
        prev_evac = ctx.cnt["dve"]

    ln_done = [0] * NTH
    sq_act = [0] * NTH
    pe_tr = [0] * NTH
    dve_lnf = [0] * NTH
    pe_proj = [0] * NTH
    dve_evac = [0] * NTH
    st_kv = [None] * NTH
    ld_x = [None] * NTH

    for s in range(NTH + 1):
        own = s >= (NTH - NT)
        par = s % 3
        J = s - 1

        if s < NTH:
            def _load(e, s=s, par=par):
                if s >= 3:
                    ctx.w(e, "dve", ln_done[s - 3])
                    ctx.w(e, "act", sq_act[s - 3])
                ld_x[s] = ctx.dma(e, xt[par][:, :],
                                  x_ext[128 * s:128 * (s + 1), :], f"d{s % 2}")
            ctx.sec("sync" if s % 2 == 0 else "gpsimd", _load)

        if s >= 1:
            jown = J >= (NTH - NT)

            def _evac(e, J=J, jown=jown):
                ctx.w(e, "pe", pe_proj[J])
                if J >= 3:
                    ctx.wp(e, st_kv[J - 3])
                e.tensor_add(kvst[J % 3][:, 0:DM], pj["k"][:, :],
                             bbc[:, DM:2 * DM])
                e.tensor_add(kvst[J % 3][:, DM:2 * DM], pj["v"][:, :],
                             bbc[:, 2 * DM:3 * DM])
                if jown:
                    so = J - (NTH - NT)
                    e.tensor_add(qsb[:, DM * so:DM * (so + 1)], pj["q"][:, :],
                                 bbc[:, 0:DM])
            ctx.sec("vector", _evac)
            dve_evac[J] = ctx.cnt["dve"]

            def _st(e, J=J):
                ctx.w(e, "dve", dve_evac[J])
                st_kv[J] = ctx.dma(e, kv_dr[128 * J:128 * (J + 1), :],
                                   kvst[J % 3][:, :], f"d{2 + J % 2}")
            ctx.sec("scalar" if J % 2 == 0 else "gpsimd", _st)

        if s < NTH:
            d = emit_ln(ctx, ln, xt[par][:, :], lnt[par][:, :], [ld_x[s]],
                        norm_waits=([("pe", pe_tr[s - 3])] if s >= 3 else ()))
            ln_done[s] = d
            sq_act[s] = ln.last_sq_act

            def _tr(e, s=s, par=par, d=d):
                ctx.w(e, "dve", d)
                if s >= 2:
                    ctx.w(e, "dve", dve_lnf[s - 2])
                last = None
                for b in range(8):
                    last = e.transpose(ptr[s % 2][:, 128 * b:128 * (b + 1)],
                                       lnt[par][:, 128 * b:128 * (b + 1)],
                                       ident[:, :])
                ctx.inc(last, "pe")
            ctx.sec("tensor", _tr)
            pe_tr[s] = ctx.cnt["pe"]

            def _lnf(e, s=s):
                ctx.w(e, "pe", pe_tr[s])
                if s >= 2:
                    ctx.w(e, "pe", pe_proj[s - 2])
                e.tensor_copy(lnf[s % 2][:, :], ptr[s % 2][:, :])
            ctx.sec("vector", _lnf)
            dve_lnf[s] = ctx.cnt["dve"]

            projs = ("q", "k", "v") if own else ("k", "v")

            def _proj(e, s=s, projs=projs):
                ctx.w(e, "dve", dve_lnf[s])
                if s >= 1:
                    ctx.w(e, "dve", dve_evac[s - 1])
                started = {p: [False, False] for p in projs}
                last = None
                for bp in range(8):
                    lhsT = lnf[s % 2][:, 128 * bp:128 * (bp + 1)]
                    for p in projs:
                        tri = [(bp, bp)]
                        if bp + 1 < 8:
                            tri.append((bp + 1, 8 + bp))
                        if bp - 1 >= 0:
                            tri.append((bp - 1, 15 + bp - 1))
                        for (b, jj) in tri:
                            bank = b // 4
                            st = not started[p][bank]
                            started[p][bank] = True
                            last = e.matmul(pj[p][:, 128 * b:128 * (b + 1)],
                                            lhsT,
                                            wu[p][:, 128 * jj:128 * (jj + 1)],
                                            start=st, stop=False,
                                            skip_group_check=True)
                ctx.inc(last, "pe")
            ctx.sec("tensor", _proj)
            pe_proj[s] = ctx.cnt["pe"]

    d2_tot, d3_tot = ctx.cnt["d2"], ctx.cnt["d3"]
    ctx.barrier(drains=st_kv[-2:])
    esA.close()

    esB = ExitStack()
    CHG = [[(0, [0, 1, 2, 3, 4])],
           [(5, [8, 16]), (7, [32, 64]), (9, [128])],
           [(10, [256]), (11, [512, 1024]), (13, [2048, 4096])]]
    CW = 5 * DM
    PVROW = NBANDS * DM
    kcb = [sal(ctx, esB, f"b_kcb{i}", [128, CW], BF16) for i in range(3)]
    vcb = [sal(ctx, esB, f"b_vcb{i}", [128, CW], BF16) for i in range(3)]
    qkprod = sal(ctx, esB, "b_qkp", [128, CW], BF16)
    pvcP = sal(ctx, esB, "b_pvcP", [128, CW], BF16)
    pvcD = sal(ctx, esB, "b_pvcD", [128, CW], BF16)
    scor = [sal(ctx, esB, f"b_scor{i}", [128, NBANDS * NH], F32) for i in range(2)]
    escr = [sal(ctx, esB, f"b_escr{i}", [128, NBANDS * NH], F32) for i in range(2)]
    Pb = [sal(ctx, esB, f"b_P{i}", [128, NBANDS * NH], BF16) for i in range(2)]
    sm = sal(ctx, esB, "b_sm", [128, 3 * NH], F32)
    accA = [sal(ctx, esB, f"b_accA{i}", [128, DM], F32) for i in range(2)]
    accB = [sal(ctx, esB, f"b_accB{i}", [128, DM], F32) for i in range(2)]
    accb = [sal(ctx, esB, f"b_accb{i}", [128, DM], BF16) for i in range(2)]
    msk = sal(ctx, esB, "b_msk", [128, NT * NBANDS], F32)
    afm = [sal(ctx, esB, f"c_afm{i}", [128, DM], BF16) for i in range(2)]
    x0t = [sal(ctx, esB, f"c_x0{i}", [128, DM], F32) for i in range(2)]
    ot = [sal(ctx, esB, f"c_ot{i}", [128, DM], F32) for i in range(2)]
    lnn = sal(ctx, esB, "c_lnn", [128, DM], F32)
    x1t = [sal(ctx, esB, f"c_x1{i}", [128, DM], F32) for i in range(2)]
    lnC = LNState(ctx, esB, "c")
    ptrc = [pal(ctx, esB, f"c_ptr{i}", [128, DM], BF16) for i in range(2)]
    po = [pal(ctx, esB, f"c_po{i}", [128, DM], F32) for i in range(2)]

    def _mload(e):
        src = V(W["masks"], 0, [[NBANDS, 128], [128 * NBANDS, NT], [1, NBANDS]])
        dst = msk.ap().rearrange("p (i j) -> p i j", i=NT, j=NBANDS)
        ctx.dma(e, dst, src, "d7")
        e.wait_ge(ctx.sems["d7"], ctx.cnt["d7"])
    ctx.sec("gpsimd", _mload)
    d7_m = ctx.cnt["d7"]

    NC3 = 3 * NT
    ld_k = [None] * NC3
    ld_v = [None] * NC3
    dve_qkc = [0] * NC3
    pool_pvm = [0] * NC3
    dve_pvm = [0] * NC3
    dve_qk = [0] * NT
    act_exp = [0] * NT
    dve_P = [0] * NT
    pool_pvA = [0] * NT
    dve_fin = [0] * NT
    pe_tr = [0] * NT
    a_afm = [0] * NT
    pe_o = [0] * NT
    a_ot = [0] * NT
    dve_x1 = [0] * NT
    ld_x0 = [None] * NT
    st_x1 = [None] * NT
    kvrow = 2 * DM

    for I in range(NT + 1):
        par = I % 2
        J = I - 1
        jpar = J % 2
        base_row = HALO + 128 * I

        if I < NT:
            def _kload(e, I=I, base_row=base_row):
                if I == 0:
                    ctx.w(e, "d2", d2_tot)
                    ctx.w(e, "d3", d3_tot)
                for c in range(3):
                    g = 3 * I + c
                    if g >= 3:
                        ctx.w(e, "dve", dve_qkc[g - 3])
                    last = None
                    for (pos, ds) in CHG[c]:
                        step = ds[1] - ds[0] if len(ds) > 1 else 1
                        src = V(kv_dr, (base_row - ds[0]) * kvrow,
                                [[kvrow, 128], [-step * kvrow, len(ds)], [1, DM]])
                        dst = V(kcb[g % 3], DM * (pos - 5 * c),
                                [[CW, 128], [DM, len(ds)], [1, DM]])
                        last = ctx.dma(e, dst, src, f"d{g % 2}")
                    ld_k[g] = last
            ctx.sec("sync", _kload)

            def _vload(e, I=I, base_row=base_row):
                for c in range(3):
                    g = 3 * I + c
                    if g >= 3:
                        if c == 2:
                            ctx.w(e, "dve", dve_pvm[g - 3])
                        else:
                            ctx.w(e, "pool", pool_pvm[g - 3])
                    last = None
                    for (pos, ds) in CHG[c]:
                        step = ds[1] - ds[0] if len(ds) > 1 else 1
                        src = V(kv_dr, (base_row - ds[0]) * kvrow + DM,
                                [[kvrow, 128], [-step * kvrow, len(ds)], [1, DM]])
                        dst = V(vcb[g % 3], DM * (pos - 5 * c),
                                [[CW, 128], [DM, len(ds)], [1, DM]])
                        last = ctx.dma(e, dst, src, f"d{2 + g % 2}")
                    ld_v[g] = last
            ctx.sec("scalar", _vload)

            def _qk(e, I=I, par=par):
                if I == 0:
                    ctx.w(e, "d7", d7_m)
                if I >= 2:
                    ctx.w(e, "act", act_exp[I - 2])
                for c in range(3):
                    g = 3 * I + c
                    ctx.wp(e, ld_k[g])
                    qv = V(qsb, DM * I,
                           [[NT * DM, 128], [0, 5], [64, NH], [1, HD]])
                    kv_ = V(kcb[g % 3], 0,
                            [[CW, 128], [DM, 5], [64, NH], [1, HD]])
                    pv = V(qkprod, 0, [[CW, 128], [DM, 5], [64, NH], [1, HD]])
                    e.tensor_mul(pv, qv, kv_)
                    rin = V(qkprod, 0, [[CW, 128], [64, 5 * NH], [1, HD]])
                    e.reduce_sum(scor[par][:, 5 * NH * c:5 * NH * (c + 1)],
                                 rin, AX.X)
                    dve_qkc[g] = ctx.cnt["dve"]
                mv = V(msk, NBANDS * I, [[NT * NBANDS, 128], [1, NBANDS], [0, NH]])
                sv = V(scor[par], 0, [[NBANDS * NH, 128], [NH, NBANDS], [1, NH]])
                e.tensor_add(sv, sv, mv)
                svh = V(scor[par], 0, [[NBANDS * NH, 128], [1, NH], [NH, NBANDS]])
                e.reduce_max(sm[:, 0:NH], svh, AX.X)
                mxv = V(sm, 0, [[3 * NH, 128], [0, NBANDS], [1, NH]])
                e.tensor_sub(scor[par][:, :], scor[par][:, :], mxv)
            ctx.sec("vector", _qk)
            dve_qk[I] = ctx.cnt["dve"]

            def _exp(e, I=I, par=par):
                ctx.w(e, "dve", dve_qk[I])
                if I >= 2:
                    ctx.w(e, "dve", dve_P[I - 2])
                e.activation(out=escr[par][:, :], in_=scor[par][:, :],
                             func=AF.Exp, scale=0.125)
            ctx.sec("scalar", _exp)
            act_exp[I] = ctx.cnt["act"]

            def _P(e, I=I, par=par):
                ctx.w(e, "act", act_exp[I])
                if I >= 2:
                    ctx.w(e, "pool", pool_pvm[3 * (I - 2) + 1])
                ev = V(escr[par], 0, [[NBANDS * NH, 128], [1, NH], [NH, NBANDS]])
                e.reduce_sum(sm[:, NH:2 * NH], ev, AX.X)
                e.reciprocal(sm[:, 2 * NH:3 * NH], sm[:, NH:2 * NH])
                rzv = V(sm, 2 * NH, [[3 * NH, 128], [0, NBANDS], [1, NH]])
                e.tensor_mul(Pb[par][:, :], escr[par][:, :], rzv)
            ctx.sec("vector", _P)
            dve_P[I] = ctx.cnt["dve"]

            def _pvB(e, I=I, par=par):
                g = 3 * I + 2
                ctx.wp(e, ld_v[g])
                pvw = V(Pb[par], 10 * NH,
                        [[NBANDS * NH, 128], [NH, 5], [1, NH], [0, HD]])
                vbw = V(vcb[g % 3], 0, [[CW, 128], [DM, 5], [64, NH], [1, HD]])
                e.tensor_mul(pvcD[:, :], vbw, pvw)
                dve_pvm[g] = ctx.cnt["dve"]
                e.tensor_add(accB[par][:, :], pvcD[:, 0:DM], pvcD[:, DM:2 * DM])
                for d in range(2, 5):
                    e.tensor_add(accB[par][:, :], accB[par][:, :],
                                 pvcD[:, DM * d:DM * (d + 1)])
            ctx.sec("vector", _pvB)

            def _pvA(e, I=I, par=par):
                ctx.w(e, "dve", dve_P[I])
                if I >= 2:
                    ctx.w(e, "dve", dve_fin[I - 2])
                for c in range(2):
                    g = 3 * I + c
                    ctx.wp(e, ld_v[g])
                    pvw = V(Pb[par], 5 * c * NH,
                            [[NBANDS * NH, 128], [NH, 5], [1, NH], [0, HD]])
                    vbw = V(vcb[g % 3], 0, [[CW, 128], [DM, 5], [64, NH], [1, HD]])
                    e.tensor_mul(pvcP[:, :], vbw, pvw)
                    pool_pvm[g] = ctx.cnt["pool"]
                    if c == 0:
                        e.tensor_add(accA[par][:, :], pvcP[:, 0:DM],
                                     pvcP[:, DM:2 * DM])
                    else:
                        e.tensor_add(accA[par][:, :], accA[par][:, :],
                                     pvcP[:, 0:DM])
                        e.tensor_add(accA[par][:, :], accA[par][:, :],
                                     pvcP[:, DM:2 * DM])
                    for d in range(2, 5):
                        e.tensor_add(accA[par][:, :], accA[par][:, :],
                                     pvcP[:, DM * d:DM * (d + 1)])
            ctx.sec("gpsimd", _pvA)
            pool_pvA[I] = ctx.cnt["pool"]

            def _x0(e, I=I, par=par):
                if I >= 2:
                    ctx.w(e, "dve", dve_x1[I - 2])
                ld_x0[I] = ctx.dma(e, x0t[par][:, :],
                                   x_ext[HALO + 128 * I:HALO + 128 * (I + 1), :],
                                   "d6")
            ctx.sec("gpsimd", _x0)

        if I >= 1:
            def _fin(e, J=J, jpar=jpar):
                ctx.w(e, "pool", pool_pvA[J])
                if J >= 2:
                    ctx.w(e, "pe", pe_tr[J - 2])
                e.tensor_add(accb[jpar][:, :], accA[jpar][:, :],
                             accB[jpar][:, :])
            ctx.sec("vector", _fin)
            dve_fin[J] = ctx.cnt["dve"]

            def _tr(e, J=J, jpar=jpar):
                ctx.w(e, "dve", dve_fin[J])
                if J >= 2:
                    ctx.w(e, "act", a_afm[J - 2])
                last = None
                for b in range(8):
                    last = e.transpose(ptrc[jpar][:, 128 * b:128 * (b + 1)],
                                       accb[jpar][:, 128 * b:128 * (b + 1)],
                                       ident[:, :])
                ctx.inc(last, "pe")
            ctx.sec("tensor", _tr)
            pe_tr[J] = ctx.cnt["pe"]

            def _afm(e, J=J, jpar=jpar):
                ctx.w(e, "pe", pe_tr[J])
                if J >= 2:
                    ctx.w(e, "pe", pe_o[J - 2])
                e.activation(out=afm[jpar][:, :], in_=ptrc[jpar][:, :],
                             func=AF.Copy)
            ctx.sec("scalar", _afm)
            a_afm[J] = ctx.cnt["act"]

            def _oproj(e, J=J, jpar=jpar):
                ctx.w(e, "act", a_afm[J])
                if J >= 2:
                    ctx.w(e, "act", a_ot[J - 2])
                started = [False, False]
                for bp in range(8):
                    lhsT = afm[jpar][:, 128 * bp:128 * (bp + 1)]
                    tri = [(bp, bp)]
                    if bp + 1 < 8:
                        tri.append((bp + 1, 8 + bp))
                    if bp - 1 >= 0:
                        tri.append((bp - 1, 15 + bp - 1))
                    for (b, jj) in tri:
                        bank = b // 4
                        st = not started[bank]
                        started[bank] = True
                        last = e.matmul(po[jpar][:, 128 * b:128 * (b + 1)], lhsT,
                                        wu["o"][:, 128 * jj:128 * (jj + 1)],
                                        start=st, stop=False,
                                        skip_group_check=True)
                for h in range(2):
                    last = e.matmul(po[jpar][:, 512 * h:512 * (h + 1)],
                                    ones1[0:1, :],
                                    obias[0:1, 512 * h:512 * (h + 1)],
                                    start=False, stop=False,
                                    skip_group_check=True)
                ctx.inc(last, "pe")
            ctx.sec("tensor", _oproj)
            pe_o[J] = ctx.cnt["pe"]

            def _ot(e, J=J, jpar=jpar):
                ctx.w(e, "pe", pe_o[J])
                if J >= 2:
                    ctx.w(e, "dve", dve_x1[J - 2])
                e.activation(out=ot[jpar][:, :], in_=po[jpar][:, :],
                             func=AF.Copy)
            ctx.sec("scalar", _ot)
            a_ot[J] = ctx.cnt["act"]

            d = emit_ln(ctx, lnC, ot[jpar][:, :], lnn[:, :],
                        [("act", a_ot[J]), ld_x0[J]],
                        res_add_ap=x0t[jpar][:, :], dst2=x1t[jpar][:, :],
                        norm_waits=([st_x1[J - 2]] if J >= 2 else ()))
            dve_x1[J] = d

            def _stx(e, J=J, jpar=jpar):
                ctx.w(e, "dve", dve_x1[J])
                st_x1[J] = ctx.dma(e, x1_dr[128 * J:128 * (J + 1), :],
                                   x1t[jpar][:, :], f"d{4 + jpar}")
            ctx.sec("scalar", _stx)

    ctx.pending = [sv for sv in st_x1 if sv is not None][-2:]
    ctx.barrier(drains=st_x1[-2:])
    esB.close()
    es0.close()


def emit_l0f(ctx, W, x_in, x_out, ident):
    nc = ctx.nc
    NO1, NOH, NOC = 32, 32, 8
    drains = getattr(ctx, "pending", [])
    ctx.pending = []

    esS = ExitStack()
    for nm, w_ in (("s1cm", 32), ("s2cm", 32), ("h1scm", 32), ("h2scm", 32),
                   ("s3cm", 8)):
        W[nm + "_sb"] = sal(ctx, esS, "sb_" + nm, [128, w_], F32)

    def _sprolog(e):
        for nm in ("s1cm", "s2cm", "h1scm", "h2scm", "s3cm"):
            ctx.dma(e, W[nm + "_sb"][:, :], W[nm][:, :], "d7")
        e.wait_ge(ctx.sems["d7"], ctx.cnt["d7"])
    ctx.sec("gpsimd", _sprolog)
    scale_tot = ("d7", ctx.cnt["d7"])

    for tg in range(2):
        es = ExitStack()
        esL = ExitStack()
        g_sb = sal(ctx, es, f"f_g{tg}", [128, NO1 * 1024], BF16)
        xt = [sal(ctx, es, f"f_xt{tg}{i}", [128, DM], F32) for i in range(2)]
        lnt = [sal(ctx, es, f"f_lnt{tg}{i}", [128, DM], BF16) for i in range(2)]
        wh = [sal(ctx, es, f"f_wh{tg}{i}", [128, 32 * 128], BF16) for i in range(2)]
        dt_scr = sal(ctx, es, f"f_dt{tg}", [128, DM], BF16)
        x2t = [sal(ctx, es, f"f_x2{tg}{i}", [128, DM], F32) for i in range(2)]
        lnF = LNState(ctx, es, f"f{tg}")
        lnO = LNState(ctx, es, f"fo{tg}")
        lnx = sal(ctx, esL, f"f_lnx{tg}", [128, 8 * 1024], BF16)
        u_scr = sal(ctx, esL, f"f_u{tg}", [128, 1024], BF16)
        v_scr = sal(ctx, esL, f"f_v{tg}", [128, 1024], BF16)
        w12 = [sal(ctx, esL, f"f_w12{tg}{i}", [128, 16 * 128], BF16) for i in range(2)]
        pu = pal(ctx, es, f"f_pu{tg}", [128, 1024], F32)
        pv2 = pal(ctx, es, f"f_pv2{tg}", [128, 1024], F32)
        esP = ExitStack()
        ptr = pal(ctx, esP, f"f_ptr{tg}", [128, DM], BF16)

        ln_done = [0] * 8
        sq_act = [0] * 8
        pe_tr = [0] * 8
        dve_lnx = [0] * 8
        ld_x = [None] * 8
        for j in range(8):
            par = j % 2
            row0 = tg * 1024 + 128 * j

            def _ld(e, j=j, par=par, row0=row0, tg=tg):
                if tg == 0 and j == 0:
                    for sv in drains:
                        ctx.wp(e, sv)
                if j >= 2:
                    ctx.w(e, "dve", ln_done[j - 2])
                    ctx.w(e, "act", sq_act[j - 2])
                ld_x[j] = ctx.dma(e, xt[par][:, :],
                                  x_in[row0:row0 + 128, :], f"d{par}")
            ctx.sec("sync", _ld)

            d = emit_ln(ctx, lnF, xt[par][:, :], lnt[par][:, :], [ld_x[j]],
                        norm_waits=([("pe", pe_tr[j - 2])] if j >= 2 else ()))
            ln_done[j] = d
            sq_act[j] = lnF.last_sq_act

            def _tr(e, j=j, par=par, d=d):
                ctx.w(e, "dve", d)
                if j >= 1:
                    ctx.w(e, "dve", dve_lnx[j - 1])
                last = None
                for b in range(8):
                    last = e.transpose(ptr[:, 128 * b:128 * (b + 1)],
                                       lnt[par][:, 128 * b:128 * (b + 1)],
                                       ident[:, :])
                ctx.inc(last, "pe")
            ctx.sec("tensor", _tr)
            pe_tr[j] = ctx.cnt["pe"]

            def _ev(e, j=j):
                ctx.w(e, "pe", pe_tr[j])
                dst = V(lnx, 128 * j, [[8 * 1024, 128], [1024, 8], [1, 128]])
                e.tensor_copy(dst, ptr[:, :])
            ctx.sec("vector", _ev)
            dve_lnx[j] = ctx.cnt["dve"]

        lnx_done = dve_lnx[7]
        esP.close()
        esH = ExitStack()
        phw2 = [pal(ctx, esH, f"f_phw{tg}{i}", [128, 1024], F32) for i in range(2)]

        ld_w12 = [None] * NO1
        pe_uv = [0] * NO1
        act_u = [0] * NO1
        dve_g = [0] * NO1
        for o in range(NO1):
            par = o % 2

            def _wl(e, o=o, par=par):
                if o >= 2:
                    ctx.w(e, "pe", pe_uv[o - 2])
                else:
                    ctx.w(e, "pe", getattr(ctx, "wgate", 0))
                for wnm in ("w1T", "w2T"):
                    off = 0 if wnm == "w1T" else 8 * 128
                    src = V(W[wnm], 128 * o,
                            [[DH, 128], [128 * DH, 8], [1, 128]])
                    dst = V(w12[par], off, [[16 * 128, 128], [128, 8], [1, 128]])
                    ld_w12[o] = ctx.dma(e, dst, src, f"d{6 + par}")
            ctx.sec("gpsimd", _wl)

            def _mm(e, o=o, par=par):
                if o == 0:
                    ctx.w(e, "dve", lnx_done)
                ctx.wp(e, ld_w12[o])
                last = None
                for wi, ps in ((0, pu), (1, pv2)):
                    if wi == 0 and o >= 1:
                        ctx.w(e, "act", act_u[o - 1])
                    if wi == 1 and o >= 1:
                        ctx.w(e, "dve", dve_g[o - 1])
                    for ki in range(8):
                        lhsT = w12[par][:, (8 * wi + ki) * 128:(8 * wi + ki + 1) * 128]
                        for h in range(2):
                            last = e.matmul(ps[:, 512 * h:512 * (h + 1)], lhsT,
                                            V(lnx, ki * 1024 + 512 * h,
                                              [[8 * 1024, 128], [1, 512]]),
                                            start=(ki == 0), stop=(ki == 7),
                                            skip_group_check=True)
                    if wi == 0:
                        ctx.inc(last, "pe")
                ctx.inc(last, "pe")
            ctx.sec("tensor", _mm)
            pe_uv[o] = ctx.cnt["pe"]
            pe_u_part = pe_uv[o] - 1

            def _ue(e, o=o, tg=tg, pw=pe_u_part):
                ctx.w(e, "pe", pw)
                if tg == 0 and o == 0:
                    ctx.wp(e, scale_tot)
                if o >= 1:
                    ctx.w(e, "dve", dve_g[o - 1])
                e.activation(out=u_scr[:, :], in_=pu[:, :], func=AF.Silu,
                             scale=W["s1cm_sb"][:, o:o + 1])
            ctx.sec("scalar", _ue)
            act_u[o] = ctx.cnt["act"]

            def _ge(e, o=o, tg=tg):
                ctx.w(e, "pe", pe_uv[o])
                if tg == 0 and o == 0:
                    ctx.wp(e, scale_tot)
                e.tensor_scalar(v_scr[:, :], pv2[:, :],
                                W["s2cm_sb"][:, o:o + 1], None, ALU.mult)
                ctx.w(e, "act", act_u[o])
                e.tensor_mul(g_sb[:, 1024 * o:1024 * (o + 1)], u_scr[:, :],
                             v_scr[:, :])
            ctx.sec("vector", _ge)
            dve_g[o] = ctx.cnt["dve"]

        esL.close()
        ctx.wgate = pe_uv[NO1 - 1]
        es_b = ExitStack()
        b_sb = sal(ctx, es_b, f"f_b{tg}", [128, NOH * 1024], BF16)

        def hpass(src_sb, dst_write, wname, scname, first_wait_dve, first_wait_act,
                  war_pe_total):
            ld_wh = [None] * NOH
            pe_h = [0] * NOH
            act_h = [0] * NOH
            for o in range(NOH):
                par = o % 2

                def _wl(e, o=o, par=par):
                    if o >= 2:
                        ctx.w(e, "pe", pe_h[o - 2])
                    else:
                        ctx.w(e, "pe", getattr(ctx, "wgate", 0))
                    src = V(W[wname], 128 * o,
                            [[DH, 128], [128 * DH, 32], [1, 128]])
                    dst = wh[par].ap().rearrange("p (k c) -> p k c", k=32, c=128)
                    ld_wh[o] = ctx.dma(e, dst, src, f"d{6 + par}")
                ctx.sec("gpsimd", _wl)

                def _mm(e, o=o, par=par):
                    if o == 0:
                        if first_wait_dve:
                            ctx.w(e, "dve", first_wait_dve)
                        if first_wait_act:
                            ctx.w(e, "act", first_wait_act)
                    ctx.wp(e, ld_wh[o])
                    if o >= 2:
                        ctx.w(e, "act", act_h[o - 2])
                    ph = phw2[o % 2]
                    last = None
                    for ki in range(32):
                        lhsT = wh[par][:, ki * 128:(ki + 1) * 128]
                        for h in range(2):
                            last = e.matmul(ph[:, 512 * h:512 * (h + 1)], lhsT,
                                            V(src_sb, ki * 1024 + 512 * h,
                                              [[NOH * 1024, 128], [1, 512]]),
                                            start=(ki == 0), stop=(ki == 31),
                                            skip_group_check=True)
                    ctx.inc(last, "pe")
                ctx.sec("tensor", _mm)
                pe_h[o] = ctx.cnt["pe"]

                def _be(e, o=o, tg=tg):
                    ctx.w(e, "pe", pe_h[o])
                    if tg == 0 and o == 0:
                        ctx.wp(e, scale_tot)
                    if war_pe_total and o == 0:
                        ctx.w(e, "pe", war_pe_total[0])
                    e.activation(out=dst_write(o), in_=phw2[o % 2][:, :],
                                 func=AF.Gelu, scale=W[scname][:, o:o + 1])
                ctx.sec("scalar", _be)
                act_h[o] = ctx.cnt["act"]
            return pe_h, act_h

        pe_h1, act_b = hpass(g_sb, lambda o: b_sb[:, 1024 * o:1024 * (o + 1)],
                             "h1T", "h1scm_sb", dve_g[NO1 - 1], 0, None)
        ctx.wgate = pe_h1[NOH - 1]
        pe_h2, act_c = hpass(b_sb, lambda o: g_sb[:, 1024 * o:1024 * (o + 1)],
                             "h2T", "h2scm_sb", 0, act_b[NOH - 1], [pe_h1[NOH - 1]])
        ctx.wgate = pe_h2[NOH - 1]
        es_b.close()
        es_d = ExitStack()
        dfm = sal(ctx, es_d, f"f_dfm{tg}", [128, 8 * 1024], BF16)

        ld_w3 = [None] * NOC
        pe_w3 = [0] * NOC
        dve_dfm = [0] * NOC
        for oc in range(NOC):
            par = oc % 2

            def _wl(e, oc=oc, par=par):
                if oc >= 2:
                    ctx.w(e, "pe", pe_w3[oc - 2])
                else:
                    ctx.w(e, "pe", getattr(ctx, "wgate", 0))
                src = V(W["w3T"], 128 * oc,
                        [[DM, 128], [128 * DM, 32], [1, 128]])
                dst = wh[par].ap().rearrange("p (k c) -> p k c", k=32, c=128)
                ld_w3[oc] = ctx.dma(e, dst, src, f"d{6 + par}")
            ctx.sec("gpsimd", _wl)

            def _mm(e, oc=oc, par=par):
                if oc == 0:
                    ctx.w(e, "act", act_c[NOH - 1])
                ctx.wp(e, ld_w3[oc])
                if oc >= 2:
                    ctx.w(e, "dve", dve_dfm[oc - 2])
                last = None
                for ki in range(32):
                    lhsT = wh[par][:, ki * 128:(ki + 1) * 128]
                    for h in range(2):
                        last = e.matmul(phw2[oc % 2][:, 512 * h:512 * (h + 1)], lhsT,
                                        V(g_sb, ki * 1024 + 512 * h,
                                          [[NO1 * 1024, 128], [1, 512]]),
                                        start=(ki == 0), stop=(ki == 31),
                                        skip_group_check=True)
                ctx.inc(last, "pe")
            ctx.sec("tensor", _mm)
            pe_w3[oc] = ctx.cnt["pe"]

            def _de(e, oc=oc, tg=tg):
                ctx.w(e, "pe", pe_w3[oc])
                if tg == 0 and oc == 0:
                    ctx.wp(e, scale_tot)
                e.tensor_scalar(dfm[:, 1024 * oc:1024 * (oc + 1)],
                                phw2[oc % 2][:, :],
                                W["s3cm_sb"][:, oc:oc + 1], None, ALU.mult)
            ctx.sec("vector", _de)
            dve_dfm[oc] = ctx.cnt["dve"]

        ctx.wgate = pe_w3[NOC - 1]
        esH.close()
        esP2 = ExitStack()
        ptrF = pal(ctx, esP2, f"f_ptrF{tg}", [128, DM], BF16)
        pe_trF = [0] * 8
        dve_out = [0] * 8
        st_x2 = [None] * 8
        ld_x1 = [None] * 8
        for j in range(8):
            par = j % 2
            row0 = tg * 1024 + 128 * j

            def _ld(e, j=j, par=par, row0=row0):
                if j >= 2:
                    ctx.w(e, "dve", dve_out[j - 2])
                else:
                    ctx.w(e, "dve", ln_done[6 + j])
                    ctx.w(e, "act", sq_act[6 + j])
                ld_x1[j] = ctx.dma(e, xt[par][:, :], x_in[row0:row0 + 128, :],
                                   f"d{par}")
            ctx.sec("sync", _ld)

            def _tr(e, j=j):
                if j == 0:
                    ctx.w(e, "dve", dve_dfm[NOC - 1])
                if j >= 1:
                    ctx.w(e, "dve", dve_out[j - 1])
                last = None
                for oc in range(8):
                    last = e.transpose(
                        ptrF[:, 128 * oc:128 * (oc + 1)],
                        dfm[:, 1024 * oc + 128 * j:1024 * oc + 128 * (j + 1)],
                        ident[:, :])
                ctx.inc(last, "pe")
            ctx.sec("tensor", _tr)
            pe_trF[j] = ctx.cnt["pe"]

            def _ev(e, j=j):
                ctx.w(e, "pe", pe_trF[j])
                if j >= 1:
                    ctx.w(e, "act", lnO.last_sq_act)
                e.tensor_copy(dt_scr[:, :], ptrF[:, :])
            ctx.sec("vector", _ev)

            d = emit_ln(ctx, lnO, dt_scr[:, :], lnF.scr[:, :],
                        [("dve", ctx.cnt["dve"]), ld_x1[j]],
                        res_add_ap=xt[par][:, :], dst2=x2t[par][:, :],
                        norm_waits=([st_x2[j - 2]] if j >= 2 else ()))
            dve_out[j] = d

            def _st(e, j=j, par=par, row0=row0):
                ctx.w(e, "dve", dve_out[j])
                st_x2[j] = ctx.dma(e, x_out[row0:row0 + 128, :], x2t[par][:, :],
                                   f"d{4 + par}")
            ctx.sec("scalar", _st)

        ctx.pending.extend([sv for sv in st_x2 if sv is not None][-2:])
        ctx.barrier(drains=st_x2[-2:])
        esP2.close()
        es_d.close()
        es.close()
    esS.close()


def emit_l1a(ctx, W, x_in, x_out, ident, ones1, uag_in, uag_out):
    nc = ctx.nc
    drains = getattr(ctx, "pending", [])
    ctx.pending = []
    NG = 16
    es = ExitStack()
    y_sb = sal(ctx, es, "s_y", [128, NG * 2048], BF16)
    xt = [sal(ctx, es, f"s_xt{i}", [128, DM], F32) for i in range(2)]
    lnt = [sal(ctx, es, f"s_lnt{i}", [128, DM], BF16) for i in range(2)]
    u_buf = sal(ctx, es, "s_u", [128, 16], F32)
    gcol = sal(ctx, es, "s_gcol", [128, 16], F32)
    cvec = sal(ctx, es, "s_cvec", [128, 6 * 16], F32)
    bp_sb = sal(ctx, es, "s_bp", [128, 16 * 16], F32)
    cpt_sb = sal(ctx, es, "s_cpt", [128, 16 * 16], F32)
    uago = sal(ctx, es, "s_uago", [128, 4 * 16], F32)
    ugb = sal(ctx, es, "s_ugb", [128, 16], BF16)
    p9 = sal(ctx, es, "s_p9", [128, 1], F32)
    t16 = sal(ctx, es, "s_t16", [128, 16], F32)
    boutr = sal(ctx, es, "s_bout", [1, DM], BF16)
    wrow = sal(ctx, es, "s_wrow", [1, DM], BF16)
    wb_sb = sal(ctx, es, "s_wb", [128, DM], BF16)
    lnS = LNState(ctx, es, "s")
    lnSo = LNState(ctx, es, "so")
    BIN, DP, DEC, SEL, UPV, UG = range(6)

    def _pl(e):
        ctx.dma(e, cvec[:, BIN * 16:BIN * 16 + 16], W["bincm"][:, :], "d7")
        ctx.dma(e, cvec[:, DP * 16:DP * 16 + 16], W["Dpcm"][:, :], "d7")
        ctx.dma(e, cvec[:, DEC * 16:DEC * 16 + 16], W["decaycm"][:, :], "d7")
        ctx.dma(e, cvec[:, SEL * 16:SEL * 16 + 4], W["selread"][:, :], "d7")
        src = V(W["Bp"], 0, [[16, 128], [128 * 16, 16], [1, 16]])
        ctx.dma(e, bp_sb.ap().rearrange("p (g j) -> p g j", g=16, j=16), src, "d7")
        src = V(W["CpT"], 0, [[16, 128], [128 * 16, 16], [1, 16]])
        ctx.dma(e, cpt_sb.ap().rearrange("p (g j) -> p g j", g=16, j=16), src, "d7")
        ctx.dma(e, boutr[:, :], W["boutrow"][:, :], "d7")
        e.wait_ge(ctx.sems["d7"], ctx.cnt["d7"])
    ctx.sec("gpsimd", _pl)
    pl_tot = ("d7", ctx.cnt["d7"])

    def _gc(e):
        ctx.wp(e, pl_tot)
        e.memset(p9[:, :], DECAY)
        for g in range(NG):
            e.tensor_mul(t16[:, :], bp_sb[:, 16 * g:16 * (g + 1)],
                         cpt_sb[:, 16 * g:16 * (g + 1)])
            e.reduce_sum(gcol[:, g:g + 1], t16[:, :], AX.X)
    ctx.sec("vector", _gc)
    gcol_done = ctx.cnt["dve"]

    esL = ExitStack()
    lnx2 = sal(ctx, esL, "s_lnx2", [128, 8 * 2048], BF16)
    esA = ExitStack()
    ptr = pal(ctx, esA, "s_ptr", [128, DM], BF16)
    ln_done = [0] * NT
    sq_act = [0] * NT
    pe_tr = [0] * NT
    dve_lnx = [0] * NT
    ld_x = [None] * NT
    for j in range(NT):
        par = j % 2

        def _ld(e, j=j, par=par):
            if j == 0:
                for sv in drains:
                    ctx.wp(e, sv)
            if j >= 2:
                ctx.w(e, "dve", ln_done[j - 2])
                ctx.w(e, "act", sq_act[j - 2])
            ld_x[j] = ctx.dma(e, xt[par][:, :],
                              x_in[128 * j:128 * (j + 1), :], f"d{par}")
        ctx.sec("sync", _ld)

        d = emit_ln(ctx, lnS, xt[par][:, :], lnt[par][:, :], [ld_x[j]],
                    norm_waits=([("pe", pe_tr[j - 2])] if j >= 2 else ()))
        ln_done[j] = d
        sq_act[j] = lnS.last_sq_act

        def _tr(e, j=j, par=par, d=d):
            ctx.w(e, "dve", d)
            if j >= 1:
                ctx.w(e, "dve", dve_lnx[j - 1])
            last = None
            for b in range(8):
                last = e.transpose(ptr[:, 128 * b:128 * (b + 1)],
                                   lnt[par][:, 128 * b:128 * (b + 1)],
                                   ident[:, :])
            ctx.inc(last, "pe")
        ctx.sec("tensor", _tr)
        pe_tr[j] = ctx.cnt["pe"]

        def _ev(e, j=j):
            ctx.w(e, "pe", pe_tr[j])
            dst = V(lnx2, 128 * j, [[8 * 2048, 128], [2048, 8], [1, 128]])
            e.tensor_copy(dst, ptr[:, :])
        ctx.sec("vector", _ev)
        dve_lnx[j] = ctx.cnt["dve"]
    lnx_done = dve_lnx[NT - 1]
    esA.close()

    esB = ExitStack()
    xi_g = sal(ctx, esB, "s_xi", [128, 2048], F32)
    a_g = sal(ctx, esB, "s_a", [128, 2048], F32)
    t1 = sal(ctx, esB, "s_t1", [128, 2048], F32)
    wbuf = [sal(ctx, esB, f"s_wbuf{i}", [128, 8 * 128], BF16) for i in range(2)]
    pxi = [pal(ctx, esB, f"s_pxi{i}", [128, 2048], F32) for i in range(2)]
    ld_w = [None] * NG
    pe_xi = [0] * NG
    act_xi = [0] * NG
    dve_y = [0] * NG
    for g in range(NG):
        par = g % 2

        def _wl(e, g=g, par=par):
            if g >= 2:
                ctx.w(e, "pe", pe_xi[g - 2])
            else:
                ctx.w(e, "pe", getattr(ctx, "wgate", 0))
            src = V(W["Win"], 128 * g, [[DIN, 128], [128 * DIN, 8], [1, 128]])
            dst = wbuf[par].ap().rearrange("p (k c) -> p k c", k=8, c=128)
            ld_w[g] = ctx.dma(e, dst, src, f"d{6 + par}")
        ctx.sec("gpsimd", _wl)

        def _mm(e, g=g, par=par):
            if g == 0:
                ctx.w(e, "dve", lnx_done)
            ctx.wp(e, ld_w[g])
            if g >= 2:
                ctx.w(e, "act", act_xi[g - 2])
            last = None
            for ki in range(8):
                lhsT = wbuf[par][:, ki * 128:(ki + 1) * 128]
                for h in range(4):
                    last = e.matmul(pxi[g % 2][:, 512 * h:512 * (h + 1)], lhsT,
                                    V(lnx2, ki * 2048 + 512 * h,
                                      [[8 * 2048, 128], [1, 512]]),
                                    start=(ki == 0), stop=(ki == 7),
                                    skip_group_check=True)
            ctx.inc(last, "pe")
        ctx.sec("tensor", _mm)
        pe_xi[g] = ctx.cnt["pe"]

        def _xe(e, g=g):
            ctx.w(e, "pe", pe_xi[g])
            if g == 0:
                ctx.wp(e, pl_tot)
            if g >= 1:
                ctx.w(e, "dve", dve_y[g - 1])
            e.activation(out=xi_g[:, :], in_=pxi[g % 2][:, :], func=AF.Silu,
                         bias=cvec[:, BIN * 16 + g:BIN * 16 + g + 1])
        ctx.sec("scalar", _xe)
        act_xi[g] = ctx.cnt["act"]

        def _sy(e, g=g):
            ctx.w(e, "act", act_xi[g])
            d0 = V(p9, 0, [[1, 128], [0, 2048]])
            e.tensor_tensor_scan(a_g[:, :], d0, xi_g[:, :], 0.0,
                                 ALU.mult, ALU.add)
            e.tensor_copy(u_buf[:, g:g + 1], a_g[:, 2047:2048])
            e.tensor_scalar(t1[:, :], xi_g[:, :],
                            cvec[:, DP * 16 + g:DP * 16 + g + 1], None, ALU.mult)
            e.scalar_tensor_tensor(y_sb[:, 2048 * g:2048 * (g + 1)], a_g[:, :],
                                   gcol[:, g:g + 1], t1[:, :], ALU.mult, ALU.add)
        ctx.sec("vector", _sy)
        dve_y[g] = ctx.cnt["dve"]
    ctx.wgate = pe_xi[NG - 1]
    esB.close()
    esL.close()

    def _ust(e):
        ctx.w(e, "dve", dve_y[NG - 1])
        sv = ctx.dma(e, uag_in[:, :], u_buf[:, :], "d2")
        ctx.wp(e, sv)
        ctx.inc(e.nop(), "act")
    ctx.sec("scalar", _ust)
    act_ust = ctx.cnt["act"]

    def _ag(e):
        ctx.w(e, "act", act_ust)
        ctx.inc(e.collective_compute(
            "AllGather", ALU.bypass,
            replica_groups=[[0, 1, 2, 3], [4, 5, 6, 7]],
            ins=[uag_in[:, :]], outs=[uag_out[:, :]]), "cc")
        e.wait_ge(ctx.sems["cc"], ctx.cnt["cc"])
        src = V(uag_out, 0, [[16, 128], [128 * 16, 4], [1, 16]])
        sv = ctx.dma(e, uago.ap().rearrange("p (r j) -> p r j", r=4, j=16),
                     src, "d7")
        ctx.wp(e, sv)
        ctx.inc(e.engine_nop(), "pool")
    ctx.sec("gpsimd", _ag)
    pool_ag = ctx.cnt["pool"]

    def _sel(e):
        ctx.w(e, "pool", pool_ag)
        selv = V(cvec, SEL * 16, [[6 * 16, 128], [1, 4], [0, 16]])
        e.tensor_mul(uago[:, :], uago[:, :], selv)
        rin = V(uago, 0, [[4 * 16, 128], [1, 16], [16, 4]])
        e.reduce_sum(cvec[:, UPV * 16:UPV * 16 + 16], rin, AX.X)
        e.tensor_mul(cvec[:, UG * 16:UG * 16 + 16],
                     cvec[:, UPV * 16:UPV * 16 + 16], gcol[:, :])
        e.tensor_copy(ugb[:, :], cvec[:, UG * 16:UG * 16 + 16])
    ctx.sec("vector", _sel)
    dve_ug = ctx.cnt["dve"]

    esD = ExitStack()
    wo_sb = sal(ctx, esD, "s_wo", [128, NG * 1024], BF16)
    o_tm = sal(ctx, esD, "s_otm", [128, DM], F32)
    lnn = sal(ctx, esD, "s_lnn", [128, DM], F32)
    x3t = [sal(ctx, esD, f"s_x3{i}", [128, DM], F32) for i in range(2)]
    pso = [pal(ctx, esD, f"s_pso{i}", [128, DM], F32) for i in range(2)]
    pw1 = pal(ctx, esD, "s_pw1", [1, DM], F32)

    def _wol(e):
        ctx.w(e, "pe", ctx.wgate)
        src = V(W["Wout"], 0, [[DM, 128], [128 * DM, NG], [1, DM]])
        sv = ctx.dma(e, wo_sb.ap().rearrange("p (g c) -> p g c", g=NG, c=DM),
                     src, "d6")
        ld = sv
        ctx.pending_wo = ld
    ctx.sec("gpsimd", _wol)
    ld_wo = ctx.pending_wo

    def _wr(e):
        ctx.wp(e, ld_wo)
        ctx.w(e, "dve", dve_ug)
        last = None
        for g in range(NG):
            for h in range(2):
                last = e.matmul(pw1[0:1, 512 * h:512 * (h + 1)],
                                ugb[:, g:g + 1],
                                wo_sb[:, 1024 * g + 512 * h:1024 * g + 512 * (h + 1)],
                                start=(g == 0), stop=(g == NG - 1),
                                skip_group_check=True)
        ctx.inc(last, "pe")
    ctx.sec("tensor", _wr)
    pe_wr = ctx.cnt["pe"]

    def _wre(e):
        ctx.w(e, "pe", pe_wr)
        e.tensor_copy(wrow[0:1, :], pw1[0:1, :])
    ctx.sec("vector", _wre)
    dve_wre = ctx.cnt["dve"]

    def _wrb(e):
        ctx.w(e, "dve", dve_wre)
        last = None
        for h in range(2):
            last = e.matmul(pso[1][:, 512 * h:512 * (h + 1)], ones1[0:1, :],
                            wrow[0:1, 512 * h:512 * (h + 1)],
                            start=True, stop=True, skip_group_check=True)
        ctx.inc(last, "pe")
    ctx.sec("tensor", _wrb)
    pe_wrb = ctx.cnt["pe"]

    def _wbe(e):
        ctx.w(e, "pe", pe_wrb)
        e.tensor_copy(wb_sb[:, :], pso[1][:, :])
    ctx.sec("vector", _wbe)
    dve_wb = ctx.cnt["dve"]

    pe_sl = [0] * NT
    dve_out = [0] * NT
    ld_x2 = [None] * NT
    st_x3 = [None] * NT
    for sl in range(NT):
        par = sl % 2

        def _ld(e, sl=sl, par=par):
            if sl >= 2:
                ctx.w(e, "dve", dve_out[sl - 2])
            else:
                ctx.w(e, "dve", ln_done[NT - 2 + sl])
                ctx.w(e, "act", sq_act[NT - 2 + sl])
            ld_x2[sl] = ctx.dma(e, xt[par][:, :],
                                x_in[128 * sl:128 * (sl + 1), :], f"d{par}")
        ctx.sec("sync", _ld)

        def _mm(e, sl=sl):
            if sl <= 1:
                ctx.w(e, "dve", dve_wb)
            if sl >= 2:
                ctx.w(e, "dve", dve_out[sl - 2])
            last = None
            for g in range(NG):
                lhsT = y_sb[:, 2048 * g + 128 * sl:2048 * g + 128 * (sl + 1)]
                for h in range(2):
                    last = e.matmul(pso[sl % 2][:, 512 * h:512 * (h + 1)], lhsT,
                                    wo_sb[:, 1024 * g + 512 * h:1024 * g + 512 * (h + 1)],
                                    start=(g == 0), stop=False,
                                    skip_group_check=True)
            for h in range(2):
                last = e.matmul(pso[sl % 2][:, 512 * h:512 * (h + 1)], ones1[0:1, :],
                                boutr[0:1, 512 * h:512 * (h + 1)],
                                start=False, stop=(h == 1),
                                skip_group_check=True)
            ctx.inc(last, "pe")
        ctx.sec("tensor", _mm)
        pe_sl[sl] = ctx.cnt["pe"]

        def _oe(e, sl=sl):
            ctx.w(e, "pe", pe_sl[sl])
            ctx.w(e, "act", lnSo.last_sq_act)
            e.scalar_tensor_tensor(o_tm[:, :], wb_sb[:, :],
                                   cvec[:, DEC * 16 + sl:DEC * 16 + sl + 1],
                                   pso[sl % 2][:, :], ALU.mult, ALU.add)
        ctx.sec("vector", _oe)
        dve_oe = ctx.cnt["dve"]

        d = emit_ln(ctx, lnSo, o_tm[:, :], lnn[:, :],
                    [("dve", dve_oe), ld_x2[sl]],
                    res_add_ap=xt[par][:, :], dst2=x3t[par][:, :],
                    norm_waits=([st_x3[sl - 2]] if sl >= 2 else ()))
        dve_out[sl] = d

        def _st(e, sl=sl, par=par):
            ctx.w(e, "dve", dve_out[sl])
            st_x3[sl] = ctx.dma(e, x_out[128 * sl:128 * (sl + 1), :],
                                x3t[par][:, :], f"d{4 + par}")
        ctx.sec("scalar", _st)

    ctx.pending = st_x3[-2:]
    ctx.wgate = pe_sl[NT - 1]
    ctx.barrier(drains=st_x3[-2:])
    esD.close()
    es.close()


def emit_l1f(ctx, W, x_in, y_out, ident, ones1, xp_in, xp_out, c_dr, yt_in,
             yt_out):
    nc = ctx.nc
    drains = getattr(ctx, "pending", [])
    ctx.pending = []
    es = ExitStack()
    xt = [sal(ctx, es, f"m_xt{i}", [128, DM], F32) for i in range(2)]
    lnt = sal(ctx, es, "m_lnt", [128, DM], BF16)
    rS = sal(ctx, es, "m_rS", [128, TC], F32)
    phib = sal(ctx, es, "m_phib", [1, NP], BF16)
    phibc = sal(ctx, es, "m_phibc", [128, 16], F32)
    sel32 = sal(ctx, es, "m_sel32", [128, 32], BF16)
    rep32 = sal(ctx, es, "m_rep32", [32, 128], BF16)
    oav = sal(ctx, es, "m_oav", [128, 128], BF16)
    lnM = LNState(ctx, es, "m")
    lnMo = LNState(ctx, es, "mo")

    def _pl(e):
        ctx.dma(e, phib[:, :], W["phibrow"][:, :], "d7")
        ctx.dma(e, phibc[:, :], W["phibcm"][:, :], "d7")
        ctx.dma(e, sel32[:, :], W["sel32"][:, :], "d7")
        ctx.dma(e, rep32[:, :], W["rep32"][:, :], "d7")
        ctx.dma(e, oav[:, :], W["oav"][:, :], "d7")
        e.wait_ge(ctx.sems["d7"], ctx.cnt["d7"])
    ctx.sec("gpsimd", _pl)
    pl_tot = ("d7", ctx.cnt["d7"])

    es1 = ExitStack()
    phit = sal(ctx, es1, "m_phit", [128, 8 * NP], BF16)
    lnx3fm = sal(ctx, es1, "m_lnx3fm", [128, 8 * TC], BF16)

    def _pwl(e):
        ctx.w(e, "pe", getattr(ctx, "wgate", 0))
        src = V(W["phiT"], 0, [[NP, 128], [128 * NP, 8], [1, NP]])
        ctx.phit_ld = ctx.dma(
            e, phit.ap().rearrange("p (k c) -> p k c", k=8, c=NP), src, "d6")
    ctx.sec("gpsimd", _pwl)
    phit_ld = ctx.phit_ld

    es2 = ExitStack()
    lnx3tm = sal(ctx, es2, "m_lnx3tm", [128, NT * DM], BF16)
    esA = ExitStack()
    ptr = pal(ctx, esA, "m_ptr", [128, DM], BF16)
    ln_done = [0] * NT
    sq_act = [0] * NT
    pe_tr = [0] * NT
    dve_fm = [0] * NT
    ld_x = [None] * NT
    for j in range(NT):
        par = j % 2

        def _ld(e, j=j, par=par):
            if j == 0:
                for sv in drains:
                    ctx.wp(e, sv)
            if j >= 2:
                ctx.w(e, "dve", ln_done[j - 2])
                ctx.w(e, "act", sq_act[j - 2])
            ld_x[j] = ctx.dma(e, xt[par][:, :],
                              x_in[128 * j:128 * (j + 1), :], f"d{par}")
        ctx.sec("sync", _ld)

        d = emit_ln(ctx, lnM, xt[par][:, :], lnx3tm[:, DM * j:DM * (j + 1)],
                    [ld_x[j]], norm_waits=([("pe", pe_tr[j - 2])] if j >= 2 else ()))
        ln_done[j] = d
        sq_act[j] = lnM.last_sq_act

        def _tr(e, j=j, d=d):
            ctx.w(e, "dve", d)
            if j >= 1:
                ctx.w(e, "dve", dve_fm[j - 1])
            last = None
            for b in range(8):
                last = e.transpose(ptr[:, 128 * b:128 * (b + 1)],
                                   lnx3tm[:, DM * j + 128 * b:DM * j + 128 * (b + 1)],
                                   ident[:, :])
            ctx.inc(last, "pe")
        ctx.sec("tensor", _tr)
        pe_tr[j] = ctx.cnt["pe"]

        def _ev(e, j=j):
            ctx.w(e, "pe", pe_tr[j])
            dst = V(lnx3fm, 128 * j, [[8 * TC, 128], [TC, 8], [1, 128]])
            e.tensor_copy(dst, ptr[:, :])
        ctx.sec("vector", _ev)
        dve_fm[j] = ctx.cnt["dve"]
    lnx_done = dve_fm[NT - 1]
    esA.close()

    es3 = ExitStack()
    D_sb = sal(ctx, es3, "m_D", [128, NT * NP], BF16)
    e_d = sal(ctx, es3, "m_ed", [128, NP], BF16)
    sd = sal(ctx, es3, "m_sd", [128, 2 * NEXP], F32)
    plog = pal(ctx, es3, "m_plog", [128, NP], F32)
    pe_lg = [0] * NT
    act_ex = [0] * NT
    dve_D = [0] * NT
    for I in range(NT):
        def _mm(e, I=I):
            if I == 0:
                ctx.w(e, "dve", lnx_done)
                ctx.wp(e, phit_ld)
                ctx.wp(e, pl_tot)
            if I >= 1:
                ctx.w(e, "act", act_ex[I - 1])
            last = None
            for ct in range(8):
                lhsT = V(lnx3fm, ct * TC + 128 * I, [[8 * TC, 128], [1, 128]])
                for h in range(4):
                    last = e.matmul(plog[:, 512 * h:512 * (h + 1)], lhsT,
                                    phit[:, ct * NP + 512 * h:ct * NP + 512 * (h + 1)],
                                    start=(ct == 0), stop=False,
                                    skip_group_check=True)
            for h in range(4):
                last = e.matmul(plog[:, 512 * h:512 * (h + 1)], ones1[0:1, :],
                                phib[0:1, 512 * h:512 * (h + 1)],
                                start=False, stop=(h == 3), skip_group_check=True)
            ctx.inc(last, "pe")
        ctx.sec("tensor", _mm)
        pe_lg[I] = ctx.cnt["pe"]

        def _ex(e, I=I):
            ctx.w(e, "pe", pe_lg[I])
            if I >= 1:
                ctx.w(e, "dve", dve_D[I - 1])
            e.activation(out=e_d[:, :], in_=plog[:, :], func=AF.Exp)
        ctx.sec("scalar", _ex)
        act_ex[I] = ctx.cnt["act"]

        def _D(e, I=I):
            ctx.w(e, "act", act_ex[I])
            rin = V(e_d, 0, [[NP, 128], [PSLOT, NEXP], [1, PSLOT]])
            e.reduce_sum(sd[:, 0:NEXP], rin, AX.X)
            e.reciprocal(sd[:, NEXP:2 * NEXP], sd[:, 0:NEXP])
            rv = V(sd, NEXP, [[2 * NEXP, 128], [1, NEXP], [0, PSLOT]])
            e.tensor_mul(D_sb[:, NP * I:NP * (I + 1)], e_d[:, :], rv)
        ctx.sec("vector", _D)
        dve_D[I] = ctx.cnt["dve"]

    es4 = ExitStack()
    xps = sal(ctx, es4, "m_xps", [128, DM], F32)
    pxp = [pal(ctx, es4, f"m_pxp{i}", [128, DM], F32) for i in range(2)]
    pe_xp = [0] * 16
    dve_xp = [0] * 16
    st_xp = [None] * 16
    for sn in range(16):
        def _mm(e, sn=sn):
            if sn == 0:
                ctx.w(e, "dve", dve_D[NT - 1])
            if sn >= 2:
                ctx.w(e, "dve", dve_xp[sn - 2])
            last = None
            for I in range(NT):
                lhsT = D_sb[:, NP * I + 128 * sn:NP * I + 128 * (sn + 1)]
                for h in range(2):
                    last = e.matmul(pxp[sn % 2][:, 512 * h:512 * (h + 1)], lhsT,
                                    lnx3tm[:, DM * I + 512 * h:DM * I + 512 * (h + 1)],
                                    start=(I == 0), stop=(I == NT - 1),
                                    skip_group_check=True)
            ctx.inc(last, "pe")
        ctx.sec("tensor", _mm)
        pe_xp[sn] = ctx.cnt["pe"]

        def _ev(e, sn=sn):
            ctx.w(e, "pe", pe_xp[sn])
            if sn >= 1:
                ctx.wp(e, st_xp[sn - 1])
            e.tensor_copy(xps[:, :], pxp[sn % 2][:, :])
        ctx.sec("vector", _ev)
        dve_xp[sn] = ctx.cnt["dve"]

        def _st(e, sn=sn):
            ctx.w(e, "dve", dve_xp[sn])
            st_xp[sn] = ctx.dma(e, xp_in[128 * sn:128 * (sn + 1), :],
                                xps[:, :], f"d{2 + sn % 2}")
        ctx.sec("scalar", _st)
    ctx.wgate = pe_xp[15]

    def _rs(e):
        for sv in st_xp[-2:]:
            ctx.wp(e, sv)
        ctx.inc(e.collective_compute(
            "ReduceScatter", ALU.add, replica_groups=[[0, 1, 2, 3, 4, 5, 6, 7]],
            ins=[xp_in[:, :]], outs=[xp_out[:, :]]), "cc")
    ctx.sec("gpsimd", _rs)
    cc_rs = ctx.cnt["cc"]
    es4.close()
    es3.close()
    es2.close()

    es5 = ExitStack()
    e_fm = sal(ctx, es5, "m_efm", [128, TC], F32)
    e_bf = [sal(ctx, es5, f"m_ebf{i}", [128, TC], BF16) for i in range(2)]
    sC = sal(ctx, es5, "m_sC", [32, TC], BF16)
    pfm = pal(ctx, es5, "m_pfm", [128, TC], F32)
    psC = pal(ctx, es5, "m_psC", [32, TC], F32)
    pe_fm = [0] * 16
    act_fx = [0] * 16
    dve_eb = [0] * 16
    pe_sC = [0] * 16
    st_eb = [None] * 16
    for sn in range(16):
        par = sn % 2

        def _mm(e, sn=sn):
            if sn >= 1:
                ctx.w(e, "act", act_fx[sn - 1])
            last = None
            for ct in range(8):
                lhsT = V(phit, ct * NP + 128 * sn, [[8 * NP, 128], [1, 128]])
                for h in range(4):
                    last = e.matmul(pfm[:, 512 * h:512 * (h + 1)], lhsT,
                                    V(lnx3fm, ct * TC + 512 * h,
                                      [[8 * TC, 128], [1, 512]]),
                                    start=(ct == 0), stop=(ct == 7),
                                    skip_group_check=True)
            ctx.inc(last, "pe")
        ctx.sec("tensor", _mm)
        pe_fm[sn] = ctx.cnt["pe"]

        def _ex(e, sn=sn):
            ctx.w(e, "pe", pe_fm[sn])
            if sn >= 1:
                ctx.w(e, "dve", dve_eb[sn - 1])
            e.activation(out=e_fm[:, :], in_=pfm[:, :], func=AF.Exp,
                         bias=phibc[:, sn:sn + 1])
        ctx.sec("scalar", _ex)
        act_fx[sn] = ctx.cnt["act"]

        def _eb(e, sn=sn, par=par):
            ctx.w(e, "act", act_fx[sn])
            if sn >= 2:
                ctx.wp(e, st_eb[sn - 2])
                ctx.w(e, "pe", pe_sC[sn - 2])
            e.tensor_copy(e_bf[par][:, :], e_fm[:, :])
        ctx.sec("vector", _eb)
        dve_eb[sn] = ctx.cnt["dve"]

        def _st(e, sn=sn, par=par):
            ctx.w(e, "dve", dve_eb[sn])
            if sn < 2:
                ctx.wp(e, st_xp[14 + sn])
            st_eb[sn] = ctx.dma(e, c_dr[128 * sn:128 * (sn + 1), :],
                                e_bf[par][:, :], f"d{2 + par}")
        ctx.sec("scalar", _st)

        def _sc(e, sn=sn, par=par):
            ctx.w(e, "dve", dve_eb[sn])
            last = None
            for h in range(4):
                last = e.matmul(psC[0:32, 512 * h:512 * (h + 1)], sel32[:, :],
                                e_bf[par][:, 512 * h:512 * (h + 1)],
                                start=(sn == 0), stop=(sn == 15),
                                skip_group_check=True)
            ctx.inc(last, "pe")
        ctx.sec("tensor", _sc)
        pe_sC[sn] = ctx.cnt["pe"]

    def _sce(e):
        ctx.w(e, "pe", pe_sC[15])
        e.tensor_copy(sC[0:32, :], psC[0:32, :])
    ctx.sec("vector", _sce)
    dve_sC = ctx.cnt["dve"]

    def _scb(e):
        ctx.w(e, "dve", dve_sC)
        last = None
        for h in range(4):
            last = e.matmul(pfm[:, 512 * h:512 * (h + 1)], rep32[0:32, :],
                            sC[0:32, 512 * h:512 * (h + 1)],
                            start=True, stop=True, skip_group_check=True)
        ctx.inc(last, "pe")
    ctx.sec("tensor", _scb)
    pe_scb = ctx.cnt["pe"]

    def _rse(e):
        ctx.w(e, "pe", pe_scb)
        e.reciprocal(rS[:, :], pfm[:, :])
    ctx.sec("vector", _rse)
    dve_rS = ctx.cnt["dve"]
    ctx.barrier()
    es5.close()
    es1.close()

    es6 = ExitStack()
    xrs = sal(ctx, es6, "m_xrs", [128, DM], F32)
    xcl = sal(ctx, es6, "m_xcl", [128, DM], F32)
    xln = sal(ctx, es6, "m_xln", [128, DM], F32)
    xtfm = sal(ctx, es6, "m_xtfm", [128, 8 * 256], BF16)
    xth = sal(ctx, es6, "m_xth", [128, DM], BF16)
    ptr2 = pal(ctx, es6, "m_ptr2", [128, DM], BF16)
    pe_trx = [0] * 2
    dcl_h = [0] * 2
    for half in range(2):
        def _ld(e, half=half):
            if half == 0:
                e.wait_ge(ctx.sems["cc"], cc_rs)
            else:
                ctx.w(e, "dve", dcl_h[0])
            ld = ctx.dma(e, xrs[:, :], xp_out[128 * half:128 * (half + 1), :],
                         f"d{half}")
            ctx.ld_xrs = ld
        ctx.sec("sync", _ld)
        ld_xrs = ctx.ld_xrs

        def _cl(e, half=half, ld=ld_xrs):
            ctx.wp(e, ld)
            if half == 1:
                ctx.w(e, "pe", pe_trx[0])
                ctx.w(e, "act", lnM.last_sq_act)
            e.tensor_scalar(xcl[:, :], xrs[:, :], -33000.0, 65000.0,
                            ALU.max, ALU.min)
        ctx.sec("vector", _cl)
        d_cl = ctx.cnt["dve"]
        dcl_h[half] = d_cl

        d = emit_ln(ctx, lnM, xcl[:, :], xln[:, :], [("dve", d_cl)])

        def _th(e, half=half, d=d):
            ctx.w(e, "dve", d)
            e.activation(out=xth[:, :], in_=xln[:, :], func=AF.Tanh)
        ctx.sec("scalar", _th)
        a_th = ctx.cnt["act"]

        def _tr(e, half=half, a=a_th):
            ctx.w(e, "act", a)
            last = None
            for b in range(8):
                last = e.transpose(ptr2[:, 128 * b:128 * (b + 1)],
                                   xth[:, 128 * b:128 * (b + 1)], ident[:, :])
            ctx.inc(last, "pe")
        ctx.sec("tensor", _tr)
        pe_trx[half] = ctx.cnt["pe"]

        def _ev(e, half=half):
            ctx.w(e, "pe", pe_trx[half])
            dst = V(xtfm, 128 * half, [[8 * 256, 128], [256, 8], [1, 128]])
            e.tensor_copy(dst, ptr2[:, :])
        ctx.sec("vector", _ev)
    dve_xt = ctx.cnt["dve"]

    ebias = sal(ctx, es6, "m_ebias", [1, 8 * DM], BF16)
    wex = [sal(ctx, es6, f"m_wex{i}", [128, 8 * DM], BF16) for i in range(2)]
    yrow = sal(ctx, es6, "m_yrow", [32, DM], BF16)
    pY = pal(ctx, es6, "m_pY", [32, DM], F32)

    def _ebl(e):
        ctx.dma(e, ebias[:, :], W["eb_shard"][:, :], "d7")
        e.wait_ge(ctx.sems["d7"], ctx.cnt["d7"])
    ctx.sec("gpsimd", _ebl)
    eb_tot = ("d7", ctx.cnt["d7"])

    ld_we = [None] * 8
    pe_Y = [0] * 8
    dve_Y = [0] * 8
    st_Y = [None] * 8
    for eL in range(8):
        par = eL % 2

        def _wl(e, eL=eL, par=par):
            if eL >= 2:
                ctx.w(e, "pe", pe_Y[eL - 2])
            else:
                ctx.w(e, "pe", ctx.wgate)
            src = V(W["ew_shard"], eL * DM * DM,
                    [[DM, 128], [128 * DM, 8], [1, DM]])
            ld_we[eL] = ctx.dma(
                e, wex[par].ap().rearrange("p (k c) -> p k c", k=8, c=DM),
                src, f"d{6 + par}")
        ctx.sec("gpsimd", _wl)

        def _mm(e, eL=eL, par=par):
            if eL == 0:
                ctx.w(e, "dve", dve_xt)
                ctx.wp(e, eb_tot)
            ctx.wp(e, ld_we[eL])
            if eL >= 1:
                ctx.w(e, "dve", dve_Y[eL - 1])
            last = None
            for ds in range(8):
                lhsT = V(xtfm, ds * 256 + 32 * eL, [[8 * 256, 128], [1, 32]])
                for h in range(2):
                    last = e.matmul(pY[0:32, 512 * h:512 * (h + 1)], lhsT,
                                    wex[par][:, ds * DM + 512 * h:ds * DM + 512 * (h + 1)],
                                    start=(ds == 0), stop=False,
                                    skip_group_check=True)
            for h in range(2):
                last = e.matmul(pY[0:32, 512 * h:512 * (h + 1)], ones1[0:1, 0:32],
                                ebias[0:1, eL * DM + 512 * h:eL * DM + 512 * (h + 1)],
                                start=False, stop=(h == 1), skip_group_check=True)
            ctx.inc(last, "pe")
        ctx.sec("tensor", _mm)
        pe_Y[eL] = ctx.cnt["pe"]

        def _ev(e, eL=eL):
            ctx.w(e, "pe", pe_Y[eL])
            if eL >= 1:
                ctx.wp(e, st_Y[eL - 1])
            e.tensor_copy(yrow[0:32, :], pY[0:32, :])
        ctx.sec("vector", _ev)
        dve_Y[eL] = ctx.cnt["dve"]

        def _st(e, eL=eL):
            ctx.w(e, "dve", dve_Y[eL])
            if eL < 2:
                ctx.wp(e, st_eb[14 + eL])
            st_Y[eL] = ctx.dma(e, yt_in[32 * eL:32 * (eL + 1), :],
                               yrow[0:32, :], f"d{2 + eL % 2}")
        ctx.sec("scalar", _st)

    def _ag(e):
        for sv in st_Y[-2:]:
            ctx.wp(e, sv)
        ctx.inc(e.collective_compute(
            "AllGather", ALU.bypass, replica_groups=[[0, 1, 2, 3, 4, 5, 6, 7]],
            ins=[yt_in[:, :]], outs=[yt_out[:, :]]), "cc")
    ctx.sec("gpsimd", _ag)
    cc_ag = ctx.cnt["cc"]
    es6.close()

    es7 = ExitStack()
    C_all = sal(ctx, es7, "m_C", [128, 16 * TC], BF16)
    ytk = sal(ctx, es7, "m_ytk", [128, NT * DM], BF16)
    ceb = [sal(ctx, es7, f"m_ceb{i}", [128, TC], BF16) for i in range(4)]
    o_tm = sal(ctx, es7, "m_otm", [128, DM], F32)
    lnn = sal(ctx, es7, "m_lnn", [128, DM], F32)
    x4t = [sal(ctx, es7, f"m_x4{i}", [128, DM], F32) for i in range(2)]
    pY2 = [pal(ctx, es7, f"m_pY2{i}", [128, DM], F32) for i in range(2)]

    def _ytl(e):
        e.wait_ge(ctx.sems["cc"], cc_ag)
        src = V(yt_out, 0, [[DM, 128], [128 * DM, NT], [1, DM]])
        ctx.ld_ytk = ctx.dma(
            e, ytk.ap().rearrange("p (g c) -> p g c", g=NT, c=DM), src, "d6")
    ctx.sec("gpsimd", _ytl)
    ld_ytk = ctx.ld_ytk

    ld_eb2 = [None] * 16
    dve_C = [0] * 16
    for sn in range(16):
        par = sn % 4

        def _ld(e, sn=sn, par=par):
            if sn >= 4:
                ctx.w(e, "dve", dve_C[sn - 4])
            elif sn < 2:
                ctx.w(e, "dve", dcl_h[1])
                ctx.wp(e, st_eb[14 + sn])
            ld_eb2[sn] = ctx.dma(e, ceb[par][:, :],
                                 c_dr[128 * sn:128 * (sn + 1), :], f"d{sn % 2}")
        ctx.sec("sync", _ld)

        def _C(e, sn=sn, par=par):
            ctx.wp(e, ld_eb2[sn])
            if sn == 0:
                ctx.w(e, "dve", dve_rS)
            e.tensor_mul(C_all[:, TC * sn:TC * (sn + 1)], ceb[par][:, :],
                         rS[:, :])
        ctx.sec("vector", _C)
        dve_C[sn] = ctx.cnt["dve"]

    pe_Y2 = [0] * NT
    dve_o = [0] * NT
    ld_x3 = [None] * NT
    st_x4 = [None] * NT
    for sl in range(NT):
        par = sl % 2

        def _ld(e, sl=sl, par=par):
            if sl >= 2:
                ctx.w(e, "dve", dve_o[sl - 2])
            else:
                ctx.w(e, "dve", ln_done[NT - 2 + sl])
                ctx.w(e, "act", sq_act[NT - 2 + sl])
                e.wait_ge(ctx.sems["cc"], cc_ag)
            ld_x3[sl] = ctx.dma(e, xt[par][:, :],
                                x_in[128 * sl:128 * (sl + 1), :], f"d{2 + par}")
        ctx.sec("sync", _ld)

        def _mm(e, sl=sl):
            if sl == 0:
                ctx.w(e, "dve", dve_C[15])
                ctx.wp(e, ld_ytk)
            if sl >= 2:
                ctx.w(e, "dve", dve_o[sl - 2])
            last = None
            for sn in range(16):
                lhsT = C_all[:, TC * sn + 128 * sl:TC * sn + 128 * (sl + 1)]
                for h in range(2):
                    last = e.matmul(pY2[sl % 2][:, 512 * h:512 * (h + 1)], lhsT,
                                    ytk[:, DM * sn + 512 * h:DM * sn + 512 * (h + 1)],
                                    start=(sn == 0), stop=(sn == 15),
                                    skip_group_check=True)
            ctx.inc(last, "pe")
        ctx.sec("tensor", _mm)
        pe_Y2[sl] = ctx.cnt["pe"]

        def _oe(e, sl=sl):
            ctx.w(e, "pe", pe_Y2[sl])
            ctx.w(e, "act", lnMo.last_sq_act)
            e.tensor_copy(o_tm[:, :], pY2[sl % 2][:, :])
        ctx.sec("vector", _oe)
        dve_oe = ctx.cnt["dve"]

        d = emit_ln(ctx, lnMo, o_tm[:, :], lnn[:, :],
                    [("dve", dve_oe), ld_x3[sl]],
                    res_add_ap=xt[par][:, :], dst2=x4t[par][:, :],
                    norm_waits=([st_x4[sl - 2]] if sl >= 2 else ()))
        dve_o[sl] = d

        def _st(e, sl=sl, par=par):
            ctx.w(e, "dve", dve_o[sl])
            st_x4[sl] = ctx.dma(e, y_out[128 * sl:128 * (sl + 1), :],
                                x4t[par][:, :], f"d{4 + par}")
        ctx.sec("scalar", _st)

    ctx.barrier(drains=st_x4[-2:])
    es7.close()
    es.close()


from concourse.bass_utils import run_bass_kernel_spmd

_CACHE = {}


def kernel(**inputs):
    sh, pc = prep_inputs(inputs)
    if "nc" not in _CACHE:
        _CACHE["nc"] = build(("l0a", "l0f", "l1a", "l1f"))
    nc = _CACHE["nc"]
    declared = set(nc._mk_W.keys()) | {"x_ext"}
    in_maps = [{k: v for k, v in dict(sh, **pc[c]).items() if k in declared}
               for c in range(NCORES)]
    res = run_bass_kernel_spmd(nc, in_maps, core_ids=list(range(NCORES)))
    out = np.concatenate([np.asarray(res.results[c]["y_out"])
                          for c in range(NCORES)], axis=0)
    return out.reshape(2, 8192, DM).astype(np.float32)


# version 17
# speedup vs baseline: 1.0141x; 1.0018x over previous
from contextlib import ExitStack
import numpy as np
import ml_dtypes
import concourse.bass as bass
import concourse.mybir as mybir

F32 = mybir.dt.float32
BF16 = mybir.dt.bfloat16
AF = mybir.ActivationFunctionType
ALU = mybir.AluOpType
AX = mybir.AxisListType
BF = ml_dtypes.bfloat16

NCORES = 8
TC = 2048
HALO = 4096
TE = TC + HALO
NT = TC // 128
NTH = TE // 128
DM = 1024
NH, HD = 16, 64
BANDS = [0, 1, 2, 3, 4, 8, 16, 32, 64, 128, 256, 512, 1024, 2048, 4096]
NBANDS = 15
BGROUPS = [(0, [0, 1, 2, 3, 4]), (5, [8, 16]), (7, [32, 64]),
           (9, [128, 256]), (11, [512, 1024]), (13, [2048, 4096])]
PV_CUM = []
_c = 0
for _gi, (_pos, _ds) in enumerate(BGROUPS):
    _c += (1 + 2 * (len(_ds) - 1)) if _gi == 0 else 2 * len(_ds)
    PV_CUM.append(_c)
PV_OPS_I = PV_CUM[-1] + 1

DECAY = 0.9
LN_EPS = 1e-5
NEXP, PSLOT = 64, 32
NP = NEXP * PSLOT
DIN = 2 * DM
DH = 4096


def prep_inputs(inputs):
    f = lambda a: np.ascontiguousarray(np.asarray(a), dtype=np.float32)
    sh = {}
    x = f(inputs["L0_attn_x"]).reshape(2 * 8192, DM)
    bias_cat = []
    for p in ("q", "k", "v", "o"):
        Wv = f(inputs[f"L0_attn_{p}_Wv"])
        Wr = f(inputs[f"L0_attn_{p}_Wr"])
        Wl = f(inputs[f"L0_attn_{p}_Wl"])
        sh[f"u_{p}"] = np.concatenate([Wv, Wr, Wl], 0).reshape(22 * 128, 128)
        bias_cat.append(f(inputs[f"L0_attn_{p}_b"]))
    sh["ubias"] = np.concatenate(bias_cat).reshape(1, 4 * DM)
    sh["w1T"] = f(inputs["L0_ffn_w1_w"].T)
    sh["w2T"] = f(inputs["L0_ffn_w2_w"].T)
    sh["h1T"] = f(inputs["L0_ffn_h1_w"].T)
    sh["h2T"] = f(inputs["L0_ffn_h2_w"].T)
    sh["w3T"] = f(inputs["L0_ffn_w3_w"].T)
    cm = lambda s: f(np.asarray(s).reshape(-1, 128).T)
    sh["s1cm"] = cm(inputs["L0_ffn_w1_s"])
    sh["s2cm"] = cm(inputs["L0_ffn_w2_s"])
    sh["h1scm"] = cm(inputs["L0_ffn_h1_s"])
    sh["h2scm"] = cm(inputs["L0_ffn_h2_s"])
    sh["s3row"] = f(inputs["L0_ffn_w3_s"]).reshape(1, DM)
    sh["s3cm"] = cm(inputs["L0_ffn_w3_s"])
    sh["Win"] = f(inputs["L1_attn_Win"])
    sh["bincm"] = cm(inputs["L1_attn_bin_"])
    sh["Bp"] = f(inputs["L1_attn_Bp"])
    sh["CpT"] = f(inputs["L1_attn_Cp"].T)
    sh["Dpcm"] = cm(inputs["L1_attn_Dp"])
    sh["Wout"] = f(inputs["L1_attn_Wout"])
    sh["boutrow"] = f(inputs["L1_attn_bout"]).reshape(1, DM)
    sh["phiT"] = f(inputs["L1_ffn_phi_w"].T)
    sh["phibcm"] = cm(inputs["L1_ffn_phi_b"])
    sh["phibrow"] = f(inputs["L1_ffn_phi_b"]).reshape(1, NP)
    ew = f(inputs["L1_ffn_expert_w"])
    eb = f(inputs["L1_ffn_expert_b"])
    sh["ident"] = np.eye(128, dtype=BF)
    sh["ones1x128"] = np.ones((1, 128), dtype=BF)
    sh["oav"] = np.full((128, 128), 1.0 / DM, dtype=BF)
    sh["sel32"] = (np.arange(128)[:, None] % 32 == np.arange(32)[None, :]).astype(BF)
    sh["rep32"] = (np.arange(32)[:, None] == np.arange(128)[None, :] % 32).astype(BF)
    t_loc = np.arange(TC)
    sh["decaycm"] = (DECAY ** (t_loc + 1.0)).astype(np.float32).reshape(NT, 128).T

    per_core = []
    for c in range(NCORES):
        d = {}
        r0 = c * TC
        xe = np.zeros((TE, DM), dtype=np.float32)
        lo = r0 - HALO
        b0 = (c // 4) * 8192
        lo_c = max(lo, b0)
        xe[lo_c - lo:] = x[lo_c:r0 + TC]
        d["x_ext"] = xe
        tb = (c % 4) * TC + np.arange(TC)
        m = np.where(tb[:, None] - np.asarray(BANDS)[None, :] >= 0, 0.0, -1e30)
        d["masks"] = np.ascontiguousarray(m.astype(np.float32).reshape(NT, 128, NBANDS))
        selr = np.zeros((128, 4), dtype=np.float32)
        if c % 4 > 0:
            selr[:, (c % 4) - 1] = 1.0
        d["selread"] = selr
        d["ew_shard"] = np.ascontiguousarray(ew[8 * c:8 * c + 8])
        d["eb_shard"] = np.ascontiguousarray(eb[8 * c:8 * c + 8].reshape(1, 8 * DM))
        per_core.append(d)
    return sh, per_core


def V(handle_or_ap, off, dims):
    a = handle_or_ap if isinstance(handle_or_ap, bass.AP) else handle_or_ap.ap()
    return bass.AP(a.tensor, off, [list(d) for d in dims])


CHAINED = {
    "tensor_copy", "tensor_tensor", "tensor_add", "tensor_sub", "tensor_mul",
    "tensor_max", "tensor_scalar", "tensor_single_scalar", "tensor_scalar_add",
    "tensor_scalar_sub", "tensor_scalar_mul", "tensor_scalar_max",
    "tensor_scalar_min", "tensor_relu", "scalar_tensor_tensor", "tensor_reduce",
    "reduce_sum", "reduce_max", "reciprocal", "memset", "memzero",
    "tensor_tensor_scan", "select", "transpose", "bn_stats", "bn_aggr",
    "activation", "copy", "mul", "add", "square", "sqrt", "sign", "iota",
    "affine_select",
}
ENG_SEM = {"vector": "dve", "scalar": "act", "gpsimd": "pool"}


class ChainEng:
    def __init__(self, ctx, eng, sem):
        self._ctx, self._eng, self._sem = ctx, eng, sem

    def __getattr__(self, name):
        f = getattr(self._eng, name)
        if name not in CHAINED:
            return f
        ctx, eng, sem = self._ctx, self._eng, self._sem

        def wrap(*a, **k):
            if ctx.cnt[sem] > 0:
                eng.wait_ge(ctx.sems[sem], ctx.cnt[sem])
            inst = f(*a, **k)
            ctx.inc(inst, sem)
            return inst
        return wrap


class Ctx:
    def __init__(self, nc, block, sems):
        self.nc, self.block, self.sems = nc, block, sems
        self.cnt = {k: 0 for k in sems}
        self.nbar = 0

    def inc(self, inst, s, n=1):
        self.cnt[s] += n
        inst.then_inc(self.sems[s], n)
        return self.cnt[s]

    def w(self, eng, s, val):
        if val > 0:
            eng.wait_ge(self.sems[s], val)

    def wp(self, eng, sv):
        if sv is not None:
            s, v = sv
            if v > 0:
                eng.wait_ge(self.sems[s], v)

    def dma(self, eng, out, in_, sem):
        inst = eng.dma_start(out=out, in_=in_)
        return (sem, self.inc(inst, sem, 16))

    def sec(self, name, fn):
        if name in ENG_SEM:
            sem = ENG_SEM[name]
            getattr(self.block, name)(lambda e: fn(ChainEng(self, e, sem)))
        else:
            getattr(self.block, name)(fn)

    def barrier(self, drains=()):
        self.nbar += 1
        target = 5 * self.nbar
        for name, own in (("sync", None), ("tensor", "pe"), ("vector", "dve"),
                          ("scalar", "act"), ("gpsimd", "pool")):
            def _f(eng, t=target, own=own, first=(name == "sync")):
                if first:
                    for sv in drains:
                        self.wp(eng, sv)
                if own is not None and self.cnt[own] > 0:
                    eng.wait_ge(self.sems[own], self.cnt[own])
                eng.nop().then_inc(self.sems["bar"], 1)
                eng.wait_ge(self.sems["bar"], t)
            getattr(self.block, name)(_f)
        self.cnt["bar"] = target


def sal(ctx, es, name, shape, dt):
    free_bytes = int(np.prod(shape[1:])) * (2 if dt == BF16 else 4)
    ab = 2048 if free_bytes >= 8192 else 32
    return es.enter_context(ctx.nc.sbuf_tensor(name, shape, dt, align_bytes=ab))


def pal(ctx, es, name, shape, dt):
    return es.enter_context(ctx.nc.psum_tensor(name, shape, dt))


class LNState:
    def __init__(self, ctx, es, tag):
        self.scr = sal(ctx, es, f"lnscr_{tag}", [128, 1024], F32)
        self.stat = sal(ctx, es, f"lnstat_{tag}", [128, 8], F32)
        self.last_var_dve = 0
        self.last_sq_act = 0


def emit_ln(ctx, ln, src_ap, dst_ap, ready, res_add_ap=None, dst2=None,
            norm_waits=()):
    st = ln.stat
    inv = 1.0 / DM

    def _act1(e, ready=tuple(ready), war=ln.last_var_dve):
        for s_, v_ in ready:
            ctx.w(e, s_, v_)
        ctx.w(e, "dve", war)
        e.activation(out=ln.scr[:, :], in_=src_ap, func=AF.Square,
                     accum_out=st[:, 1:2])
    ctx.sec("scalar", _act1)
    ln.last_sq_act = a_sq = ctx.cnt["act"]

    def _vec1(e, ready=tuple(ready), a=a_sq):
        for s_, v_ in ready:
            ctx.w(e, s_, v_)
        e.reduce_sum(st[:, 0:1], src_ap, AX.X)
        e.tensor_scalar(st[:, 2:3], st[:, 0:1], inv, None, ALU.mult)
        ctx.w(e, "act", a)
        e.tensor_mul(st[:, 3:4], st[:, 2:3], st[:, 2:3])
        e.scalar_tensor_tensor(st[:, 4:5], st[:, 1:2], inv, st[:, 3:4],
                               ALU.mult, ALU.subtract)
    ctx.sec("vector", _vec1)
    d_var = ctx.cnt["dve"]
    ln.last_var_dve = d_var

    def _act2(e, d=d_var):
        ctx.w(e, "dve", d)
        e.activation(out=st[:, 6:7], in_=st[:, 4:5], func=AF.Sqrt,
                     bias=ctx.eps[:, 0:1])
    ctx.sec("scalar", _act2)
    a_rstd = ctx.cnt["act"]

    def _vec2(e, a=a_rstd, nw=tuple(norm_waits)):
        ctx.w(e, "act", a)
        for s_, v_ in nw:
            ctx.w(e, s_, v_)
        e.reciprocal(st[:, 5:6], st[:, 6:7])
        e.tensor_scalar(dst_ap, src_ap, st[:, 2:3], st[:, 5:6],
                        ALU.subtract, ALU.mult)
        if res_add_ap is not None:
            e.tensor_add(dst2, dst_ap, res_add_ap)
    ctx.sec("vector", _vec2)
    return ctx.cnt["dve"]


def build(phases=("l0a", "l0f", "l1a", "l1f")):
    nc = bass.Bass("TRN2", target_bir_lowering=False)

    x_ext = nc.dram_tensor("x_ext", [TE, DM], F32, kind="ExternalInput").ap()
    y_out = nc.dram_tensor("y_out", [TC, DM], F32, kind="ExternalOutput").ap()
    xdr = [nc.dram_tensor(f"xdr{i}", [TC, DM], F32).ap() for i in range(2)]
    kv_dr = nc.dram_tensor("kv_dr", [TE, 2 * DM], BF16).ap()
    uag_in = nc.dram_tensor("uag_in", [128, 16], F32).ap()
    uag_out = nc.dram_tensor("uag_out", [512, 16], F32).ap()
    xp_in = nc.dram_tensor("xp_in", [NP, DM], F32).ap()
    xp_out = nc.dram_tensor("xp_out", [NP // NCORES, DM], F32).ap()
    c_dr = nc.dram_tensor("c_dr", [NP, TC], BF16).ap()
    yt_in = nc.dram_tensor("yt_in", [8 * PSLOT, DM], BF16).ap()
    yt_out = nc.dram_tensor("yt_out", [NP, DM], BF16, addr_space="Shared" if "l1f" in phases else "Local").ap()

    WSPECS = {"ubias": ([1, 4 * DM], F32), "w1T": ([DM, DH], F32),
              "w2T": ([DM, DH], F32), "h1T": ([DH, DH], F32),
              "h2T": ([DH, DH], F32), "w3T": ([DH, DM], F32),
              "s1cm": ([128, 32], F32), "s2cm": ([128, 32], F32),
              "h1scm": ([128, 32], F32), "h2scm": ([128, 32], F32),
              "s3row": ([1, DM], F32), "s3cm": ([128, 8], F32),
              "Win": ([DM, DIN], F32), "bincm": ([128, 16], F32),
              "Bp": ([DIN, 16], F32), "CpT": ([DIN, 16], F32),
              "Dpcm": ([128, 16], F32), "Wout": ([DIN, DM], F32),
              "boutrow": ([1, DM], F32), "phiT": ([DM, NP], F32),
              "phibcm": ([128, 16], F32), "phibrow": ([1, NP], F32),
              "decaycm": ([128, NT], F32), "ew_shard": ([8, DM, DM], F32),
              "eb_shard": ([1, 8 * DM], F32), "masks": ([NT, 128, NBANDS], F32),
              "selread": ([128, 4], F32), "ident": ([128, 128], BF16),
              "ones1x128": ([1, 128], BF16), "oav": ([128, 128], BF16),
              "sel32": ([128, 32], BF16), "rep32": ([32, 128], BF16)}
    for p in ("q", "k", "v", "o"):
        WSPECS[f"u_{p}"] = ([22 * 128, 128], F32)

    class LazyW(dict):
        def __missing__(self, name):
            shp, dt = WSPECS[name]
            ap = nc.dram_tensor(name, list(shp), dt, kind="ExternalInput").ap()
            self[name] = ap
            return ap
    W = LazyW()
    nc._mk_W = W

    sems, semcm = {}, []
    for s in ("d0", "d1", "d2", "d3", "d4", "d5", "d6", "d7",
              "pe", "dve", "act", "pool", "cc", "bar"):
        cm_ = nc.semaphore(s)
        sems[s] = cm_.__enter__()
        semcm.append(cm_)
    block_cm = nc.Block()
    block = block_cm.__enter__()
    ctx = Ctx(nc, block, sems)

    ident = nc.alloc_sbuf_tensor("sb_ident", [128, 128], BF16)
    ones1 = nc.alloc_sbuf_tensor("sb_ones1", [1, 128], BF16)

    ctx.eps = nc.alloc_sbuf_tensor("sb_eps", [128, 1], F32)

    def _vprolog(e):
        e.memset(ctx.eps[:, :], LN_EPS)
    ctx.sec("vector", _vprolog)

    def _prolog(e):
        ctx.dma(e, ident[:, :], W["ident"][:, :], "d7")
        ctx.dma(e, ones1[:, :], W["ones1x128"][:, :], "d7")
        e.wait_ge(sems["d7"], ctx.cnt["d7"])
    ctx.sec("gpsimd", _prolog)
    ctx.prolog_d7 = ctx.cnt["d7"]
    ctx.barrier()

    cur = None
    if "l0a" in phases:
        emit_l0a(ctx, W, x_ext, xdr[0], kv_dr, ident, ones1)
        cur = xdr[0]
    if "l0f" in phases:
        src = cur if cur is not None else V(x_ext, HALO * DM, [[DM, TC], [1, DM]])
        emit_l0f(ctx, W, src, xdr[1], ident)
        cur = xdr[1]
    if "l1a" in phases:
        src = cur if cur is not None else V(x_ext, HALO * DM, [[DM, TC], [1, DM]])
        emit_l1a(ctx, W, src, xdr[0], ident, ones1, uag_in, uag_out)
        cur = xdr[0]
    if "l1f" in phases:
        src = cur if cur is not None else V(x_ext, HALO * DM, [[DM, TC], [1, DM]])
        emit_l1f(ctx, W, src, y_out, ident, ones1, xp_in, xp_out, c_dr,
                 yt_in, yt_out)
        cur = None
    if cur is not None:
        emit_copyout(ctx, cur, y_out)

    ctx.barrier()
    block_cm.__exit__(None, None, None)
    for cm_ in semcm:
        cm_.__exit__(None, None, None)
    return nc


def emit_copyout(ctx, src_dr, y_out):
    nc = ctx.nc
    es = ExitStack()
    t = sal(ctx, es, "cp_t", [128, DM], F32)
    first = [True]
    for s in range(NT):
        def _m(e, s=s):
            if first[0]:
                for sv in getattr(ctx, "pending", []):
                    ctx.wp(e, sv)
                first[0] = False
            sv = ctx.dma(e, t[:, :], src_dr[128 * s:128 * (s + 1), :], "d0")
            ctx.wp(e, sv)
            sv2 = ctx.dma(e, y_out[128 * s:128 * (s + 1), :], t[:, :], "d0")
            ctx.wp(e, sv2)
        ctx.sec("sync", _m)
    es.close()


def emit_l0a(ctx, W, x_ext, x1_dr, kv_dr, ident, ones1):
    nc = ctx.nc
    es0 = ExitStack()
    esA = ExitStack()
    qsb = sal(ctx, es0, "a_qsb", [128, NT * DM], BF16)
    wu = {"o": sal(ctx, es0, "a_wu_o", [128, 22 * 128], BF16)}
    obias = sal(ctx, es0, "a_obias", [1, DM], BF16)
    xt = [sal(ctx, esA, f"a_xt{i}", [128, DM], F32) for i in range(3)]
    lnt = [sal(ctx, esA, f"a_lnt{i}", [128, DM], BF16) for i in range(3)]
    lnf = [sal(ctx, esA, f"a_lnf{i}", [128, DM], BF16) for i in range(2)]
    kvst = [sal(ctx, esA, f"a_kvst{i}", [128, 2 * DM], BF16) for i in range(3)]
    for p in ("q", "k", "v"):
        wu[p] = sal(ctx, esA, f"a_wu_{p}", [128, 22 * 128], BF16)
    bbc = sal(ctx, esA, "a_bbc_qkv", [128, 3 * DM], BF16)
    ubias = sal(ctx, esA, "sb_ubias", [1, 4 * DM], BF16)
    ln = LNState(ctx, esA, "a")
    ptr = [pal(ctx, esA, f"a_ptr{i}", [128, DM], BF16) for i in range(2)]
    pj = {p: pal(ctx, esA, f"a_p{p}", [128, DM], F32) for p in ("q", "k", "v")}

    def _pl(e):
        ctx.dma(e, ubias[:, :], W["ubias"][:, :], "d7")
        for p in ("q", "k", "v", "o"):
            src = V(W[f"u_{p}"], 0, [[128, 128], [128 * 128, 22], [1, 128]])
            dst = wu[p].ap().rearrange("p (j c) -> p j c", j=22, c=128)
            ctx.dma(e, dst, src, "d7")
        e.wait_ge(ctx.sems["d7"], ctx.cnt["d7"])
    ctx.sec("gpsimd", _pl)
    d7_w = ctx.cnt["d7"]

    prev_evac = 0
    for j in range(3):
        def _pe(e, j=j, pv=prev_evac):
            if j == 0:
                ctx.w(e, "d7", d7_w)
            ctx.w(e, "dve", pv)
            last = None
            for h in range(2):
                last = e.matmul(pj["v"][:, 512 * h:512 * (h + 1)], ones1[0:1, :],
                                ubias[0:1, 1024 * j + 512 * h:1024 * j + 512 * (h + 1)],
                                start=True, stop=True)
            ctx.inc(last, "pe")
        ctx.sec("tensor", _pe)
        pe_now = ctx.cnt["pe"]

        def _ve(e, j=j, pe_now=pe_now):
            ctx.w(e, "pe", pe_now)
            if j == 0:
                e.tensor_copy(obias[0:1, :], ubias[0:1, 3 * DM:4 * DM])
            e.tensor_copy(bbc[:, 1024 * j:1024 * (j + 1)], pj["v"][:, :])
        ctx.sec("vector", _ve)
        prev_evac = ctx.cnt["dve"]

    ln_done = [0] * NTH
    sq_act = [0] * NTH
    pe_tr = [0] * NTH
    dve_lnf = [0] * NTH
    pe_proj = [0] * NTH
    dve_evac = [0] * NTH
    st_kv = [None] * NTH
    ld_x = [None] * NTH

    for s in range(NTH + 1):
        own = s >= (NTH - NT)
        par = s % 3
        J = s - 1

        if s < NTH:
            def _load(e, s=s, par=par):
                if s >= 3:
                    ctx.w(e, "dve", ln_done[s - 3])
                    ctx.w(e, "act", sq_act[s - 3])
                ld_x[s] = ctx.dma(e, xt[par][:, :],
                                  x_ext[128 * s:128 * (s + 1), :], f"d{s % 2}")
            ctx.sec("sync" if s % 2 == 0 else "gpsimd", _load)

        if s >= 1:
            jown = J >= (NTH - NT)

            def _evac(e, J=J, jown=jown):
                ctx.w(e, "pe", pe_proj[J])
                if J >= 3:
                    ctx.wp(e, st_kv[J - 3])
                e.tensor_add(kvst[J % 3][:, 0:DM], pj["k"][:, :],
                             bbc[:, DM:2 * DM])
                e.tensor_add(kvst[J % 3][:, DM:2 * DM], pj["v"][:, :],
                             bbc[:, 2 * DM:3 * DM])
                if jown:
                    so = J - (NTH - NT)
                    e.tensor_add(qsb[:, DM * so:DM * (so + 1)], pj["q"][:, :],
                                 bbc[:, 0:DM])
            ctx.sec("vector", _evac)
            dve_evac[J] = ctx.cnt["dve"]

            def _st(e, J=J):
                ctx.w(e, "dve", dve_evac[J])
                st_kv[J] = ctx.dma(e, kv_dr[128 * J:128 * (J + 1), :],
                                   kvst[J % 3][:, :], f"d{2 + J % 2}")
            ctx.sec("scalar" if J % 2 == 0 else "gpsimd", _st)

        if s < NTH:
            d = emit_ln(ctx, ln, xt[par][:, :], lnt[par][:, :], [ld_x[s]],
                        norm_waits=([("pe", pe_tr[s - 3])] if s >= 3 else ()))
            ln_done[s] = d
            sq_act[s] = ln.last_sq_act

            def _tr(e, s=s, par=par, d=d):
                ctx.w(e, "dve", d)
                if s >= 2:
                    ctx.w(e, "dve", dve_lnf[s - 2])
                last = None
                for b in range(8):
                    last = e.transpose(ptr[s % 2][:, 128 * b:128 * (b + 1)],
                                       lnt[par][:, 128 * b:128 * (b + 1)],
                                       ident[:, :])
                ctx.inc(last, "pe")
            ctx.sec("tensor", _tr)
            pe_tr[s] = ctx.cnt["pe"]

            def _lnf(e, s=s):
                ctx.w(e, "pe", pe_tr[s])
                if s >= 2:
                    ctx.w(e, "pe", pe_proj[s - 2])
                e.tensor_copy(lnf[s % 2][:, :], ptr[s % 2][:, :])
            ctx.sec("vector", _lnf)
            dve_lnf[s] = ctx.cnt["dve"]

            projs = ("q", "k", "v") if own else ("k", "v")

            def _proj(e, s=s, projs=projs):
                ctx.w(e, "dve", dve_lnf[s])
                if s >= 1:
                    ctx.w(e, "dve", dve_evac[s - 1])
                started = {p: [False, False] for p in projs}
                last = None
                for bp in range(8):
                    lhsT = lnf[s % 2][:, 128 * bp:128 * (bp + 1)]
                    for p in projs:
                        tri = [(bp, bp)]
                        if bp + 1 < 8:
                            tri.append((bp + 1, 8 + bp))
                        if bp - 1 >= 0:
                            tri.append((bp - 1, 15 + bp - 1))
                        for (b, jj) in tri:
                            bank = b // 4
                            st = not started[p][bank]
                            started[p][bank] = True
                            last = e.matmul(pj[p][:, 128 * b:128 * (b + 1)],
                                            lhsT,
                                            wu[p][:, 128 * jj:128 * (jj + 1)],
                                            start=st, stop=False,
                                            skip_group_check=True)
                ctx.inc(last, "pe")
            ctx.sec("tensor", _proj)
            pe_proj[s] = ctx.cnt["pe"]

    d2_tot, d3_tot = ctx.cnt["d2"], ctx.cnt["d3"]
    ctx.barrier(drains=st_kv[-2:])
    esA.close()

    esB = ExitStack()
    CHG = [[(0, [0, 1, 2, 3, 4])],
           [(5, [8, 16]), (7, [32, 64]), (9, [128])],
           [(10, [256]), (11, [512, 1024]), (13, [2048, 4096])]]
    CW = 5 * DM
    PVROW = NBANDS * DM
    kcb = [sal(ctx, esB, f"b_kcb{i}", [128, CW], BF16) for i in range(3)]
    vcb = [sal(ctx, esB, f"b_vcb{i}", [128, CW], BF16) for i in range(3)]
    qkprod = sal(ctx, esB, "b_qkp", [128, CW], BF16)
    pvcP = sal(ctx, esB, "b_pvcP", [128, CW], BF16)
    pvcD = sal(ctx, esB, "b_pvcD", [128, CW], BF16)
    scor = [sal(ctx, esB, f"b_scor{i}", [128, NBANDS * NH], F32) for i in range(2)]
    escr = [sal(ctx, esB, f"b_escr{i}", [128, NBANDS * NH], F32) for i in range(2)]
    Pb = [sal(ctx, esB, f"b_P{i}", [128, NBANDS * NH], BF16) for i in range(2)]
    sm = sal(ctx, esB, "b_sm", [128, 3 * NH], F32)
    accA = [sal(ctx, esB, f"b_accA{i}", [128, DM], F32) for i in range(2)]
    accB = [sal(ctx, esB, f"b_accB{i}", [128, DM], F32) for i in range(2)]
    accb = [sal(ctx, esB, f"b_accb{i}", [128, DM], BF16) for i in range(2)]
    msk = sal(ctx, esB, "b_msk", [128, NT * NBANDS], F32)
    afm = [sal(ctx, esB, f"c_afm{i}", [128, DM], BF16) for i in range(2)]
    x0t = [sal(ctx, esB, f"c_x0{i}", [128, DM], F32) for i in range(2)]
    ot = [sal(ctx, esB, f"c_ot{i}", [128, DM], F32) for i in range(2)]
    lnn = sal(ctx, esB, "c_lnn", [128, DM], F32)
    x1t = [sal(ctx, esB, f"c_x1{i}", [128, DM], F32) for i in range(2)]
    lnC = LNState(ctx, esB, "c")
    ptrc = [pal(ctx, esB, f"c_ptr{i}", [128, DM], BF16) for i in range(2)]
    po = [pal(ctx, esB, f"c_po{i}", [128, DM], F32) for i in range(2)]

    def _mload(e):
        src = V(W["masks"], 0, [[NBANDS, 128], [128 * NBANDS, NT], [1, NBANDS]])
        dst = msk.ap().rearrange("p (i j) -> p i j", i=NT, j=NBANDS)
        ctx.dma(e, dst, src, "d7")
        e.wait_ge(ctx.sems["d7"], ctx.cnt["d7"])
    ctx.sec("gpsimd", _mload)
    d7_m = ctx.cnt["d7"]

    NC3 = 3 * NT
    ld_k = [None] * NC3
    ld_v = [None] * NC3
    dve_qkc = [0] * NC3
    pool_pvm = [0] * NC3
    dve_pvm = [0] * NC3
    dve_qk = [0] * NT
    act_exp = [0] * NT
    dve_P = [0] * NT
    pool_pvA = [0] * NT
    dve_fin = [0] * NT
    pe_tr = [0] * NT
    a_afm = [0] * NT
    pe_o = [0] * NT
    a_ot = [0] * NT
    dve_x1 = [0] * NT
    ld_x0 = [None] * NT
    st_x1 = [None] * NT
    kvrow = 2 * DM

    for I in range(NT + 1):
        par = I % 2
        J = I - 1
        jpar = J % 2
        base_row = HALO + 128 * I

        if I < NT:
            def _kload(e, I=I, base_row=base_row):
                if I == 0:
                    ctx.w(e, "d2", d2_tot)
                    ctx.w(e, "d3", d3_tot)
                for c in range(3):
                    g = 3 * I + c
                    if g >= 3:
                        ctx.w(e, "dve", dve_qkc[g - 3])
                    last = None
                    for (pos, ds) in CHG[c]:
                        step = ds[1] - ds[0] if len(ds) > 1 else 1
                        src = V(kv_dr, (base_row - ds[0]) * kvrow,
                                [[kvrow, 128], [-step * kvrow, len(ds)], [1, DM]])
                        dst = V(kcb[g % 3], DM * (pos - 5 * c),
                                [[CW, 128], [DM, len(ds)], [1, DM]])
                        last = ctx.dma(e, dst, src, f"d{g % 2}")
                    ld_k[g] = last
            ctx.sec("sync", _kload)

            def _vload(e, I=I, base_row=base_row):
                for c in range(3):
                    g = 3 * I + c
                    if g >= 3:
                        if c == 2:
                            ctx.w(e, "dve", dve_pvm[g - 3])
                        else:
                            ctx.w(e, "pool", pool_pvm[g - 3])
                    last = None
                    for (pos, ds) in CHG[c]:
                        step = ds[1] - ds[0] if len(ds) > 1 else 1
                        src = V(kv_dr, (base_row - ds[0]) * kvrow + DM,
                                [[kvrow, 128], [-step * kvrow, len(ds)], [1, DM]])
                        dst = V(vcb[g % 3], DM * (pos - 5 * c),
                                [[CW, 128], [DM, len(ds)], [1, DM]])
                        last = ctx.dma(e, dst, src, f"d{2 + g % 2}")
                    ld_v[g] = last
            ctx.sec("scalar", _vload)

            def _qk(e, I=I, par=par):
                if I == 0:
                    ctx.w(e, "d7", d7_m)
                if I >= 2:
                    ctx.w(e, "act", act_exp[I - 2])
                for c in range(3):
                    g = 3 * I + c
                    ctx.wp(e, ld_k[g])
                    qv = V(qsb, DM * I,
                           [[NT * DM, 128], [0, 5], [64, NH], [1, HD]])
                    kv_ = V(kcb[g % 3], 0,
                            [[CW, 128], [DM, 5], [64, NH], [1, HD]])
                    pv = V(qkprod, 0, [[CW, 128], [DM, 5], [64, NH], [1, HD]])
                    e.tensor_mul(pv, qv, kv_)
                    rin = V(qkprod, 0, [[CW, 128], [64, 5 * NH], [1, HD]])
                    e.reduce_sum(scor[par][:, 5 * NH * c:5 * NH * (c + 1)],
                                 rin, AX.X)
                    dve_qkc[g] = ctx.cnt["dve"]
                mv = V(msk, NBANDS * I, [[NT * NBANDS, 128], [1, NBANDS], [0, NH]])
                sv = V(scor[par], 0, [[NBANDS * NH, 128], [NH, NBANDS], [1, NH]])
                e.tensor_add(sv, sv, mv)
                svh = V(scor[par], 0, [[NBANDS * NH, 128], [1, NH], [NH, NBANDS]])
                e.reduce_max(sm[:, 0:NH], svh, AX.X)
                mxv = V(sm, 0, [[3 * NH, 128], [0, NBANDS], [1, NH]])
                e.tensor_sub(scor[par][:, :], scor[par][:, :], mxv)
            ctx.sec("vector", _qk)
            dve_qk[I] = ctx.cnt["dve"]

            def _exp(e, I=I, par=par):
                ctx.w(e, "dve", dve_qk[I])
                if I >= 2:
                    ctx.w(e, "dve", dve_P[I - 2])
                e.activation(out=escr[par][:, :], in_=scor[par][:, :],
                             func=AF.Exp, scale=0.125)
            ctx.sec("scalar", _exp)
            act_exp[I] = ctx.cnt["act"]

            def _P(e, I=I, par=par):
                ctx.w(e, "act", act_exp[I])
                if I >= 2:
                    ctx.w(e, "pool", pool_pvm[3 * (I - 2) + 1])
                ev = V(escr[par], 0, [[NBANDS * NH, 128], [1, NH], [NH, NBANDS]])
                e.reduce_sum(sm[:, NH:2 * NH], ev, AX.X)
                e.reciprocal(sm[:, 2 * NH:3 * NH], sm[:, NH:2 * NH])
                rzv = V(sm, 2 * NH, [[3 * NH, 128], [0, NBANDS], [1, NH]])
                e.tensor_mul(Pb[par][:, :], escr[par][:, :], rzv)
            ctx.sec("vector", _P)
            dve_P[I] = ctx.cnt["dve"]

            def _pvB(e, I=I, par=par):
                g = 3 * I + 2
                ctx.wp(e, ld_v[g])
                pvw = V(Pb[par], 10 * NH,
                        [[NBANDS * NH, 128], [NH, 5], [1, NH], [0, HD]])
                vbw = V(vcb[g % 3], 0, [[CW, 128], [DM, 5], [64, NH], [1, HD]])
                e.tensor_mul(pvcD[:, :], vbw, pvw)
                dve_pvm[g] = ctx.cnt["dve"]
                e.tensor_add(accB[par][:, :], pvcD[:, 0:DM], pvcD[:, DM:2 * DM])
                for d in range(2, 5):
                    e.tensor_add(accB[par][:, :], accB[par][:, :],
                                 pvcD[:, DM * d:DM * (d + 1)])
            ctx.sec("vector", _pvB)

            def _pvA(e, I=I, par=par):
                ctx.w(e, "dve", dve_P[I])
                if I >= 2:
                    ctx.w(e, "dve", dve_fin[I - 2])
                for c in range(2):
                    g = 3 * I + c
                    ctx.wp(e, ld_v[g])
                    pvw = V(Pb[par], 5 * c * NH,
                            [[NBANDS * NH, 128], [NH, 5], [1, NH], [0, HD]])
                    vbw = V(vcb[g % 3], 0, [[CW, 128], [DM, 5], [64, NH], [1, HD]])
                    e.tensor_mul(pvcP[:, :], vbw, pvw)
                    pool_pvm[g] = ctx.cnt["pool"]
                    if c == 0:
                        e.tensor_add(accA[par][:, :], pvcP[:, 0:DM],
                                     pvcP[:, DM:2 * DM])
                    else:
                        e.tensor_add(accA[par][:, :], accA[par][:, :],
                                     pvcP[:, 0:DM])
                        e.tensor_add(accA[par][:, :], accA[par][:, :],
                                     pvcP[:, DM:2 * DM])
                    for d in range(2, 5):
                        e.tensor_add(accA[par][:, :], accA[par][:, :],
                                     pvcP[:, DM * d:DM * (d + 1)])
            ctx.sec("gpsimd", _pvA)
            pool_pvA[I] = ctx.cnt["pool"]

            def _x0(e, I=I, par=par):
                if I >= 2:
                    ctx.w(e, "dve", dve_x1[I - 2])
                ld_x0[I] = ctx.dma(e, x0t[par][:, :],
                                   x_ext[HALO + 128 * I:HALO + 128 * (I + 1), :],
                                   "d6")
            ctx.sec("gpsimd", _x0)

        if I >= 1:
            def _fin(e, J=J, jpar=jpar):
                ctx.w(e, "pool", pool_pvA[J])
                if J >= 2:
                    ctx.w(e, "pe", pe_tr[J - 2])
                e.tensor_add(accb[jpar][:, :], accA[jpar][:, :],
                             accB[jpar][:, :])
            ctx.sec("vector", _fin)
            dve_fin[J] = ctx.cnt["dve"]

            def _tr(e, J=J, jpar=jpar):
                ctx.w(e, "dve", dve_fin[J])
                if J >= 2:
                    ctx.w(e, "act", a_afm[J - 2])
                last = None
                for b in range(8):
                    last = e.transpose(ptrc[jpar][:, 128 * b:128 * (b + 1)],
                                       accb[jpar][:, 128 * b:128 * (b + 1)],
                                       ident[:, :])
                ctx.inc(last, "pe")
            ctx.sec("tensor", _tr)
            pe_tr[J] = ctx.cnt["pe"]

            def _afm(e, J=J, jpar=jpar):
                ctx.w(e, "pe", pe_tr[J])
                if J >= 2:
                    ctx.w(e, "pe", pe_o[J - 2])
                e.activation(out=afm[jpar][:, :], in_=ptrc[jpar][:, :],
                             func=AF.Copy)
            ctx.sec("scalar", _afm)
            a_afm[J] = ctx.cnt["act"]

            def _oproj(e, J=J, jpar=jpar):
                ctx.w(e, "act", a_afm[J])
                if J >= 2:
                    ctx.w(e, "act", a_ot[J - 2])
                started = [False, False]
                for bp in range(8):
                    lhsT = afm[jpar][:, 128 * bp:128 * (bp + 1)]
                    tri = [(bp, bp)]
                    if bp + 1 < 8:
                        tri.append((bp + 1, 8 + bp))
                    if bp - 1 >= 0:
                        tri.append((bp - 1, 15 + bp - 1))
                    for (b, jj) in tri:
                        bank = b // 4
                        st = not started[bank]
                        started[bank] = True
                        last = e.matmul(po[jpar][:, 128 * b:128 * (b + 1)], lhsT,
                                        wu["o"][:, 128 * jj:128 * (jj + 1)],
                                        start=st, stop=False,
                                        skip_group_check=True)
                for h in range(2):
                    last = e.matmul(po[jpar][:, 512 * h:512 * (h + 1)],
                                    ones1[0:1, :],
                                    obias[0:1, 512 * h:512 * (h + 1)],
                                    start=False, stop=False,
                                    skip_group_check=True)
                ctx.inc(last, "pe")
            ctx.sec("tensor", _oproj)
            pe_o[J] = ctx.cnt["pe"]

            def _ot(e, J=J, jpar=jpar):
                ctx.w(e, "pe", pe_o[J])
                if J >= 2:
                    ctx.w(e, "dve", dve_x1[J - 2])
                e.activation(out=ot[jpar][:, :], in_=po[jpar][:, :],
                             func=AF.Copy)
            ctx.sec("scalar", _ot)
            a_ot[J] = ctx.cnt["act"]

            d = emit_ln(ctx, lnC, ot[jpar][:, :], lnn[:, :],
                        [("act", a_ot[J]), ld_x0[J]],
                        res_add_ap=x0t[jpar][:, :], dst2=x1t[jpar][:, :],
                        norm_waits=([st_x1[J - 2]] if J >= 2 else ()))
            dve_x1[J] = d

            def _stx(e, J=J, jpar=jpar):
                ctx.w(e, "dve", dve_x1[J])
                st_x1[J] = ctx.dma(e, x1_dr[128 * J:128 * (J + 1), :],
                                   x1t[jpar][:, :], f"d{4 + jpar}")
            ctx.sec("scalar", _stx)

    ctx.pending = [sv for sv in st_x1 if sv is not None][-2:]
    ctx.barrier(drains=st_x1[-2:])
    esB.close()
    es0.close()


def emit_l0f(ctx, W, x_in, x_out, ident):
    nc = ctx.nc
    NO1, NOH, NOC = 32, 32, 8
    drains = getattr(ctx, "pending", [])
    ctx.pending = []

    esS = ExitStack()
    for nm, w_ in (("s1cm", 32), ("s2cm", 32), ("h1scm", 32), ("h2scm", 32),
                   ("s3cm", 8)):
        W[nm + "_sb"] = sal(ctx, esS, "sb_" + nm, [128, w_], F32)

    def _sprolog(e):
        for nm in ("s1cm", "s2cm", "h1scm", "h2scm", "s3cm"):
            ctx.dma(e, W[nm + "_sb"][:, :], W[nm][:, :], "d7")
        e.wait_ge(ctx.sems["d7"], ctx.cnt["d7"])
    ctx.sec("gpsimd", _sprolog)
    scale_tot = ("d7", ctx.cnt["d7"])

    for tg in range(2):
        es = ExitStack()
        esL = ExitStack()
        g_sb = sal(ctx, es, f"f_g{tg}", [128, NO1 * 1024], BF16)
        xt = [sal(ctx, es, f"f_xt{tg}{i}", [128, DM], F32) for i in range(2)]
        lnt = [sal(ctx, es, f"f_lnt{tg}{i}", [128, DM], BF16) for i in range(2)]
        wh = [sal(ctx, es, f"f_wh{tg}{i}", [128, 32 * 128], BF16) for i in range(2)]
        dt_scr = sal(ctx, es, f"f_dt{tg}", [128, DM], BF16)
        x2t = [sal(ctx, es, f"f_x2{tg}{i}", [128, DM], F32) for i in range(2)]
        lnF = LNState(ctx, es, f"f{tg}")
        lnO = LNState(ctx, es, f"fo{tg}")
        lnx = sal(ctx, esL, f"f_lnx{tg}", [128, 8 * 1024], BF16)
        u_scr = sal(ctx, esL, f"f_u{tg}", [128, 1024], BF16)
        v_scr = sal(ctx, esL, f"f_v{tg}", [128, 1024], BF16)
        w12 = [sal(ctx, esL, f"f_w12{tg}{i}", [128, 16 * 128], BF16) for i in range(2)]
        pu = pal(ctx, es, f"f_pu{tg}", [128, 1024], F32)
        pv2 = pal(ctx, es, f"f_pv2{tg}", [128, 1024], F32)
        esP = ExitStack()
        ptr = pal(ctx, esP, f"f_ptr{tg}", [128, DM], BF16)

        ln_done = [0] * 8
        sq_act = [0] * 8
        pe_tr = [0] * 8
        dve_lnx = [0] * 8
        ld_x = [None] * 8
        for j in range(8):
            par = j % 2
            row0 = tg * 1024 + 128 * j

            def _ld(e, j=j, par=par, row0=row0, tg=tg):
                if tg == 0 and j == 0:
                    for sv in drains:
                        ctx.wp(e, sv)
                if j >= 2:
                    ctx.w(e, "dve", ln_done[j - 2])
                    ctx.w(e, "act", sq_act[j - 2])
                ld_x[j] = ctx.dma(e, xt[par][:, :],
                                  x_in[row0:row0 + 128, :], f"d{par}")
            ctx.sec("sync", _ld)

            d = emit_ln(ctx, lnF, xt[par][:, :], lnt[par][:, :], [ld_x[j]],
                        norm_waits=([("pe", pe_tr[j - 2])] if j >= 2 else ()))
            ln_done[j] = d
            sq_act[j] = lnF.last_sq_act

            def _tr(e, j=j, par=par, d=d):
                ctx.w(e, "dve", d)
                if j >= 1:
                    ctx.w(e, "dve", dve_lnx[j - 1])
                last = None
                for b in range(8):
                    last = e.transpose(ptr[:, 128 * b:128 * (b + 1)],
                                       lnt[par][:, 128 * b:128 * (b + 1)],
                                       ident[:, :])
                ctx.inc(last, "pe")
            ctx.sec("tensor", _tr)
            pe_tr[j] = ctx.cnt["pe"]

            def _ev(e, j=j):
                ctx.w(e, "pe", pe_tr[j])
                dst = V(lnx, 128 * j, [[8 * 1024, 128], [1024, 8], [1, 128]])
                e.tensor_copy(dst, ptr[:, :])
            ctx.sec("vector", _ev)
            dve_lnx[j] = ctx.cnt["dve"]

        lnx_done = dve_lnx[7]
        esP.close()
        esH = ExitStack()
        phw2 = [pal(ctx, esH, f"f_phw{tg}{i}", [128, 1024], F32) for i in range(2)]

        ld_w12 = [None] * NO1
        pe_uv = [0] * NO1
        act_u = [0] * NO1
        dve_g = [0] * NO1
        for o in range(NO1):
            par = o % 2

            def _wl(e, o=o, par=par):
                if o >= 2:
                    ctx.w(e, "pe", pe_uv[o - 2])
                else:
                    ctx.w(e, "pe", getattr(ctx, "wgate", 0))
                for wnm in ("w1T", "w2T"):
                    off = 0 if wnm == "w1T" else 8 * 128
                    src = V(W[wnm], 128 * o,
                            [[DH, 128], [128 * DH, 8], [1, 128]])
                    dst = V(w12[par], off, [[16 * 128, 128], [128, 8], [1, 128]])
                    ld_w12[o] = ctx.dma(e, dst, src, f"d{6 + par}")
            ctx.sec("gpsimd", _wl)

            def _mm(e, o=o, par=par):
                if o == 0:
                    ctx.w(e, "dve", lnx_done)
                ctx.wp(e, ld_w12[o])
                last = None
                for wi, ps in ((0, pu), (1, pv2)):
                    if wi == 0 and o >= 1:
                        ctx.w(e, "act", act_u[o - 1])
                    if wi == 1 and o >= 1:
                        ctx.w(e, "dve", dve_g[o - 1])
                    for ki in range(8):
                        lhsT = w12[par][:, (8 * wi + ki) * 128:(8 * wi + ki + 1) * 128]
                        for h in range(2):
                            last = e.matmul(ps[:, 512 * h:512 * (h + 1)], lhsT,
                                            V(lnx, ki * 1024 + 512 * h,
                                              [[8 * 1024, 128], [1, 512]]),
                                            start=(ki == 0), stop=(ki == 7),
                                            skip_group_check=True)
                    if wi == 0:
                        ctx.inc(last, "pe")
                ctx.inc(last, "pe")
            ctx.sec("tensor", _mm)
            pe_uv[o] = ctx.cnt["pe"]
            pe_u_part = pe_uv[o] - 1

            def _ue(e, o=o, tg=tg, pw=pe_u_part):
                ctx.w(e, "pe", pw)
                if tg == 0 and o == 0:
                    ctx.wp(e, scale_tot)
                if o >= 1:
                    ctx.w(e, "dve", dve_g[o - 1])
                e.activation(out=u_scr[:, :], in_=pu[:, :], func=AF.Silu,
                             scale=W["s1cm_sb"][:, o:o + 1])
            ctx.sec("scalar", _ue)
            act_u[o] = ctx.cnt["act"]

            def _ge(e, o=o, tg=tg):
                ctx.w(e, "pe", pe_uv[o])
                if tg == 0 and o == 0:
                    ctx.wp(e, scale_tot)
                e.tensor_scalar(v_scr[:, :], pv2[:, :],
                                W["s2cm_sb"][:, o:o + 1], None, ALU.mult)
                ctx.w(e, "act", act_u[o])
                e.tensor_mul(g_sb[:, 1024 * o:1024 * (o + 1)], u_scr[:, :],
                             v_scr[:, :])
            ctx.sec("vector", _ge)
            dve_g[o] = ctx.cnt["dve"]

        esL.close()
        ctx.wgate = pe_uv[NO1 - 1]
        es_b = ExitStack()
        b_sb = sal(ctx, es_b, f"f_b{tg}", [128, NOH * 1024], BF16)

        def hpass(src_sb, dst_write, wname, scname, first_wait_dve, first_wait_act,
                  war_pe_total):
            ld_wh = [None] * NOH
            pe_h = [0] * NOH
            act_h = [0] * NOH
            for o in range(NOH):
                par = o % 2

                def _wl(e, o=o, par=par):
                    if o >= 2:
                        ctx.w(e, "pe", pe_h[o - 2])
                    else:
                        ctx.w(e, "pe", getattr(ctx, "wgate", 0))
                    src = V(W[wname], 128 * o,
                            [[DH, 128], [128 * DH, 32], [1, 128]])
                    dst = wh[par].ap().rearrange("p (k c) -> p k c", k=32, c=128)
                    ld_wh[o] = ctx.dma(e, dst, src, f"d{6 + par}")
                ctx.sec("gpsimd", _wl)

                def _mm(e, o=o, par=par):
                    if o == 0:
                        if first_wait_dve:
                            ctx.w(e, "dve", first_wait_dve)
                        if first_wait_act:
                            ctx.w(e, "act", first_wait_act)
                    ctx.wp(e, ld_wh[o])
                    if o >= 2:
                        ctx.w(e, "act", act_h[o - 2])
                    ph = phw2[o % 2]
                    last = None
                    for ki in range(32):
                        lhsT = wh[par][:, ki * 128:(ki + 1) * 128]
                        for h in range(2):
                            last = e.matmul(ph[:, 512 * h:512 * (h + 1)], lhsT,
                                            V(src_sb, ki * 1024 + 512 * h,
                                              [[NOH * 1024, 128], [1, 512]]),
                                            start=(ki == 0), stop=(ki == 31),
                                            skip_group_check=True)
                    ctx.inc(last, "pe")
                ctx.sec("tensor", _mm)
                pe_h[o] = ctx.cnt["pe"]

                def _be(e, o=o, tg=tg):
                    ctx.w(e, "pe", pe_h[o])
                    if tg == 0 and o == 0:
                        ctx.wp(e, scale_tot)
                    if war_pe_total and o == 0:
                        ctx.w(e, "pe", war_pe_total[0])
                    e.activation(out=dst_write(o), in_=phw2[o % 2][:, :],
                                 func=AF.Gelu, scale=W[scname][:, o:o + 1])
                ctx.sec("scalar", _be)
                act_h[o] = ctx.cnt["act"]
            return pe_h, act_h

        pe_h1, act_b = hpass(g_sb, lambda o: b_sb[:, 1024 * o:1024 * (o + 1)],
                             "h1T", "h1scm_sb", dve_g[NO1 - 1], 0, None)
        ctx.wgate = pe_h1[NOH - 1]
        pe_h2, act_c = hpass(b_sb, lambda o: g_sb[:, 1024 * o:1024 * (o + 1)],
                             "h2T", "h2scm_sb", 0, act_b[NOH - 1], [pe_h1[NOH - 1]])
        ctx.wgate = pe_h2[NOH - 1]
        es_b.close()
        es_d = ExitStack()
        dfm = sal(ctx, es_d, f"f_dfm{tg}", [128, 8 * 1024], BF16)

        ld_w3 = [None] * NOC
        pe_w3 = [0] * NOC
        dve_dfm = [0] * NOC
        for oc in range(NOC):
            par = oc % 2

            def _wl(e, oc=oc, par=par):
                if oc >= 2:
                    ctx.w(e, "pe", pe_w3[oc - 2])
                else:
                    ctx.w(e, "pe", getattr(ctx, "wgate", 0))
                src = V(W["w3T"], 128 * oc,
                        [[DM, 128], [128 * DM, 32], [1, 128]])
                dst = wh[par].ap().rearrange("p (k c) -> p k c", k=32, c=128)
                ld_w3[oc] = ctx.dma(e, dst, src, f"d{6 + par}")
            ctx.sec("gpsimd", _wl)

            def _mm(e, oc=oc, par=par):
                if oc == 0:
                    ctx.w(e, "act", act_c[NOH - 1])
                ctx.wp(e, ld_w3[oc])
                if oc >= 2:
                    ctx.w(e, "dve", dve_dfm[oc - 2])
                last = None
                for ki in range(32):
                    lhsT = wh[par][:, ki * 128:(ki + 1) * 128]
                    for h in range(2):
                        last = e.matmul(phw2[oc % 2][:, 512 * h:512 * (h + 1)], lhsT,
                                        V(g_sb, ki * 1024 + 512 * h,
                                          [[NO1 * 1024, 128], [1, 512]]),
                                        start=(ki == 0), stop=(ki == 31),
                                        skip_group_check=True)
                ctx.inc(last, "pe")
            ctx.sec("tensor", _mm)
            pe_w3[oc] = ctx.cnt["pe"]

            def _de(e, oc=oc, tg=tg):
                ctx.w(e, "pe", pe_w3[oc])
                if tg == 0 and oc == 0:
                    ctx.wp(e, scale_tot)
                e.tensor_scalar(dfm[:, 1024 * oc:1024 * (oc + 1)],
                                phw2[oc % 2][:, :],
                                W["s3cm_sb"][:, oc:oc + 1], None, ALU.mult)
            ctx.sec("vector", _de)
            dve_dfm[oc] = ctx.cnt["dve"]

        ctx.wgate = pe_w3[NOC - 1]
        esH.close()
        esP2 = ExitStack()
        ptrF = pal(ctx, esP2, f"f_ptrF{tg}", [128, DM], BF16)
        pe_trF = [0] * 8
        dve_out = [0] * 8
        st_x2 = [None] * 8
        ld_x1 = [None] * 8
        for j in range(8):
            par = j % 2
            row0 = tg * 1024 + 128 * j

            def _ld(e, j=j, par=par, row0=row0):
                if j >= 2:
                    ctx.w(e, "dve", dve_out[j - 2])
                else:
                    ctx.w(e, "dve", ln_done[6 + j])
                    ctx.w(e, "act", sq_act[6 + j])
                ld_x1[j] = ctx.dma(e, xt[par][:, :], x_in[row0:row0 + 128, :],
                                   f"d{par}")
            ctx.sec("sync", _ld)

            def _tr(e, j=j):
                if j == 0:
                    ctx.w(e, "dve", dve_dfm[NOC - 1])
                if j >= 1:
                    ctx.w(e, "dve", dve_out[j - 1])
                last = None
                for oc in range(8):
                    last = e.transpose(
                        ptrF[:, 128 * oc:128 * (oc + 1)],
                        dfm[:, 1024 * oc + 128 * j:1024 * oc + 128 * (j + 1)],
                        ident[:, :])
                ctx.inc(last, "pe")
            ctx.sec("tensor", _tr)
            pe_trF[j] = ctx.cnt["pe"]

            def _ev(e, j=j):
                ctx.w(e, "pe", pe_trF[j])
                if j >= 1:
                    ctx.w(e, "act", lnO.last_sq_act)
                e.tensor_copy(dt_scr[:, :], ptrF[:, :])
            ctx.sec("vector", _ev)

            d = emit_ln(ctx, lnO, dt_scr[:, :], lnF.scr[:, :],
                        [("dve", ctx.cnt["dve"]), ld_x1[j]],
                        res_add_ap=xt[par][:, :], dst2=x2t[par][:, :],
                        norm_waits=([st_x2[j - 2]] if j >= 2 else ()))
            dve_out[j] = d

            def _st(e, j=j, par=par, row0=row0):
                ctx.w(e, "dve", dve_out[j])
                st_x2[j] = ctx.dma(e, x_out[row0:row0 + 128, :], x2t[par][:, :],
                                   f"d{4 + par}")
            ctx.sec("scalar", _st)

        ctx.pending.extend([sv for sv in st_x2 if sv is not None][-2:])
        ctx.barrier(drains=st_x2[-2:])
        esP2.close()
        es_d.close()
        es.close()
    esS.close()


def emit_l1a(ctx, W, x_in, x_out, ident, ones1, uag_in, uag_out):
    nc = ctx.nc
    drains = getattr(ctx, "pending", [])
    ctx.pending = []
    NG = 16
    es = ExitStack()
    y_sb = sal(ctx, es, "s_y", [128, NG * 2048], BF16)
    xt = [sal(ctx, es, f"s_xt{i}", [128, DM], F32) for i in range(2)]
    lnt = [sal(ctx, es, f"s_lnt{i}", [128, DM], BF16) for i in range(2)]
    u_buf = sal(ctx, es, "s_u", [128, 16], F32)
    gcol = sal(ctx, es, "s_gcol", [128, 16], F32)
    cvec = sal(ctx, es, "s_cvec", [128, 6 * 16], F32)
    bp_sb = sal(ctx, es, "s_bp", [128, 16 * 16], F32)
    cpt_sb = sal(ctx, es, "s_cpt", [128, 16 * 16], F32)
    uago = sal(ctx, es, "s_uago", [128, 4 * 16], F32)
    ugb = sal(ctx, es, "s_ugb", [128, 16], BF16)
    p9 = sal(ctx, es, "s_p9", [128, 1], F32)
    t16 = sal(ctx, es, "s_t16", [128, 16], F32)
    boutr = sal(ctx, es, "s_bout", [1, DM], BF16)
    wrow = sal(ctx, es, "s_wrow", [1, DM], BF16)
    wb_sb = sal(ctx, es, "s_wb", [128, DM], BF16)
    lnS = LNState(ctx, es, "s")
    lnSo = LNState(ctx, es, "so")
    BIN, DP, DEC, SEL, UPV, UG = range(6)

    def _pl(e):
        ctx.dma(e, cvec[:, BIN * 16:BIN * 16 + 16], W["bincm"][:, :], "d7")
        ctx.dma(e, cvec[:, DP * 16:DP * 16 + 16], W["Dpcm"][:, :], "d7")
        ctx.dma(e, cvec[:, DEC * 16:DEC * 16 + 16], W["decaycm"][:, :], "d7")
        ctx.dma(e, cvec[:, SEL * 16:SEL * 16 + 4], W["selread"][:, :], "d7")
        src = V(W["Bp"], 0, [[16, 128], [128 * 16, 16], [1, 16]])
        ctx.dma(e, bp_sb.ap().rearrange("p (g j) -> p g j", g=16, j=16), src, "d7")
        src = V(W["CpT"], 0, [[16, 128], [128 * 16, 16], [1, 16]])
        ctx.dma(e, cpt_sb.ap().rearrange("p (g j) -> p g j", g=16, j=16), src, "d7")
        ctx.dma(e, boutr[:, :], W["boutrow"][:, :], "d7")
        e.wait_ge(ctx.sems["d7"], ctx.cnt["d7"])
    ctx.sec("gpsimd", _pl)
    pl_tot = ("d7", ctx.cnt["d7"])

    def _gc(e):
        ctx.wp(e, pl_tot)
        e.memset(p9[:, :], DECAY)
        for g in range(NG):
            e.tensor_mul(t16[:, :], bp_sb[:, 16 * g:16 * (g + 1)],
                         cpt_sb[:, 16 * g:16 * (g + 1)])
            e.reduce_sum(gcol[:, g:g + 1], t16[:, :], AX.X)
    ctx.sec("vector", _gc)
    gcol_done = ctx.cnt["dve"]

    esL = ExitStack()
    lnx2 = sal(ctx, esL, "s_lnx2", [128, 8 * 2048], BF16)
    esA = ExitStack()
    ptr = pal(ctx, esA, "s_ptr", [128, DM], BF16)
    ln_done = [0] * NT
    sq_act = [0] * NT
    pe_tr = [0] * NT
    dve_lnx = [0] * NT
    ld_x = [None] * NT
    for j in range(NT):
        par = j % 2

        def _ld(e, j=j, par=par):
            if j == 0:
                for sv in drains:
                    ctx.wp(e, sv)
            if j >= 2:
                ctx.w(e, "dve", ln_done[j - 2])
                ctx.w(e, "act", sq_act[j - 2])
            ld_x[j] = ctx.dma(e, xt[par][:, :],
                              x_in[128 * j:128 * (j + 1), :], f"d{par}")
        ctx.sec("sync", _ld)

        d = emit_ln(ctx, lnS, xt[par][:, :], lnt[par][:, :], [ld_x[j]],
                    norm_waits=([("pe", pe_tr[j - 2])] if j >= 2 else ()))
        ln_done[j] = d
        sq_act[j] = lnS.last_sq_act

        def _tr(e, j=j, par=par, d=d):
            ctx.w(e, "dve", d)
            if j >= 1:
                ctx.w(e, "dve", dve_lnx[j - 1])
            last = None
            for b in range(8):
                last = e.transpose(ptr[:, 128 * b:128 * (b + 1)],
                                   lnt[par][:, 128 * b:128 * (b + 1)],
                                   ident[:, :])
            ctx.inc(last, "pe")
        ctx.sec("tensor", _tr)
        pe_tr[j] = ctx.cnt["pe"]

        def _ev(e, j=j):
            ctx.w(e, "pe", pe_tr[j])
            dst = V(lnx2, 128 * j, [[8 * 2048, 128], [2048, 8], [1, 128]])
            e.tensor_copy(dst, ptr[:, :])
        ctx.sec("vector", _ev)
        dve_lnx[j] = ctx.cnt["dve"]
    lnx_done = dve_lnx[NT - 1]
    esA.close()

    esB = ExitStack()
    xi_g = [sal(ctx, esB, f"s_xi{i}", [128, 2048], F32) for i in range(2)]
    a_g = sal(ctx, esB, "s_a", [128, 2048], F32)
    t1 = sal(ctx, esB, "s_t1", [128, 2048], F32)
    wbuf = [sal(ctx, esB, f"s_wbuf{i}", [128, 8 * 128], BF16) for i in range(2)]
    pxi = [pal(ctx, esB, f"s_pxi{i}", [128, 2048], F32) for i in range(2)]
    ld_w = [None] * NG
    pe_xi = [0] * NG
    act_xi = [0] * NG
    dve_y = [0] * NG
    for g in range(NG):
        par = g % 2

        def _wl(e, g=g, par=par):
            if g >= 2:
                ctx.w(e, "pe", pe_xi[g - 2])
            else:
                ctx.w(e, "pe", getattr(ctx, "wgate", 0))
            src = V(W["Win"], 128 * g, [[DIN, 128], [128 * DIN, 8], [1, 128]])
            dst = wbuf[par].ap().rearrange("p (k c) -> p k c", k=8, c=128)
            ld_w[g] = ctx.dma(e, dst, src, f"d{6 + par}")
        ctx.sec("gpsimd", _wl)

        def _mm(e, g=g, par=par):
            if g == 0:
                ctx.w(e, "dve", lnx_done)
            ctx.wp(e, ld_w[g])
            if g >= 2:
                ctx.w(e, "act", act_xi[g - 2])
            last = None
            for ki in range(8):
                lhsT = wbuf[par][:, ki * 128:(ki + 1) * 128]
                for h in range(4):
                    last = e.matmul(pxi[g % 2][:, 512 * h:512 * (h + 1)], lhsT,
                                    V(lnx2, ki * 2048 + 512 * h,
                                      [[8 * 2048, 128], [1, 512]]),
                                    start=(ki == 0), stop=(ki == 7),
                                    skip_group_check=True)
            ctx.inc(last, "pe")
        ctx.sec("tensor", _mm)
        pe_xi[g] = ctx.cnt["pe"]

        def _xe(e, g=g):
            ctx.w(e, "pe", pe_xi[g])
            if g == 0:
                ctx.wp(e, pl_tot)
            if g >= 2:
                ctx.w(e, "dve", dve_y[g - 2])
            e.activation(out=xi_g[g % 2][:, :], in_=pxi[g % 2][:, :],
                         func=AF.Silu,
                         bias=cvec[:, BIN * 16 + g:BIN * 16 + g + 1])
        ctx.sec("scalar", _xe)
        act_xi[g] = ctx.cnt["act"]

        def _sy(e, g=g):
            ctx.w(e, "act", act_xi[g])
            d0 = V(p9, 0, [[1, 128], [0, 2048]])
            e.tensor_tensor_scan(a_g[:, :], d0, xi_g[g % 2][:, :], 0.0,
                                 ALU.mult, ALU.add)
            e.tensor_copy(u_buf[:, g:g + 1], a_g[:, 2047:2048])
            e.tensor_scalar(t1[:, :], xi_g[g % 2][:, :],
                            cvec[:, DP * 16 + g:DP * 16 + g + 1], None, ALU.mult)
            e.scalar_tensor_tensor(y_sb[:, 2048 * g:2048 * (g + 1)], a_g[:, :],
                                   gcol[:, g:g + 1], t1[:, :], ALU.mult, ALU.add)
        ctx.sec("vector", _sy)
        dve_y[g] = ctx.cnt["dve"]
    ctx.wgate = pe_xi[NG - 1]
    esB.close()
    esL.close()

    def _ust(e):
        ctx.w(e, "dve", dve_y[NG - 1])
        sv = ctx.dma(e, uag_in[:, :], u_buf[:, :], "d2")
        ctx.wp(e, sv)
        ctx.inc(e.nop(), "act")
    ctx.sec("scalar", _ust)
    act_ust = ctx.cnt["act"]

    def _ag(e):
        ctx.w(e, "act", act_ust)
        ctx.inc(e.collective_compute(
            "AllGather", ALU.bypass,
            replica_groups=[[0, 1, 2, 3], [4, 5, 6, 7]],
            ins=[uag_in[:, :]], outs=[uag_out[:, :]]), "cc")
        e.wait_ge(ctx.sems["cc"], ctx.cnt["cc"])
        src = V(uag_out, 0, [[16, 128], [128 * 16, 4], [1, 16]])
        sv = ctx.dma(e, uago.ap().rearrange("p (r j) -> p r j", r=4, j=16),
                     src, "d7")
        ctx.wp(e, sv)
        ctx.inc(e.engine_nop(), "pool")
    ctx.sec("gpsimd", _ag)
    pool_ag = ctx.cnt["pool"]

    def _sel(e):
        ctx.w(e, "pool", pool_ag)
        selv = V(cvec, SEL * 16, [[6 * 16, 128], [1, 4], [0, 16]])
        e.tensor_mul(uago[:, :], uago[:, :], selv)
        rin = V(uago, 0, [[4 * 16, 128], [1, 16], [16, 4]])
        e.reduce_sum(cvec[:, UPV * 16:UPV * 16 + 16], rin, AX.X)
        e.tensor_mul(cvec[:, UG * 16:UG * 16 + 16],
                     cvec[:, UPV * 16:UPV * 16 + 16], gcol[:, :])
        e.tensor_copy(ugb[:, :], cvec[:, UG * 16:UG * 16 + 16])
    ctx.sec("vector", _sel)
    dve_ug = ctx.cnt["dve"]

    esD = ExitStack()
    wo_sb = sal(ctx, esD, "s_wo", [128, NG * 1024], BF16)
    o_tm = sal(ctx, esD, "s_otm", [128, DM], F32)
    lnn = sal(ctx, esD, "s_lnn", [128, DM], F32)
    x3t = [sal(ctx, esD, f"s_x3{i}", [128, DM], F32) for i in range(2)]
    pso = [pal(ctx, esD, f"s_pso{i}", [128, DM], F32) for i in range(2)]
    pw1 = pal(ctx, esD, "s_pw1", [1, DM], F32)

    def _wol(e):
        ctx.w(e, "pe", ctx.wgate)
        src = V(W["Wout"], 0, [[DM, 128], [128 * DM, NG], [1, DM]])
        sv = ctx.dma(e, wo_sb.ap().rearrange("p (g c) -> p g c", g=NG, c=DM),
                     src, "d6")
        ld = sv
        ctx.pending_wo = ld
    ctx.sec("gpsimd", _wol)
    ld_wo = ctx.pending_wo

    def _wr(e):
        ctx.wp(e, ld_wo)
        ctx.w(e, "dve", dve_ug)
        last = None
        for g in range(NG):
            for h in range(2):
                last = e.matmul(pw1[0:1, 512 * h:512 * (h + 1)],
                                ugb[:, g:g + 1],
                                wo_sb[:, 1024 * g + 512 * h:1024 * g + 512 * (h + 1)],
                                start=(g == 0), stop=(g == NG - 1),
                                skip_group_check=True)
        ctx.inc(last, "pe")
    ctx.sec("tensor", _wr)
    pe_wr = ctx.cnt["pe"]

    def _wre(e):
        ctx.w(e, "pe", pe_wr)
        e.tensor_copy(wrow[0:1, :], pw1[0:1, :])
    ctx.sec("vector", _wre)
    dve_wre = ctx.cnt["dve"]

    def _wrb(e):
        ctx.w(e, "dve", dve_wre)
        last = None
        for h in range(2):
            last = e.matmul(pso[1][:, 512 * h:512 * (h + 1)], ones1[0:1, :],
                            wrow[0:1, 512 * h:512 * (h + 1)],
                            start=True, stop=True, skip_group_check=True)
        ctx.inc(last, "pe")
    ctx.sec("tensor", _wrb)
    pe_wrb = ctx.cnt["pe"]

    def _wbe(e):
        ctx.w(e, "pe", pe_wrb)
        e.tensor_copy(wb_sb[:, :], pso[1][:, :])
    ctx.sec("vector", _wbe)
    dve_wb = ctx.cnt["dve"]

    pe_sl = [0] * NT
    dve_out = [0] * NT
    ld_x2 = [None] * NT
    st_x3 = [None] * NT
    for sl in range(NT):
        par = sl % 2

        def _ld(e, sl=sl, par=par):
            if sl >= 2:
                ctx.w(e, "dve", dve_out[sl - 2])
            else:
                ctx.w(e, "dve", ln_done[NT - 2 + sl])
                ctx.w(e, "act", sq_act[NT - 2 + sl])
            ld_x2[sl] = ctx.dma(e, xt[par][:, :],
                                x_in[128 * sl:128 * (sl + 1), :], f"d{par}")
        ctx.sec("sync", _ld)

        def _mm(e, sl=sl):
            if sl <= 1:
                ctx.w(e, "dve", dve_wb)
            if sl >= 2:
                ctx.w(e, "dve", dve_out[sl - 2])
            last = None
            for g in range(NG):
                lhsT = y_sb[:, 2048 * g + 128 * sl:2048 * g + 128 * (sl + 1)]
                for h in range(2):
                    last = e.matmul(pso[sl % 2][:, 512 * h:512 * (h + 1)], lhsT,
                                    wo_sb[:, 1024 * g + 512 * h:1024 * g + 512 * (h + 1)],
                                    start=(g == 0), stop=False,
                                    skip_group_check=True)
            for h in range(2):
                last = e.matmul(pso[sl % 2][:, 512 * h:512 * (h + 1)], ones1[0:1, :],
                                boutr[0:1, 512 * h:512 * (h + 1)],
                                start=False, stop=(h == 1),
                                skip_group_check=True)
            ctx.inc(last, "pe")
        ctx.sec("tensor", _mm)
        pe_sl[sl] = ctx.cnt["pe"]

        def _oe(e, sl=sl):
            ctx.w(e, "pe", pe_sl[sl])
            ctx.w(e, "act", lnSo.last_sq_act)
            e.scalar_tensor_tensor(o_tm[:, :], wb_sb[:, :],
                                   cvec[:, DEC * 16 + sl:DEC * 16 + sl + 1],
                                   pso[sl % 2][:, :], ALU.mult, ALU.add)
        ctx.sec("vector", _oe)
        dve_oe = ctx.cnt["dve"]

        d = emit_ln(ctx, lnSo, o_tm[:, :], lnn[:, :],
                    [("dve", dve_oe), ld_x2[sl]],
                    res_add_ap=xt[par][:, :], dst2=x3t[par][:, :],
                    norm_waits=([st_x3[sl - 2]] if sl >= 2 else ()))
        dve_out[sl] = d

        def _st(e, sl=sl, par=par):
            ctx.w(e, "dve", dve_out[sl])
            st_x3[sl] = ctx.dma(e, x_out[128 * sl:128 * (sl + 1), :],
                                x3t[par][:, :], f"d{4 + par}")
        ctx.sec("scalar", _st)

    ctx.pending = st_x3[-2:]
    ctx.wgate = pe_sl[NT - 1]
    ctx.barrier(drains=st_x3[-2:])
    esD.close()
    es.close()


def emit_l1f(ctx, W, x_in, y_out, ident, ones1, xp_in, xp_out, c_dr, yt_in,
             yt_out):
    nc = ctx.nc
    drains = getattr(ctx, "pending", [])
    ctx.pending = []
    es = ExitStack()
    xt = [sal(ctx, es, f"m_xt{i}", [128, DM], F32) for i in range(2)]
    lnt = sal(ctx, es, "m_lnt", [128, DM], BF16)
    rS = sal(ctx, es, "m_rS", [128, TC], F32)
    phib = sal(ctx, es, "m_phib", [1, NP], BF16)
    phibc = sal(ctx, es, "m_phibc", [128, 16], F32)
    sel32 = sal(ctx, es, "m_sel32", [128, 32], BF16)
    rep32 = sal(ctx, es, "m_rep32", [32, 128], BF16)
    oav = sal(ctx, es, "m_oav", [128, 128], BF16)
    lnM = LNState(ctx, es, "m")
    lnMo = LNState(ctx, es, "mo")

    def _pl(e):
        ctx.dma(e, phib[:, :], W["phibrow"][:, :], "d7")
        ctx.dma(e, phibc[:, :], W["phibcm"][:, :], "d7")
        ctx.dma(e, sel32[:, :], W["sel32"][:, :], "d7")
        ctx.dma(e, rep32[:, :], W["rep32"][:, :], "d7")
        ctx.dma(e, oav[:, :], W["oav"][:, :], "d7")
        e.wait_ge(ctx.sems["d7"], ctx.cnt["d7"])
    ctx.sec("gpsimd", _pl)
    pl_tot = ("d7", ctx.cnt["d7"])

    es1 = ExitStack()
    phit = sal(ctx, es1, "m_phit", [128, 8 * NP], BF16)
    lnx3fm = sal(ctx, es1, "m_lnx3fm", [128, 8 * TC], BF16)

    def _pwl(e):
        ctx.w(e, "pe", getattr(ctx, "wgate", 0))
        src = V(W["phiT"], 0, [[NP, 128], [128 * NP, 8], [1, NP]])
        ctx.phit_ld = ctx.dma(
            e, phit.ap().rearrange("p (k c) -> p k c", k=8, c=NP), src, "d6")
    ctx.sec("gpsimd", _pwl)
    phit_ld = ctx.phit_ld

    es2 = ExitStack()
    lnx3tm = sal(ctx, es2, "m_lnx3tm", [128, NT * DM], BF16)
    esA = ExitStack()
    ptr = pal(ctx, esA, "m_ptr", [128, DM], BF16)
    ln_done = [0] * NT
    sq_act = [0] * NT
    pe_tr = [0] * NT
    dve_fm = [0] * NT
    ld_x = [None] * NT
    for j in range(NT):
        par = j % 2

        def _ld(e, j=j, par=par):
            if j == 0:
                for sv in drains:
                    ctx.wp(e, sv)
            if j >= 2:
                ctx.w(e, "dve", ln_done[j - 2])
                ctx.w(e, "act", sq_act[j - 2])
            ld_x[j] = ctx.dma(e, xt[par][:, :],
                              x_in[128 * j:128 * (j + 1), :], f"d{par}")
        ctx.sec("sync", _ld)

        d = emit_ln(ctx, lnM, xt[par][:, :], lnx3tm[:, DM * j:DM * (j + 1)],
                    [ld_x[j]], norm_waits=([("pe", pe_tr[j - 2])] if j >= 2 else ()))
        ln_done[j] = d
        sq_act[j] = lnM.last_sq_act

        def _tr(e, j=j, d=d):
            ctx.w(e, "dve", d)
            if j >= 1:
                ctx.w(e, "dve", dve_fm[j - 1])
            last = None
            for b in range(8):
                last = e.transpose(ptr[:, 128 * b:128 * (b + 1)],
                                   lnx3tm[:, DM * j + 128 * b:DM * j + 128 * (b + 1)],
                                   ident[:, :])
            ctx.inc(last, "pe")
        ctx.sec("tensor", _tr)
        pe_tr[j] = ctx.cnt["pe"]

        def _ev(e, j=j):
            ctx.w(e, "pe", pe_tr[j])
            dst = V(lnx3fm, 128 * j, [[8 * TC, 128], [TC, 8], [1, 128]])
            e.tensor_copy(dst, ptr[:, :])
        ctx.sec("vector", _ev)
        dve_fm[j] = ctx.cnt["dve"]
    lnx_done = dve_fm[NT - 1]
    esA.close()

    es3 = ExitStack()
    D_sb = sal(ctx, es3, "m_D", [128, NT * NP], BF16)
    e_d = sal(ctx, es3, "m_ed", [128, NP], BF16)
    sd = sal(ctx, es3, "m_sd", [128, 2 * NEXP], F32)
    plog = pal(ctx, es3, "m_plog", [128, NP], F32)
    pe_lg = [0] * NT
    act_ex = [0] * NT
    dve_D = [0] * NT
    for I in range(NT):
        def _mm(e, I=I):
            if I == 0:
                ctx.w(e, "dve", lnx_done)
                ctx.wp(e, phit_ld)
                ctx.wp(e, pl_tot)
            if I >= 1:
                ctx.w(e, "act", act_ex[I - 1])
            last = None
            for ct in range(8):
                lhsT = V(lnx3fm, ct * TC + 128 * I, [[8 * TC, 128], [1, 128]])
                for h in range(4):
                    last = e.matmul(plog[:, 512 * h:512 * (h + 1)], lhsT,
                                    phit[:, ct * NP + 512 * h:ct * NP + 512 * (h + 1)],
                                    start=(ct == 0), stop=False,
                                    skip_group_check=True)
            for h in range(4):
                last = e.matmul(plog[:, 512 * h:512 * (h + 1)], ones1[0:1, :],
                                phib[0:1, 512 * h:512 * (h + 1)],
                                start=False, stop=(h == 3), skip_group_check=True)
            ctx.inc(last, "pe")
        ctx.sec("tensor", _mm)
        pe_lg[I] = ctx.cnt["pe"]

        def _ex(e, I=I):
            ctx.w(e, "pe", pe_lg[I])
            if I >= 1:
                ctx.w(e, "dve", dve_D[I - 1])
            e.activation(out=e_d[:, :], in_=plog[:, :], func=AF.Exp)
        ctx.sec("scalar", _ex)
        act_ex[I] = ctx.cnt["act"]

        def _D(e, I=I):
            ctx.w(e, "act", act_ex[I])
            rin = V(e_d, 0, [[NP, 128], [PSLOT, NEXP], [1, PSLOT]])
            e.reduce_sum(sd[:, 0:NEXP], rin, AX.X)
            e.reciprocal(sd[:, NEXP:2 * NEXP], sd[:, 0:NEXP])
            rv = V(sd, NEXP, [[2 * NEXP, 128], [1, NEXP], [0, PSLOT]])
            e.tensor_mul(D_sb[:, NP * I:NP * (I + 1)], e_d[:, :], rv)
        ctx.sec("vector", _D)
        dve_D[I] = ctx.cnt["dve"]

    es4 = ExitStack()
    xps = sal(ctx, es4, "m_xps", [128, DM], F32)
    pxp = [pal(ctx, es4, f"m_pxp{i}", [128, DM], F32) for i in range(2)]
    pe_xp = [0] * 16
    dve_xp = [0] * 16
    st_xp = [None] * 16
    for sn in range(16):
        def _mm(e, sn=sn):
            if sn == 0:
                ctx.w(e, "dve", dve_D[NT - 1])
            if sn >= 2:
                ctx.w(e, "dve", dve_xp[sn - 2])
            last = None
            for I in range(NT):
                lhsT = D_sb[:, NP * I + 128 * sn:NP * I + 128 * (sn + 1)]
                for h in range(2):
                    last = e.matmul(pxp[sn % 2][:, 512 * h:512 * (h + 1)], lhsT,
                                    lnx3tm[:, DM * I + 512 * h:DM * I + 512 * (h + 1)],
                                    start=(I == 0), stop=(I == NT - 1),
                                    skip_group_check=True)
            ctx.inc(last, "pe")
        ctx.sec("tensor", _mm)
        pe_xp[sn] = ctx.cnt["pe"]

        def _ev(e, sn=sn):
            ctx.w(e, "pe", pe_xp[sn])
            if sn >= 1:
                ctx.wp(e, st_xp[sn - 1])
            e.tensor_copy(xps[:, :], pxp[sn % 2][:, :])
        ctx.sec("vector", _ev)
        dve_xp[sn] = ctx.cnt["dve"]

        def _st(e, sn=sn):
            ctx.w(e, "dve", dve_xp[sn])
            st_xp[sn] = ctx.dma(e, xp_in[128 * sn:128 * (sn + 1), :],
                                xps[:, :], f"d{2 + sn % 2}")
        ctx.sec("scalar", _st)
    ctx.wgate = pe_xp[15]

    def _rs(e):
        for sv in st_xp[-2:]:
            ctx.wp(e, sv)
        ctx.inc(e.collective_compute(
            "ReduceScatter", ALU.add, replica_groups=[[0, 1, 2, 3, 4, 5, 6, 7]],
            ins=[xp_in[:, :]], outs=[xp_out[:, :]]), "cc")
    ctx.sec("gpsimd", _rs)
    cc_rs = ctx.cnt["cc"]
    es4.close()
    es3.close()
    es2.close()

    es5 = ExitStack()
    e_fm = sal(ctx, es5, "m_efm", [128, TC], F32)
    e_bf = [sal(ctx, es5, f"m_ebf{i}", [128, TC], BF16) for i in range(2)]
    sC = sal(ctx, es5, "m_sC", [32, TC], BF16)
    pfm = pal(ctx, es5, "m_pfm", [128, TC], F32)
    psC = pal(ctx, es5, "m_psC", [32, TC], F32)
    pe_fm = [0] * 16
    act_fx = [0] * 16
    dve_eb = [0] * 16
    pe_sC = [0] * 16
    st_eb = [None] * 16
    for sn in range(16):
        par = sn % 2

        def _mm(e, sn=sn):
            if sn >= 1:
                ctx.w(e, "act", act_fx[sn - 1])
            last = None
            for ct in range(8):
                lhsT = V(phit, ct * NP + 128 * sn, [[8 * NP, 128], [1, 128]])
                for h in range(4):
                    last = e.matmul(pfm[:, 512 * h:512 * (h + 1)], lhsT,
                                    V(lnx3fm, ct * TC + 512 * h,
                                      [[8 * TC, 128], [1, 512]]),
                                    start=(ct == 0), stop=(ct == 7),
                                    skip_group_check=True)
            ctx.inc(last, "pe")
        ctx.sec("tensor", _mm)
        pe_fm[sn] = ctx.cnt["pe"]

        def _ex(e, sn=sn):
            ctx.w(e, "pe", pe_fm[sn])
            if sn >= 1:
                ctx.w(e, "dve", dve_eb[sn - 1])
            e.activation(out=e_fm[:, :], in_=pfm[:, :], func=AF.Exp,
                         bias=phibc[:, sn:sn + 1])
        ctx.sec("scalar", _ex)
        act_fx[sn] = ctx.cnt["act"]

        def _eb(e, sn=sn, par=par):
            ctx.w(e, "act", act_fx[sn])
            if sn >= 2:
                ctx.wp(e, st_eb[sn - 2])
                ctx.w(e, "pe", pe_sC[sn - 2])
            e.tensor_copy(e_bf[par][:, :], e_fm[:, :])
        ctx.sec("vector", _eb)
        dve_eb[sn] = ctx.cnt["dve"]

        def _st(e, sn=sn, par=par):
            ctx.w(e, "dve", dve_eb[sn])
            if sn < 2:
                ctx.wp(e, st_xp[14 + sn])
            st_eb[sn] = ctx.dma(e, c_dr[128 * sn:128 * (sn + 1), :],
                                e_bf[par][:, :], f"d{2 + par}")
        ctx.sec("scalar", _st)

        def _sc(e, sn=sn, par=par):
            ctx.w(e, "dve", dve_eb[sn])
            last = None
            for h in range(4):
                last = e.matmul(psC[0:32, 512 * h:512 * (h + 1)], sel32[:, :],
                                e_bf[par][:, 512 * h:512 * (h + 1)],
                                start=(sn == 0), stop=(sn == 15),
                                skip_group_check=True)
            ctx.inc(last, "pe")
        ctx.sec("tensor", _sc)
        pe_sC[sn] = ctx.cnt["pe"]

    def _sce(e):
        ctx.w(e, "pe", pe_sC[15])
        e.tensor_copy(sC[0:32, :], psC[0:32, :])
    ctx.sec("vector", _sce)
    dve_sC = ctx.cnt["dve"]

    def _scb(e):
        ctx.w(e, "dve", dve_sC)
        last = None
        for h in range(4):
            last = e.matmul(pfm[:, 512 * h:512 * (h + 1)], rep32[0:32, :],
                            sC[0:32, 512 * h:512 * (h + 1)],
                            start=True, stop=True, skip_group_check=True)
        ctx.inc(last, "pe")
    ctx.sec("tensor", _scb)
    pe_scb = ctx.cnt["pe"]

    def _rse(e):
        ctx.w(e, "pe", pe_scb)
        e.reciprocal(rS[:, :], pfm[:, :])
    ctx.sec("vector", _rse)
    dve_rS = ctx.cnt["dve"]
    ctx.barrier()
    es5.close()
    es1.close()

    es6 = ExitStack()
    xrs = sal(ctx, es6, "m_xrs", [128, DM], F32)
    xcl = sal(ctx, es6, "m_xcl", [128, DM], F32)
    xln = sal(ctx, es6, "m_xln", [128, DM], F32)
    xtfm = sal(ctx, es6, "m_xtfm", [128, 8 * 256], BF16)
    xth = sal(ctx, es6, "m_xth", [128, DM], BF16)
    ptr2 = pal(ctx, es6, "m_ptr2", [128, DM], BF16)
    pe_trx = [0] * 2
    dcl_h = [0] * 2
    for half in range(2):
        def _ld(e, half=half):
            if half == 0:
                e.wait_ge(ctx.sems["cc"], cc_rs)
            else:
                ctx.w(e, "dve", dcl_h[0])
            ld = ctx.dma(e, xrs[:, :], xp_out[128 * half:128 * (half + 1), :],
                         f"d{half}")
            ctx.ld_xrs = ld
        ctx.sec("sync", _ld)
        ld_xrs = ctx.ld_xrs

        def _cl(e, half=half, ld=ld_xrs):
            ctx.wp(e, ld)
            if half == 1:
                ctx.w(e, "pe", pe_trx[0])
                ctx.w(e, "act", lnM.last_sq_act)
            e.tensor_scalar(xcl[:, :], xrs[:, :], -33000.0, 65000.0,
                            ALU.max, ALU.min)
        ctx.sec("vector", _cl)
        d_cl = ctx.cnt["dve"]
        dcl_h[half] = d_cl

        d = emit_ln(ctx, lnM, xcl[:, :], xln[:, :], [("dve", d_cl)])

        def _th(e, half=half, d=d):
            ctx.w(e, "dve", d)
            e.activation(out=xth[:, :], in_=xln[:, :], func=AF.Tanh)
        ctx.sec("scalar", _th)
        a_th = ctx.cnt["act"]

        def _tr(e, half=half, a=a_th):
            ctx.w(e, "act", a)
            last = None
            for b in range(8):
                last = e.transpose(ptr2[:, 128 * b:128 * (b + 1)],
                                   xth[:, 128 * b:128 * (b + 1)], ident[:, :])
            ctx.inc(last, "pe")
        ctx.sec("tensor", _tr)
        pe_trx[half] = ctx.cnt["pe"]

        def _ev(e, half=half):
            ctx.w(e, "pe", pe_trx[half])
            dst = V(xtfm, 128 * half, [[8 * 256, 128], [256, 8], [1, 128]])
            e.tensor_copy(dst, ptr2[:, :])
        ctx.sec("vector", _ev)
    dve_xt = ctx.cnt["dve"]

    ebias = sal(ctx, es6, "m_ebias", [1, 8 * DM], BF16)
    wex = [sal(ctx, es6, f"m_wex{i}", [128, 8 * DM], BF16) for i in range(2)]
    yrow = sal(ctx, es6, "m_yrow", [32, DM], BF16)
    pY = pal(ctx, es6, "m_pY", [32, DM], F32)

    def _ebl(e):
        ctx.dma(e, ebias[:, :], W["eb_shard"][:, :], "d7")
        e.wait_ge(ctx.sems["d7"], ctx.cnt["d7"])
    ctx.sec("gpsimd", _ebl)
    eb_tot = ("d7", ctx.cnt["d7"])

    ld_we = [None] * 8
    pe_Y = [0] * 8
    dve_Y = [0] * 8
    st_Y = [None] * 8
    for eL in range(8):
        par = eL % 2

        def _wl(e, eL=eL, par=par):
            if eL >= 2:
                ctx.w(e, "pe", pe_Y[eL - 2])
            else:
                ctx.w(e, "pe", ctx.wgate)
            src = V(W["ew_shard"], eL * DM * DM,
                    [[DM, 128], [128 * DM, 8], [1, DM]])
            ld_we[eL] = ctx.dma(
                e, wex[par].ap().rearrange("p (k c) -> p k c", k=8, c=DM),
                src, f"d{6 + par}")
        ctx.sec("gpsimd", _wl)

        def _mm(e, eL=eL, par=par):
            if eL == 0:
                ctx.w(e, "dve", dve_xt)
                ctx.wp(e, eb_tot)
            ctx.wp(e, ld_we[eL])
            if eL >= 1:
                ctx.w(e, "dve", dve_Y[eL - 1])
            last = None
            for ds in range(8):
                lhsT = V(xtfm, ds * 256 + 32 * eL, [[8 * 256, 128], [1, 32]])
                for h in range(2):
                    last = e.matmul(pY[0:32, 512 * h:512 * (h + 1)], lhsT,
                                    wex[par][:, ds * DM + 512 * h:ds * DM + 512 * (h + 1)],
                                    start=(ds == 0), stop=False,
                                    skip_group_check=True)
            for h in range(2):
                last = e.matmul(pY[0:32, 512 * h:512 * (h + 1)], ones1[0:1, 0:32],
                                ebias[0:1, eL * DM + 512 * h:eL * DM + 512 * (h + 1)],
                                start=False, stop=(h == 1), skip_group_check=True)
            ctx.inc(last, "pe")
        ctx.sec("tensor", _mm)
        pe_Y[eL] = ctx.cnt["pe"]

        def _ev(e, eL=eL):
            ctx.w(e, "pe", pe_Y[eL])
            if eL >= 1:
                ctx.wp(e, st_Y[eL - 1])
            e.tensor_copy(yrow[0:32, :], pY[0:32, :])
        ctx.sec("vector", _ev)
        dve_Y[eL] = ctx.cnt["dve"]

        def _st(e, eL=eL):
            ctx.w(e, "dve", dve_Y[eL])
            if eL < 2:
                ctx.wp(e, st_eb[14 + eL])
            st_Y[eL] = ctx.dma(e, yt_in[32 * eL:32 * (eL + 1), :],
                               yrow[0:32, :], f"d{2 + eL % 2}")
        ctx.sec("scalar", _st)

    def _ag(e):
        for sv in st_Y[-2:]:
            ctx.wp(e, sv)
        ctx.inc(e.collective_compute(
            "AllGather", ALU.bypass, replica_groups=[[0, 1, 2, 3, 4, 5, 6, 7]],
            ins=[yt_in[:, :]], outs=[yt_out[:, :]]), "cc")
    ctx.sec("gpsimd", _ag)
    cc_ag = ctx.cnt["cc"]
    es6.close()

    es7 = ExitStack()
    C_all = sal(ctx, es7, "m_C", [128, 16 * TC], BF16)
    ytk = sal(ctx, es7, "m_ytk", [128, NT * DM], BF16)
    ceb = [sal(ctx, es7, f"m_ceb{i}", [128, TC], BF16) for i in range(2)]
    o_tm = sal(ctx, es7, "m_otm", [128, DM], F32)
    lnn = sal(ctx, es7, "m_lnn", [128, DM], F32)
    x4t = [sal(ctx, es7, f"m_x4{i}", [128, DM], F32) for i in range(2)]
    pY2 = [pal(ctx, es7, f"m_pY2{i}", [128, DM], F32) for i in range(2)]

    def _ytl(e):
        e.wait_ge(ctx.sems["cc"], cc_ag)
        src = V(yt_out, 0, [[DM, 128], [128 * DM, NT], [1, DM]])
        ctx.ld_ytk = ctx.dma(
            e, ytk.ap().rearrange("p (g c) -> p g c", g=NT, c=DM), src, "d6")
    ctx.sec("gpsimd", _ytl)
    ld_ytk = ctx.ld_ytk

    ld_eb2 = [None] * 16
    dve_C = [0] * 16
    for sn in range(16):
        par = sn % 2

        def _ld(e, sn=sn, par=par):
            if sn >= 2:
                ctx.w(e, "dve", dve_C[sn - 2])
            else:
                ctx.w(e, "dve", dcl_h[1])
                ctx.wp(e, st_eb[14 + sn])
            ld_eb2[sn] = ctx.dma(e, ceb[par][:, :],
                                 c_dr[128 * sn:128 * (sn + 1), :], f"d{par}")
        ctx.sec("sync", _ld)

        def _C(e, sn=sn, par=par):
            ctx.wp(e, ld_eb2[sn])
            if sn == 0:
                ctx.w(e, "dve", dve_rS)
            e.tensor_mul(C_all[:, TC * sn:TC * (sn + 1)], ceb[par][:, :],
                         rS[:, :])
        ctx.sec("vector", _C)
        dve_C[sn] = ctx.cnt["dve"]

    pe_Y2 = [0] * NT
    dve_o = [0] * NT
    ld_x3 = [None] * NT
    st_x4 = [None] * NT
    for sl in range(NT):
        par = sl % 2

        def _ld(e, sl=sl, par=par):
            if sl >= 2:
                ctx.w(e, "dve", dve_o[sl - 2])
            else:
                ctx.w(e, "dve", ln_done[NT - 2 + sl])
                ctx.w(e, "act", sq_act[NT - 2 + sl])
                e.wait_ge(ctx.sems["cc"], cc_ag)
            ld_x3[sl] = ctx.dma(e, xt[par][:, :],
                                x_in[128 * sl:128 * (sl + 1), :], f"d{2 + par}")
        ctx.sec("sync", _ld)

        def _mm(e, sl=sl):
            if sl == 0:
                ctx.w(e, "dve", dve_C[15])
                ctx.wp(e, ld_ytk)
            if sl >= 2:
                ctx.w(e, "dve", dve_o[sl - 2])
            last = None
            for sn in range(16):
                lhsT = C_all[:, TC * sn + 128 * sl:TC * sn + 128 * (sl + 1)]
                for h in range(2):
                    last = e.matmul(pY2[sl % 2][:, 512 * h:512 * (h + 1)], lhsT,
                                    ytk[:, DM * sn + 512 * h:DM * sn + 512 * (h + 1)],
                                    start=(sn == 0), stop=(sn == 15),
                                    skip_group_check=True)
            ctx.inc(last, "pe")
        ctx.sec("tensor", _mm)
        pe_Y2[sl] = ctx.cnt["pe"]

        def _oe(e, sl=sl):
            ctx.w(e, "pe", pe_Y2[sl])
            ctx.w(e, "act", lnMo.last_sq_act)
            e.tensor_copy(o_tm[:, :], pY2[sl % 2][:, :])
        ctx.sec("vector", _oe)
        dve_oe = ctx.cnt["dve"]

        d = emit_ln(ctx, lnMo, o_tm[:, :], lnn[:, :],
                    [("dve", dve_oe), ld_x3[sl]],
                    res_add_ap=xt[par][:, :], dst2=x4t[par][:, :],
                    norm_waits=([st_x4[sl - 2]] if sl >= 2 else ()))
        dve_o[sl] = d

        def _st(e, sl=sl, par=par):
            ctx.w(e, "dve", dve_o[sl])
            st_x4[sl] = ctx.dma(e, y_out[128 * sl:128 * (sl + 1), :],
                                x4t[par][:, :], f"d{4 + par}")
        ctx.sec("scalar", _st)

    ctx.barrier(drains=st_x4[-2:])
    es7.close()
    es.close()


from concourse.bass_utils import run_bass_kernel_spmd

_CACHE = {}


def kernel(**inputs):
    sh, pc = prep_inputs(inputs)
    if "nc" not in _CACHE:
        _CACHE["nc"] = build(("l0a", "l0f", "l1a", "l1f"))
    nc = _CACHE["nc"]
    declared = set(nc._mk_W.keys()) | {"x_ext"}
    in_maps = [{k: v for k, v in dict(sh, **pc[c]).items() if k in declared}
               for c in range(NCORES)]
    res = run_bass_kernel_spmd(nc, in_maps, core_ids=list(range(NCORES)))
    out = np.concatenate([np.asarray(res.results[c]["y_out"])
                          for c in range(NCORES)], axis=0)
    return out.reshape(2, 8192, DM).astype(np.float32)


# version 18 (final)
# speedup vs baseline: 1.0193x; 1.0052x over previous
from contextlib import ExitStack
import numpy as np
import ml_dtypes
import concourse.bass as bass
import concourse.mybir as mybir

F32 = mybir.dt.float32
BF16 = mybir.dt.bfloat16
AF = mybir.ActivationFunctionType
ALU = mybir.AluOpType
AX = mybir.AxisListType
BF = ml_dtypes.bfloat16

NCORES = 8
TC = 2048
HALO = 4096
TE = TC + HALO
NT = TC // 128
NTH = TE // 128
DM = 1024
NH, HD = 16, 64
BANDS = [0, 1, 2, 3, 4, 8, 16, 32, 64, 128, 256, 512, 1024, 2048, 4096]
NBANDS = 15
BGROUPS = [(0, [0, 1, 2, 3, 4]), (5, [8, 16]), (7, [32, 64]),
           (9, [128, 256]), (11, [512, 1024]), (13, [2048, 4096])]
PV_CUM = []
_c = 0
for _gi, (_pos, _ds) in enumerate(BGROUPS):
    _c += (1 + 2 * (len(_ds) - 1)) if _gi == 0 else 2 * len(_ds)
    PV_CUM.append(_c)
PV_OPS_I = PV_CUM[-1] + 1

DECAY = 0.9
LN_EPS = 1e-5
NEXP, PSLOT = 64, 32
NP = NEXP * PSLOT
DIN = 2 * DM
DH = 4096


def prep_inputs(inputs):
    f = lambda a: np.ascontiguousarray(np.asarray(a), dtype=np.float32)
    sh = {}
    x = f(inputs["L0_attn_x"]).reshape(2 * 8192, DM)
    bias_cat = []
    for p in ("q", "k", "v", "o"):
        Wv = f(inputs[f"L0_attn_{p}_Wv"])
        Wr = f(inputs[f"L0_attn_{p}_Wr"])
        Wl = f(inputs[f"L0_attn_{p}_Wl"])
        sh[f"u_{p}"] = np.concatenate([Wv, Wr, Wl], 0).reshape(22 * 128, 128)
        bias_cat.append(f(inputs[f"L0_attn_{p}_b"]))
    sh["ubias"] = np.concatenate(bias_cat).reshape(1, 4 * DM)
    sh["w1T"] = f(inputs["L0_ffn_w1_w"].T)
    sh["w2T"] = f(inputs["L0_ffn_w2_w"].T)
    sh["h1T"] = f(inputs["L0_ffn_h1_w"].T)
    sh["h2T"] = f(inputs["L0_ffn_h2_w"].T)
    sh["w3T"] = f(inputs["L0_ffn_w3_w"].T)
    cm = lambda s: f(np.asarray(s).reshape(-1, 128).T)
    sh["s1cm"] = cm(inputs["L0_ffn_w1_s"])
    sh["s2cm"] = cm(inputs["L0_ffn_w2_s"])
    sh["h1scm"] = cm(inputs["L0_ffn_h1_s"])
    sh["h2scm"] = cm(inputs["L0_ffn_h2_s"])
    sh["s3row"] = f(inputs["L0_ffn_w3_s"]).reshape(1, DM)
    sh["s3cm"] = cm(inputs["L0_ffn_w3_s"])
    sh["Win"] = f(inputs["L1_attn_Win"])
    sh["bincm"] = cm(inputs["L1_attn_bin_"])
    sh["Bp"] = f(inputs["L1_attn_Bp"])
    sh["CpT"] = f(inputs["L1_attn_Cp"].T)
    sh["Dpcm"] = cm(inputs["L1_attn_Dp"])
    sh["Wout"] = f(inputs["L1_attn_Wout"])
    sh["boutrow"] = f(inputs["L1_attn_bout"]).reshape(1, DM)
    sh["phiT"] = f(inputs["L1_ffn_phi_w"].T)
    sh["phibcm"] = cm(inputs["L1_ffn_phi_b"])
    sh["phibrow"] = f(inputs["L1_ffn_phi_b"]).reshape(1, NP)
    ew = f(inputs["L1_ffn_expert_w"])
    eb = f(inputs["L1_ffn_expert_b"])
    sh["ident"] = np.eye(128, dtype=BF)
    sh["ones1x128"] = np.ones((1, 128), dtype=BF)
    sh["oav"] = np.full((128, 128), 1.0 / DM, dtype=BF)
    sh["sel32"] = (np.arange(128)[:, None] % 32 == np.arange(32)[None, :]).astype(BF)
    sh["rep32"] = (np.arange(32)[:, None] == np.arange(128)[None, :] % 32).astype(BF)
    t_loc = np.arange(TC)
    sh["decaycm"] = (DECAY ** (t_loc + 1.0)).astype(np.float32).reshape(NT, 128).T

    per_core = []
    for c in range(NCORES):
        d = {}
        r0 = c * TC
        xe = np.zeros((TE, DM), dtype=np.float32)
        lo = r0 - HALO
        b0 = (c // 4) * 8192
        lo_c = max(lo, b0)
        xe[lo_c - lo:] = x[lo_c:r0 + TC]
        d["x_ext"] = xe
        tb = (c % 4) * TC + np.arange(TC)
        m = np.where(tb[:, None] - np.asarray(BANDS)[None, :] >= 0, 0.0, -1e30)
        d["masks"] = np.ascontiguousarray(m.astype(np.float32).reshape(NT, 128, NBANDS))
        selr = np.zeros((128, 4), dtype=np.float32)
        if c % 4 > 0:
            selr[:, (c % 4) - 1] = 1.0
        d["selread"] = selr
        d["ew_shard"] = np.ascontiguousarray(ew[8 * c:8 * c + 8])
        d["eb_shard"] = np.ascontiguousarray(eb[8 * c:8 * c + 8].reshape(1, 8 * DM))
        per_core.append(d)
    return sh, per_core


def V(handle_or_ap, off, dims):
    a = handle_or_ap if isinstance(handle_or_ap, bass.AP) else handle_or_ap.ap()
    return bass.AP(a.tensor, off, [list(d) for d in dims])


CHAINED = {
    "tensor_copy", "tensor_tensor", "tensor_add", "tensor_sub", "tensor_mul",
    "tensor_max", "tensor_scalar", "tensor_single_scalar", "tensor_scalar_add",
    "tensor_scalar_sub", "tensor_scalar_mul", "tensor_scalar_max",
    "tensor_scalar_min", "tensor_relu", "scalar_tensor_tensor", "tensor_reduce",
    "reduce_sum", "reduce_max", "reciprocal", "memset", "memzero",
    "tensor_tensor_scan", "select", "transpose", "bn_stats", "bn_aggr",
    "activation", "copy", "mul", "add", "square", "sqrt", "sign", "iota",
    "affine_select",
}
ENG_SEM = {"vector": "dve", "scalar": "act", "gpsimd": "pool"}


class ChainEng:
    def __init__(self, ctx, eng, sem):
        self._ctx, self._eng, self._sem = ctx, eng, sem

    def __getattr__(self, name):
        f = getattr(self._eng, name)
        if name not in CHAINED:
            return f
        ctx, eng, sem = self._ctx, self._eng, self._sem

        def wrap(*a, **k):
            if ctx.cnt[sem] > 0:
                eng.wait_ge(ctx.sems[sem], ctx.cnt[sem])
            inst = f(*a, **k)
            ctx.inc(inst, sem)
            return inst
        return wrap


class Ctx:
    def __init__(self, nc, block, sems):
        self.nc, self.block, self.sems = nc, block, sems
        self.cnt = {k: 0 for k in sems}
        self.nbar = 0

    def inc(self, inst, s, n=1):
        self.cnt[s] += n
        inst.then_inc(self.sems[s], n)
        return self.cnt[s]

    def w(self, eng, s, val):
        if val > 0:
            eng.wait_ge(self.sems[s], val)

    def wp(self, eng, sv):
        if sv is not None:
            s, v = sv
            if v > 0:
                eng.wait_ge(self.sems[s], v)

    def dma(self, eng, out, in_, sem):
        inst = eng.dma_start(out=out, in_=in_)
        return (sem, self.inc(inst, sem, 16))

    def sec(self, name, fn):
        if name in ENG_SEM:
            sem = ENG_SEM[name]
            getattr(self.block, name)(lambda e: fn(ChainEng(self, e, sem)))
        else:
            getattr(self.block, name)(fn)

    def barrier(self, drains=()):
        self.nbar += 1
        target = 5 * self.nbar
        for name, own in (("sync", None), ("tensor", "pe"), ("vector", "dve"),
                          ("scalar", "act"), ("gpsimd", "pool")):
            def _f(eng, t=target, own=own, first=(name == "sync")):
                if first:
                    for sv in drains:
                        self.wp(eng, sv)
                if own is not None and self.cnt[own] > 0:
                    eng.wait_ge(self.sems[own], self.cnt[own])
                eng.nop().then_inc(self.sems["bar"], 1)
                eng.wait_ge(self.sems["bar"], t)
            getattr(self.block, name)(_f)
        self.cnt["bar"] = target


def sal(ctx, es, name, shape, dt):
    free_bytes = int(np.prod(shape[1:])) * (2 if dt == BF16 else 4)
    ab = 2048 if free_bytes >= 8192 else 32
    return es.enter_context(ctx.nc.sbuf_tensor(name, shape, dt, align_bytes=ab))


def pal(ctx, es, name, shape, dt):
    return es.enter_context(ctx.nc.psum_tensor(name, shape, dt))


class LNState:
    def __init__(self, ctx, es, tag):
        self.scr = sal(ctx, es, f"lnscr_{tag}", [128, 1024], F32)
        self.stat = sal(ctx, es, f"lnstat_{tag}", [128, 8], F32)
        self.last_var_dve = 0
        self.last_sq_act = 0


def emit_ln(ctx, ln, src_ap, dst_ap, ready, res_add_ap=None, dst2=None,
            norm_waits=()):
    st = ln.stat
    inv = 1.0 / DM

    def _act1(e, ready=tuple(ready), war=ln.last_var_dve):
        for s_, v_ in ready:
            ctx.w(e, s_, v_)
        ctx.w(e, "dve", war)
        e.activation(out=ln.scr[:, :], in_=src_ap, func=AF.Square,
                     accum_out=st[:, 1:2])
    ctx.sec("scalar", _act1)
    ln.last_sq_act = a_sq = ctx.cnt["act"]

    def _vec1(e, ready=tuple(ready), a=a_sq):
        for s_, v_ in ready:
            ctx.w(e, s_, v_)
        e.reduce_sum(st[:, 0:1], src_ap, AX.X)
        e.tensor_scalar(st[:, 2:3], st[:, 0:1], inv, None, ALU.mult)
        ctx.w(e, "act", a)
        e.tensor_mul(st[:, 3:4], st[:, 2:3], st[:, 2:3])
        e.scalar_tensor_tensor(st[:, 4:5], st[:, 1:2], inv, st[:, 3:4],
                               ALU.mult, ALU.subtract)
    ctx.sec("vector", _vec1)
    d_var = ctx.cnt["dve"]
    ln.last_var_dve = d_var

    def _act2(e, d=d_var):
        ctx.w(e, "dve", d)
        e.activation(out=st[:, 6:7], in_=st[:, 4:5], func=AF.Sqrt,
                     bias=ctx.eps[:, 0:1])
    ctx.sec("scalar", _act2)
    a_rstd = ctx.cnt["act"]

    def _vec2(e, a=a_rstd, nw=tuple(norm_waits)):
        ctx.w(e, "act", a)
        for s_, v_ in nw:
            ctx.w(e, s_, v_)
        e.reciprocal(st[:, 5:6], st[:, 6:7])
        e.tensor_scalar(dst_ap, src_ap, st[:, 2:3], st[:, 5:6],
                        ALU.subtract, ALU.mult)
        if res_add_ap is not None:
            e.tensor_add(dst2, dst_ap, res_add_ap)
    ctx.sec("vector", _vec2)
    return ctx.cnt["dve"]


def build(phases=("l0a", "l0f", "l1a", "l1f")):
    nc = bass.Bass("TRN2", target_bir_lowering=False)

    x_ext = nc.dram_tensor("x_ext", [TE, DM], F32, kind="ExternalInput").ap()
    y_out = nc.dram_tensor("y_out", [TC, DM], F32, kind="ExternalOutput").ap()
    xdr = [nc.dram_tensor(f"xdr{i}", [TC, DM], F32).ap() for i in range(2)]
    kv_dr = nc.dram_tensor("kv_dr", [TE, 2 * DM], BF16).ap()
    uag_in = nc.dram_tensor("uag_in", [128, 16], F32).ap()
    uag_out = nc.dram_tensor("uag_out", [512, 16], F32).ap()
    xp_in = nc.dram_tensor("xp_in", [NP, DM], F32).ap()
    xp_out = nc.dram_tensor("xp_out", [NP // NCORES, DM], F32).ap()
    c_dr = nc.dram_tensor("c_dr", [NP, TC], BF16).ap()
    yt_in = nc.dram_tensor("yt_in", [8 * PSLOT, DM], BF16).ap()
    yt_out = nc.dram_tensor("yt_out", [NP, DM], BF16, addr_space="Shared" if "l1f" in phases else "Local").ap()

    WSPECS = {"ubias": ([1, 4 * DM], F32), "w1T": ([DM, DH], F32),
              "w2T": ([DM, DH], F32), "h1T": ([DH, DH], F32),
              "h2T": ([DH, DH], F32), "w3T": ([DH, DM], F32),
              "s1cm": ([128, 32], F32), "s2cm": ([128, 32], F32),
              "h1scm": ([128, 32], F32), "h2scm": ([128, 32], F32),
              "s3row": ([1, DM], F32), "s3cm": ([128, 8], F32),
              "Win": ([DM, DIN], F32), "bincm": ([128, 16], F32),
              "Bp": ([DIN, 16], F32), "CpT": ([DIN, 16], F32),
              "Dpcm": ([128, 16], F32), "Wout": ([DIN, DM], F32),
              "boutrow": ([1, DM], F32), "phiT": ([DM, NP], F32),
              "phibcm": ([128, 16], F32), "phibrow": ([1, NP], F32),
              "decaycm": ([128, NT], F32), "ew_shard": ([8, DM, DM], F32),
              "eb_shard": ([1, 8 * DM], F32), "masks": ([NT, 128, NBANDS], F32),
              "selread": ([128, 4], F32), "ident": ([128, 128], BF16),
              "ones1x128": ([1, 128], BF16), "oav": ([128, 128], BF16),
              "sel32": ([128, 32], BF16), "rep32": ([32, 128], BF16)}
    for p in ("q", "k", "v", "o"):
        WSPECS[f"u_{p}"] = ([22 * 128, 128], F32)

    class LazyW(dict):
        def __missing__(self, name):
            shp, dt = WSPECS[name]
            ap = nc.dram_tensor(name, list(shp), dt, kind="ExternalInput").ap()
            self[name] = ap
            return ap
    W = LazyW()
    nc._mk_W = W

    sems, semcm = {}, []
    for s in ("d0", "d1", "d2", "d3", "d4", "d5", "d6", "d7",
              "pe", "dve", "act", "pool", "cc", "bar"):
        cm_ = nc.semaphore(s)
        sems[s] = cm_.__enter__()
        semcm.append(cm_)
    block_cm = nc.Block()
    block = block_cm.__enter__()
    ctx = Ctx(nc, block, sems)

    ident = nc.alloc_sbuf_tensor("sb_ident", [128, 128], BF16)
    ones1 = nc.alloc_sbuf_tensor("sb_ones1", [1, 128], BF16)

    ctx.eps = nc.alloc_sbuf_tensor("sb_eps", [128, 1], F32)

    def _vprolog(e):
        e.memset(ctx.eps[:, :], LN_EPS)
    ctx.sec("vector", _vprolog)

    def _prolog(e):
        ctx.dma(e, ident[:, :], W["ident"][:, :], "d7")
        ctx.dma(e, ones1[:, :], W["ones1x128"][:, :], "d7")
        e.wait_ge(sems["d7"], ctx.cnt["d7"])
    ctx.sec("gpsimd", _prolog)
    ctx.prolog_d7 = ctx.cnt["d7"]
    ctx.barrier()

    cur = None
    if "l0a" in phases:
        emit_l0a(ctx, W, x_ext, xdr[0], kv_dr, ident, ones1)
        cur = xdr[0]
    if "l0f" in phases:
        src = cur if cur is not None else V(x_ext, HALO * DM, [[DM, TC], [1, DM]])
        emit_l0f(ctx, W, src, xdr[1], ident)
        cur = xdr[1]
    if "l1a" in phases:
        src = cur if cur is not None else V(x_ext, HALO * DM, [[DM, TC], [1, DM]])
        emit_l1a(ctx, W, src, xdr[0], ident, ones1, uag_in, uag_out)
        cur = xdr[0]
    if "l1f" in phases:
        src = cur if cur is not None else V(x_ext, HALO * DM, [[DM, TC], [1, DM]])
        emit_l1f(ctx, W, src, y_out, ident, ones1, xp_in, xp_out, c_dr,
                 yt_in, yt_out)
        cur = None
    if cur is not None:
        emit_copyout(ctx, cur, y_out)

    ctx.barrier()
    block_cm.__exit__(None, None, None)
    for cm_ in semcm:
        cm_.__exit__(None, None, None)
    return nc


def emit_copyout(ctx, src_dr, y_out):
    nc = ctx.nc
    es = ExitStack()
    t = sal(ctx, es, "cp_t", [128, DM], F32)
    first = [True]
    for s in range(NT):
        def _m(e, s=s):
            if first[0]:
                for sv in getattr(ctx, "pending", []):
                    ctx.wp(e, sv)
                first[0] = False
            sv = ctx.dma(e, t[:, :], src_dr[128 * s:128 * (s + 1), :], "d0")
            ctx.wp(e, sv)
            sv2 = ctx.dma(e, y_out[128 * s:128 * (s + 1), :], t[:, :], "d0")
            ctx.wp(e, sv2)
        ctx.sec("sync", _m)
    es.close()


def emit_l0a(ctx, W, x_ext, x1_dr, kv_dr, ident, ones1):
    nc = ctx.nc
    es0 = ExitStack()
    esA = ExitStack()
    qsb = sal(ctx, es0, "a_qsb", [128, NT * DM], BF16)
    wu = {"o": sal(ctx, es0, "a_wu_o", [128, 22 * 128], BF16)}
    obias = sal(ctx, es0, "a_obias", [1, DM], BF16)
    xt = [sal(ctx, esA, f"a_xt{i}", [128, DM], F32) for i in range(3)]
    lnt = [sal(ctx, esA, f"a_lnt{i}", [128, DM], BF16) for i in range(3)]
    lnf = [sal(ctx, esA, f"a_lnf{i}", [128, DM], BF16) for i in range(2)]
    kvst = [sal(ctx, esA, f"a_kvst{i}", [128, 2 * DM], BF16) for i in range(3)]
    for p in ("q", "k", "v"):
        wu[p] = sal(ctx, esA, f"a_wu_{p}", [128, 22 * 128], BF16)
    bbc = sal(ctx, esA, "a_bbc_qkv", [128, 3 * DM], BF16)
    ubias = sal(ctx, esA, "sb_ubias", [1, 4 * DM], BF16)
    ln = LNState(ctx, esA, "a")
    ptr = [pal(ctx, esA, f"a_ptr{i}", [128, DM], BF16) for i in range(2)]
    pj = {p: pal(ctx, esA, f"a_p{p}", [128, DM], F32) for p in ("q", "k", "v")}

    def _pl(e):
        ctx.dma(e, ubias[:, :], W["ubias"][:, :], "d7")
        for p in ("q", "k", "v", "o"):
            src = V(W[f"u_{p}"], 0, [[128, 128], [128 * 128, 22], [1, 128]])
            dst = wu[p].ap().rearrange("p (j c) -> p j c", j=22, c=128)
            ctx.dma(e, dst, src, "d7")
        e.wait_ge(ctx.sems["d7"], ctx.cnt["d7"])
    ctx.sec("gpsimd", _pl)
    d7_w = ctx.cnt["d7"]

    prev_evac = 0
    for j in range(3):
        def _pe(e, j=j, pv=prev_evac):
            if j == 0:
                ctx.w(e, "d7", d7_w)
            ctx.w(e, "dve", pv)
            last = None
            for h in range(2):
                last = e.matmul(pj["v"][:, 512 * h:512 * (h + 1)], ones1[0:1, :],
                                ubias[0:1, 1024 * j + 512 * h:1024 * j + 512 * (h + 1)],
                                start=True, stop=True)
            ctx.inc(last, "pe")
        ctx.sec("tensor", _pe)
        pe_now = ctx.cnt["pe"]

        def _ve(e, j=j, pe_now=pe_now):
            ctx.w(e, "pe", pe_now)
            if j == 0:
                e.tensor_copy(obias[0:1, :], ubias[0:1, 3 * DM:4 * DM])
            e.tensor_copy(bbc[:, 1024 * j:1024 * (j + 1)], pj["v"][:, :])
        ctx.sec("vector", _ve)
        prev_evac = ctx.cnt["dve"]

    ln_done = [0] * NTH
    sq_act = [0] * NTH
    pe_tr = [0] * NTH
    dve_lnf = [0] * NTH
    pe_proj = [0] * NTH
    dve_evac = [0] * NTH
    st_kv = [None] * NTH
    ld_x = [None] * NTH

    for s in range(NTH + 1):
        own = s >= (NTH - NT)
        par = s % 3
        J = s - 1

        if s < NTH:
            def _load(e, s=s, par=par):
                if s >= 3:
                    ctx.w(e, "dve", ln_done[s - 3])
                    ctx.w(e, "act", sq_act[s - 3])
                ld_x[s] = ctx.dma(e, xt[par][:, :],
                                  x_ext[128 * s:128 * (s + 1), :], f"d{s % 2}")
            ctx.sec("sync" if s % 2 == 0 else "gpsimd", _load)

        if s >= 1:
            jown = J >= (NTH - NT)

            def _evac(e, J=J, jown=jown):
                ctx.w(e, "pe", pe_proj[J])
                if J >= 3:
                    ctx.wp(e, st_kv[J - 3])
                e.tensor_add(kvst[J % 3][:, 0:DM], pj["k"][:, :],
                             bbc[:, DM:2 * DM])
                e.tensor_add(kvst[J % 3][:, DM:2 * DM], pj["v"][:, :],
                             bbc[:, 2 * DM:3 * DM])
                if jown:
                    so = J - (NTH - NT)
                    e.tensor_add(qsb[:, DM * so:DM * (so + 1)], pj["q"][:, :],
                                 bbc[:, 0:DM])
            ctx.sec("vector", _evac)
            dve_evac[J] = ctx.cnt["dve"]

            def _st(e, J=J):
                ctx.w(e, "dve", dve_evac[J])
                st_kv[J] = ctx.dma(e, kv_dr[128 * J:128 * (J + 1), :],
                                   kvst[J % 3][:, :], f"d{2 + J % 2}")
            ctx.sec("scalar" if J % 2 == 0 else "gpsimd", _st)

        if s < NTH:
            d = emit_ln(ctx, ln, xt[par][:, :], lnt[par][:, :], [ld_x[s]],
                        norm_waits=([("pe", pe_tr[s - 3])] if s >= 3 else ()))
            ln_done[s] = d
            sq_act[s] = ln.last_sq_act

            def _tr(e, s=s, par=par, d=d):
                ctx.w(e, "dve", d)
                if s >= 2:
                    ctx.w(e, "dve", dve_lnf[s - 2])
                last = None
                for b in range(8):
                    last = e.transpose(ptr[s % 2][:, 128 * b:128 * (b + 1)],
                                       lnt[par][:, 128 * b:128 * (b + 1)],
                                       ident[:, :])
                ctx.inc(last, "pe")
            ctx.sec("tensor", _tr)
            pe_tr[s] = ctx.cnt["pe"]

            def _lnf(e, s=s):
                ctx.w(e, "pe", pe_tr[s])
                if s >= 2:
                    ctx.w(e, "pe", pe_proj[s - 2])
                e.tensor_copy(lnf[s % 2][:, :], ptr[s % 2][:, :])
            ctx.sec("vector", _lnf)
            dve_lnf[s] = ctx.cnt["dve"]

            projs = ("q", "k", "v") if own else ("k", "v")

            def _proj(e, s=s, projs=projs):
                ctx.w(e, "dve", dve_lnf[s])
                if s >= 1:
                    ctx.w(e, "dve", dve_evac[s - 1])
                started = {p: [False, False] for p in projs}
                last = None
                for bp in range(8):
                    lhsT = lnf[s % 2][:, 128 * bp:128 * (bp + 1)]
                    for p in projs:
                        tri = [(bp, bp)]
                        if bp + 1 < 8:
                            tri.append((bp + 1, 8 + bp))
                        if bp - 1 >= 0:
                            tri.append((bp - 1, 15 + bp - 1))
                        for (b, jj) in tri:
                            bank = b // 4
                            st = not started[p][bank]
                            started[p][bank] = True
                            last = e.matmul(pj[p][:, 128 * b:128 * (b + 1)],
                                            lhsT,
                                            wu[p][:, 128 * jj:128 * (jj + 1)],
                                            start=st, stop=False,
                                            skip_group_check=True)
                ctx.inc(last, "pe")
            ctx.sec("tensor", _proj)
            pe_proj[s] = ctx.cnt["pe"]

    d2_tot, d3_tot = ctx.cnt["d2"], ctx.cnt["d3"]
    ctx.barrier(drains=st_kv[-2:])
    esA.close()

    esB = ExitStack()
    CHG = [[(0, [0, 1, 2, 3, 4])],
           [(5, [8, 16]), (7, [32, 64]), (9, [128])],
           [(10, [256]), (11, [512, 1024]), (13, [2048, 4096])]]
    CW = 5 * DM
    PVROW = NBANDS * DM
    kcb = [sal(ctx, esB, f"b_kcb{i}", [128, CW], BF16) for i in range(3)]
    vcb = [sal(ctx, esB, f"b_vcb{i}", [128, CW], BF16) for i in range(3)]
    qkprod = sal(ctx, esB, "b_qkp", [128, CW], BF16)
    pvcP = sal(ctx, esB, "b_pvcP", [128, CW], BF16)
    pvcD = sal(ctx, esB, "b_pvcD", [128, CW], BF16)
    scor = [sal(ctx, esB, f"b_scor{i}", [128, NBANDS * NH], F32) for i in range(2)]
    escr = [sal(ctx, esB, f"b_escr{i}", [128, NBANDS * NH], F32) for i in range(2)]
    Pb = [sal(ctx, esB, f"b_P{i}", [128, NBANDS * NH], BF16) for i in range(2)]
    sm = sal(ctx, esB, "b_sm", [128, 3 * NH], F32)
    accA = [sal(ctx, esB, f"b_accA{i}", [128, DM], F32) for i in range(2)]
    accB = [sal(ctx, esB, f"b_accB{i}", [128, DM], F32) for i in range(2)]
    accb = [sal(ctx, esB, f"b_accb{i}", [128, DM], BF16) for i in range(2)]
    msk = sal(ctx, esB, "b_msk", [128, NT * NBANDS], F32)
    afm = [sal(ctx, esB, f"c_afm{i}", [128, DM], BF16) for i in range(2)]
    x0t = [sal(ctx, esB, f"c_x0{i}", [128, DM], F32) for i in range(2)]
    ot = [sal(ctx, esB, f"c_ot{i}", [128, DM], F32) for i in range(2)]
    lnn = sal(ctx, esB, "c_lnn", [128, DM], F32)
    x1t = [sal(ctx, esB, f"c_x1{i}", [128, DM], F32) for i in range(2)]
    lnC = LNState(ctx, esB, "c")
    ptrc = [pal(ctx, esB, f"c_ptr{i}", [128, DM], BF16) for i in range(2)]
    po = [pal(ctx, esB, f"c_po{i}", [128, DM], F32) for i in range(2)]

    def _mload(e):
        src = V(W["masks"], 0, [[NBANDS, 128], [128 * NBANDS, NT], [1, NBANDS]])
        dst = msk.ap().rearrange("p (i j) -> p i j", i=NT, j=NBANDS)
        ctx.dma(e, dst, src, "d7")
        e.wait_ge(ctx.sems["d7"], ctx.cnt["d7"])
    ctx.sec("gpsimd", _mload)
    d7_m = ctx.cnt["d7"]

    NC3 = 3 * NT
    ld_k = [None] * NC3
    ld_v = [None] * NC3
    dve_qkc = [0] * NC3
    pool_pvm = [0] * NC3
    dve_pvm = [0] * NC3
    dve_qk = [0] * NT
    act_exp = [0] * NT
    dve_P = [0] * NT
    pool_pvA = [0] * NT
    dve_fin = [0] * NT
    pe_tr = [0] * NT
    a_afm = [0] * NT
    pe_o = [0] * NT
    a_ot = [0] * NT
    dve_x1 = [0] * NT
    ld_x0 = [None] * NT
    st_x1 = [None] * NT
    kvrow = 2 * DM

    for I in range(NT + 1):
        par = I % 2
        J = I - 1
        jpar = J % 2
        base_row = HALO + 128 * I

        if I < NT:
            def _kload(e, I=I, base_row=base_row):
                if I == 0:
                    ctx.w(e, "d2", d2_tot)
                    ctx.w(e, "d3", d3_tot)
                for c in range(3):
                    g = 3 * I + c
                    if g >= 3:
                        ctx.w(e, "dve", dve_qkc[g - 3])
                    last = None
                    for (pos, ds) in CHG[c]:
                        step = ds[1] - ds[0] if len(ds) > 1 else 1
                        src = V(kv_dr, (base_row - ds[0]) * kvrow,
                                [[kvrow, 128], [-step * kvrow, len(ds)], [1, DM]])
                        dst = V(kcb[g % 3], DM * (pos - 5 * c),
                                [[CW, 128], [DM, len(ds)], [1, DM]])
                        last = ctx.dma(e, dst, src, f"d{g % 2}")
                    ld_k[g] = last
            ctx.sec("sync", _kload)

            def _vload(e, I=I, base_row=base_row):
                for c in range(3):
                    g = 3 * I + c
                    if g >= 3:
                        if c == 2:
                            ctx.w(e, "dve", dve_pvm[g - 3])
                        else:
                            ctx.w(e, "pool", pool_pvm[g - 3])
                    last = None
                    for (pos, ds) in CHG[c]:
                        step = ds[1] - ds[0] if len(ds) > 1 else 1
                        src = V(kv_dr, (base_row - ds[0]) * kvrow + DM,
                                [[kvrow, 128], [-step * kvrow, len(ds)], [1, DM]])
                        dst = V(vcb[g % 3], DM * (pos - 5 * c),
                                [[CW, 128], [DM, len(ds)], [1, DM]])
                        last = ctx.dma(e, dst, src, f"d{2 + g % 2}")
                    ld_v[g] = last
            ctx.sec("scalar", _vload)

            def _qk(e, I=I, par=par):
                if I == 0:
                    ctx.w(e, "d7", d7_m)
                if I >= 2:
                    ctx.w(e, "act", act_exp[I - 2])
                for c in range(3):
                    g = 3 * I + c
                    ctx.wp(e, ld_k[g])
                    qv = V(qsb, DM * I,
                           [[NT * DM, 128], [0, 5], [64, NH], [1, HD]])
                    kv_ = V(kcb[g % 3], 0,
                            [[CW, 128], [DM, 5], [64, NH], [1, HD]])
                    pv = V(qkprod, 0, [[CW, 128], [DM, 5], [64, NH], [1, HD]])
                    e.tensor_mul(pv, qv, kv_)
                    rin = V(qkprod, 0, [[CW, 128], [64, 5 * NH], [1, HD]])
                    e.reduce_sum(scor[par][:, 5 * NH * c:5 * NH * (c + 1)],
                                 rin, AX.X)
                    dve_qkc[g] = ctx.cnt["dve"]
                mv = V(msk, NBANDS * I, [[NT * NBANDS, 128], [1, NBANDS], [0, NH]])
                sv = V(scor[par], 0, [[NBANDS * NH, 128], [NH, NBANDS], [1, NH]])
                e.tensor_add(sv, sv, mv)
                svh = V(scor[par], 0, [[NBANDS * NH, 128], [1, NH], [NH, NBANDS]])
                e.reduce_max(sm[:, 0:NH], svh, AX.X)
                mxv = V(sm, 0, [[3 * NH, 128], [0, NBANDS], [1, NH]])
                e.tensor_sub(scor[par][:, :], scor[par][:, :], mxv)
            ctx.sec("vector", _qk)
            dve_qk[I] = ctx.cnt["dve"]

            def _exp(e, I=I, par=par):
                ctx.w(e, "dve", dve_qk[I])
                if I >= 2:
                    ctx.w(e, "dve", dve_P[I - 2])
                e.activation(out=escr[par][:, :], in_=scor[par][:, :],
                             func=AF.Exp, scale=0.125)
            ctx.sec("scalar", _exp)
            act_exp[I] = ctx.cnt["act"]

            def _P(e, I=I, par=par):
                ctx.w(e, "act", act_exp[I])
                if I >= 2:
                    ctx.w(e, "pool", pool_pvm[3 * (I - 2) + 1])
                ev = V(escr[par], 0, [[NBANDS * NH, 128], [1, NH], [NH, NBANDS]])
                e.reduce_sum(sm[:, NH:2 * NH], ev, AX.X)
                e.reciprocal(sm[:, 2 * NH:3 * NH], sm[:, NH:2 * NH])
                rzv = V(sm, 2 * NH, [[3 * NH, 128], [0, NBANDS], [1, NH]])
                e.tensor_mul(Pb[par][:, :], escr[par][:, :], rzv)
            ctx.sec("vector", _P)
            dve_P[I] = ctx.cnt["dve"]

            def _pvB(e, I=I, par=par):
                g = 3 * I + 2
                ctx.wp(e, ld_v[g])
                pvw = V(Pb[par], 10 * NH,
                        [[NBANDS * NH, 128], [NH, 5], [1, NH], [0, HD]])
                vbw = V(vcb[g % 3], 0, [[CW, 128], [DM, 5], [64, NH], [1, HD]])
                e.tensor_mul(pvcD[:, :], vbw, pvw)
                dve_pvm[g] = ctx.cnt["dve"]
                e.tensor_add(accB[par][:, :], pvcD[:, 0:DM], pvcD[:, DM:2 * DM])
                for d in range(2, 5):
                    e.tensor_add(accB[par][:, :], accB[par][:, :],
                                 pvcD[:, DM * d:DM * (d + 1)])
            ctx.sec("vector", _pvB)

            def _pvA(e, I=I, par=par):
                ctx.w(e, "dve", dve_P[I])
                if I >= 2:
                    ctx.w(e, "dve", dve_fin[I - 2])
                for c in range(2):
                    g = 3 * I + c
                    ctx.wp(e, ld_v[g])
                    pvw = V(Pb[par], 5 * c * NH,
                            [[NBANDS * NH, 128], [NH, 5], [1, NH], [0, HD]])
                    vbw = V(vcb[g % 3], 0, [[CW, 128], [DM, 5], [64, NH], [1, HD]])
                    e.tensor_mul(pvcP[:, :], vbw, pvw)
                    pool_pvm[g] = ctx.cnt["pool"]
                    if c == 0:
                        e.tensor_add(accA[par][:, :], pvcP[:, 0:DM],
                                     pvcP[:, DM:2 * DM])
                    else:
                        e.tensor_add(accA[par][:, :], accA[par][:, :],
                                     pvcP[:, 0:DM])
                        e.tensor_add(accA[par][:, :], accA[par][:, :],
                                     pvcP[:, DM:2 * DM])
                    for d in range(2, 5):
                        e.tensor_add(accA[par][:, :], accA[par][:, :],
                                     pvcP[:, DM * d:DM * (d + 1)])
            ctx.sec("gpsimd", _pvA)
            pool_pvA[I] = ctx.cnt["pool"]

            def _x0(e, I=I, par=par):
                if I >= 2:
                    ctx.w(e, "dve", dve_x1[I - 2])
                ld_x0[I] = ctx.dma(e, x0t[par][:, :],
                                   x_ext[HALO + 128 * I:HALO + 128 * (I + 1), :],
                                   "d6")
            ctx.sec("gpsimd", _x0)

        if I >= 1:
            def _fin(e, J=J, jpar=jpar):
                ctx.w(e, "pool", pool_pvA[J])
                if J >= 2:
                    ctx.w(e, "pe", pe_tr[J - 2])
                e.tensor_add(accb[jpar][:, :], accA[jpar][:, :],
                             accB[jpar][:, :])
            ctx.sec("vector", _fin)
            dve_fin[J] = ctx.cnt["dve"]

            def _tr(e, J=J, jpar=jpar):
                ctx.w(e, "dve", dve_fin[J])
                if J >= 2:
                    ctx.w(e, "act", a_afm[J - 2])
                last = None
                for b in range(8):
                    last = e.transpose(ptrc[jpar][:, 128 * b:128 * (b + 1)],
                                       accb[jpar][:, 128 * b:128 * (b + 1)],
                                       ident[:, :])
                ctx.inc(last, "pe")
            ctx.sec("tensor", _tr)
            pe_tr[J] = ctx.cnt["pe"]

            def _afm(e, J=J, jpar=jpar):
                ctx.w(e, "pe", pe_tr[J])
                if J >= 2:
                    ctx.w(e, "pe", pe_o[J - 2])
                e.activation(out=afm[jpar][:, :], in_=ptrc[jpar][:, :],
                             func=AF.Copy)
            ctx.sec("scalar", _afm)
            a_afm[J] = ctx.cnt["act"]

            def _oproj(e, J=J, jpar=jpar):
                ctx.w(e, "act", a_afm[J])
                if J >= 2:
                    ctx.w(e, "act", a_ot[J - 2])
                started = [False, False]
                for bp in range(8):
                    lhsT = afm[jpar][:, 128 * bp:128 * (bp + 1)]
                    tri = [(bp, bp)]
                    if bp + 1 < 8:
                        tri.append((bp + 1, 8 + bp))
                    if bp - 1 >= 0:
                        tri.append((bp - 1, 15 + bp - 1))
                    for (b, jj) in tri:
                        bank = b // 4
                        st = not started[bank]
                        started[bank] = True
                        last = e.matmul(po[jpar][:, 128 * b:128 * (b + 1)], lhsT,
                                        wu["o"][:, 128 * jj:128 * (jj + 1)],
                                        start=st, stop=False,
                                        skip_group_check=True)
                for h in range(2):
                    last = e.matmul(po[jpar][:, 512 * h:512 * (h + 1)],
                                    ones1[0:1, :],
                                    obias[0:1, 512 * h:512 * (h + 1)],
                                    start=False, stop=False,
                                    skip_group_check=True)
                ctx.inc(last, "pe")
            ctx.sec("tensor", _oproj)
            pe_o[J] = ctx.cnt["pe"]

            def _ot(e, J=J, jpar=jpar):
                ctx.w(e, "pe", pe_o[J])
                if J >= 2:
                    ctx.w(e, "dve", dve_x1[J - 2])
                e.activation(out=ot[jpar][:, :], in_=po[jpar][:, :],
                             func=AF.Copy)
            ctx.sec("scalar", _ot)
            a_ot[J] = ctx.cnt["act"]

            d = emit_ln(ctx, lnC, ot[jpar][:, :], lnn[:, :],
                        [("act", a_ot[J]), ld_x0[J]],
                        res_add_ap=x0t[jpar][:, :], dst2=x1t[jpar][:, :],
                        norm_waits=([st_x1[J - 2]] if J >= 2 else ()))
            dve_x1[J] = d

            def _stx(e, J=J, jpar=jpar):
                ctx.w(e, "dve", dve_x1[J])
                st_x1[J] = ctx.dma(e, x1_dr[128 * J:128 * (J + 1), :],
                                   x1t[jpar][:, :], f"d{4 + jpar}")
            ctx.sec("scalar", _stx)

    ctx.pending = [sv for sv in st_x1 if sv is not None][-2:]
    ctx.barrier(drains=st_x1[-2:])
    esB.close()
    es0.close()


def emit_l0f(ctx, W, x_in, x_out, ident):
    nc = ctx.nc
    NO1, NOH, NOC = 32, 32, 8
    drains = getattr(ctx, "pending", [])
    ctx.pending = []

    esS = ExitStack()
    for nm, w_ in (("s1cm", 32), ("s2cm", 32), ("h1scm", 32), ("h2scm", 32),
                   ("s3cm", 8)):
        W[nm + "_sb"] = sal(ctx, esS, "sb_" + nm, [128, w_], F32)

    def _sprolog(e):
        for nm in ("s1cm", "s2cm", "h1scm", "h2scm", "s3cm"):
            ctx.dma(e, W[nm + "_sb"][:, :], W[nm][:, :], "d7")
        e.wait_ge(ctx.sems["d7"], ctx.cnt["d7"])
    ctx.sec("gpsimd", _sprolog)
    scale_tot = ("d7", ctx.cnt["d7"])

    for tg in range(2):
        es = ExitStack()
        esL = ExitStack()
        g_sb = sal(ctx, es, f"f_g{tg}", [128, NO1 * 1024], BF16)
        xt = [sal(ctx, es, f"f_xt{tg}{i}", [128, DM], F32) for i in range(2)]
        lnt = [sal(ctx, es, f"f_lnt{tg}{i}", [128, DM], BF16) for i in range(2)]
        wh = [sal(ctx, es, f"f_wh{tg}{i}", [128, 32 * 128], BF16) for i in range(2)]
        dt_scr = sal(ctx, es, f"f_dt{tg}", [128, DM], BF16)
        x2t = [sal(ctx, es, f"f_x2{tg}{i}", [128, DM], F32) for i in range(2)]
        lnF = LNState(ctx, es, f"f{tg}")
        lnO = LNState(ctx, es, f"fo{tg}")
        lnx = sal(ctx, esL, f"f_lnx{tg}", [128, 8 * 1024], BF16)
        u_scr = sal(ctx, esL, f"f_u{tg}", [128, 1024], BF16)
        v_scr = sal(ctx, esL, f"f_v{tg}", [128, 1024], BF16)
        w12 = [sal(ctx, esL, f"f_w12{tg}{i}", [128, 16 * 128], BF16) for i in range(2)]
        pu = pal(ctx, es, f"f_pu{tg}", [128, 1024], F32)
        pv2 = pal(ctx, es, f"f_pv2{tg}", [128, 1024], F32)
        esP = ExitStack()
        ptr = pal(ctx, esP, f"f_ptr{tg}", [128, DM], BF16)

        ln_done = [0] * 8
        sq_act = [0] * 8
        pe_tr = [0] * 8
        dve_lnx = [0] * 8
        ld_x = [None] * 8
        for j in range(8):
            par = j % 2
            row0 = tg * 1024 + 128 * j

            def _ld(e, j=j, par=par, row0=row0, tg=tg):
                if tg == 0 and j == 0:
                    for sv in drains:
                        ctx.wp(e, sv)
                if j >= 2:
                    ctx.w(e, "dve", ln_done[j - 2])
                    ctx.w(e, "act", sq_act[j - 2])
                ld_x[j] = ctx.dma(e, xt[par][:, :],
                                  x_in[row0:row0 + 128, :], f"d{par}")
            ctx.sec("sync", _ld)

            d = emit_ln(ctx, lnF, xt[par][:, :], lnt[par][:, :], [ld_x[j]],
                        norm_waits=([("pe", pe_tr[j - 2])] if j >= 2 else ()))
            ln_done[j] = d
            sq_act[j] = lnF.last_sq_act

            def _tr(e, j=j, par=par, d=d):
                ctx.w(e, "dve", d)
                if j >= 1:
                    ctx.w(e, "dve", dve_lnx[j - 1])
                last = None
                for b in range(8):
                    last = e.transpose(ptr[:, 128 * b:128 * (b + 1)],
                                       lnt[par][:, 128 * b:128 * (b + 1)],
                                       ident[:, :])
                ctx.inc(last, "pe")
            ctx.sec("tensor", _tr)
            pe_tr[j] = ctx.cnt["pe"]

            def _ev(e, j=j):
                ctx.w(e, "pe", pe_tr[j])
                dst = V(lnx, 128 * j, [[8 * 1024, 128], [1024, 8], [1, 128]])
                e.tensor_copy(dst, ptr[:, :])
            ctx.sec("vector", _ev)
            dve_lnx[j] = ctx.cnt["dve"]

        lnx_done = dve_lnx[7]
        esP.close()
        esH = ExitStack()
        phw2 = [pal(ctx, esH, f"f_phw{tg}{i}", [128, 1024], F32) for i in range(2)]

        ld_w12 = [None] * NO1
        pe_uv = [0] * NO1
        act_u = [0] * NO1
        dve_g = [0] * NO1
        for o in range(NO1):
            par = o % 2

            def _wl(e, o=o, par=par):
                if o >= 2:
                    ctx.w(e, "pe", pe_uv[o - 2])
                else:
                    ctx.w(e, "pe", getattr(ctx, "wgate", 0))
                for wnm in ("w1T", "w2T"):
                    off = 0 if wnm == "w1T" else 8 * 128
                    src = V(W[wnm], 128 * o,
                            [[DH, 128], [128 * DH, 8], [1, 128]])
                    dst = V(w12[par], off, [[16 * 128, 128], [128, 8], [1, 128]])
                    ld_w12[o] = ctx.dma(e, dst, src, f"d{6 + par}")
            ctx.sec("gpsimd", _wl)

            def _mm(e, o=o, par=par):
                if o == 0:
                    ctx.w(e, "dve", lnx_done)
                ctx.wp(e, ld_w12[o])
                last = None
                for wi, ps in ((0, pu), (1, pv2)):
                    if wi == 0 and o >= 1:
                        ctx.w(e, "act", act_u[o - 1])
                    if wi == 1 and o >= 1:
                        ctx.w(e, "dve", dve_g[o - 1])
                    for ki in range(8):
                        lhsT = w12[par][:, (8 * wi + ki) * 128:(8 * wi + ki + 1) * 128]
                        for h in range(2):
                            last = e.matmul(ps[:, 512 * h:512 * (h + 1)], lhsT,
                                            V(lnx, ki * 1024 + 512 * h,
                                              [[8 * 1024, 128], [1, 512]]),
                                            start=(ki == 0), stop=(ki == 7),
                                            skip_group_check=True)
                    if wi == 0:
                        ctx.inc(last, "pe")
                ctx.inc(last, "pe")
            ctx.sec("tensor", _mm)
            pe_uv[o] = ctx.cnt["pe"]
            pe_u_part = pe_uv[o] - 1

            def _ue(e, o=o, tg=tg, pw=pe_u_part):
                ctx.w(e, "pe", pw)
                if tg == 0 and o == 0:
                    ctx.wp(e, scale_tot)
                if o >= 1:
                    ctx.w(e, "dve", dve_g[o - 1])
                e.activation(out=u_scr[:, :], in_=pu[:, :], func=AF.Silu,
                             scale=W["s1cm_sb"][:, o:o + 1])
            ctx.sec("scalar", _ue)
            act_u[o] = ctx.cnt["act"]

            def _ge(e, o=o, tg=tg):
                ctx.w(e, "pe", pe_uv[o])
                if tg == 0 and o == 0:
                    ctx.wp(e, scale_tot)
                e.tensor_scalar(v_scr[:, :], pv2[:, :],
                                W["s2cm_sb"][:, o:o + 1], None, ALU.mult)
                ctx.w(e, "act", act_u[o])
                e.tensor_mul(g_sb[:, 1024 * o:1024 * (o + 1)], u_scr[:, :],
                             v_scr[:, :])
            ctx.sec("vector", _ge)
            dve_g[o] = ctx.cnt["dve"]

        esL.close()
        ctx.wgate = pe_uv[NO1 - 1]
        es_b = ExitStack()
        b_sb = sal(ctx, es_b, f"f_b{tg}", [128, NOH * 1024], BF16)

        def hpass(src_sb, dst_write, wname, scname, first_wait_dve, first_wait_act,
                  war_pe_total):
            ld_wh = [None] * NOH
            pe_h = [0] * NOH
            act_h = [0] * NOH
            for o in range(NOH):
                par = o % 2

                def _wl(e, o=o, par=par):
                    if o >= 2:
                        ctx.w(e, "pe", pe_h[o - 2])
                    else:
                        ctx.w(e, "pe", getattr(ctx, "wgate", 0))
                    src = V(W[wname], 128 * o,
                            [[DH, 128], [128 * DH, 32], [1, 128]])
                    dst = wh[par].ap().rearrange("p (k c) -> p k c", k=32, c=128)
                    ld_wh[o] = ctx.dma(e, dst, src, f"d{6 + par}")
                ctx.sec("gpsimd", _wl)

                def _mm(e, o=o, par=par):
                    if o == 0:
                        if first_wait_dve:
                            ctx.w(e, "dve", first_wait_dve)
                        if first_wait_act:
                            ctx.w(e, "act", first_wait_act)
                    ctx.wp(e, ld_wh[o])
                    if o >= 2:
                        ctx.w(e, "act", act_h[o - 2])
                    ph = phw2[o % 2]
                    last = None
                    for ki in range(32):
                        lhsT = wh[par][:, ki * 128:(ki + 1) * 128]
                        for h in range(2):
                            last = e.matmul(ph[:, 512 * h:512 * (h + 1)], lhsT,
                                            V(src_sb, ki * 1024 + 512 * h,
                                              [[NOH * 1024, 128], [1, 512]]),
                                            start=(ki == 0), stop=(ki == 31),
                                            skip_group_check=True)
                    ctx.inc(last, "pe")
                ctx.sec("tensor", _mm)
                pe_h[o] = ctx.cnt["pe"]

                def _be(e, o=o, tg=tg):
                    ctx.w(e, "pe", pe_h[o])
                    if tg == 0 and o == 0:
                        ctx.wp(e, scale_tot)
                    if war_pe_total and o == 0:
                        ctx.w(e, "pe", war_pe_total[0])
                    e.activation(out=dst_write(o), in_=phw2[o % 2][:, :],
                                 func=AF.Gelu, scale=W[scname][:, o:o + 1])
                ctx.sec("scalar", _be)
                act_h[o] = ctx.cnt["act"]
            return pe_h, act_h

        pe_h1, act_b = hpass(g_sb, lambda o: b_sb[:, 1024 * o:1024 * (o + 1)],
                             "h1T", "h1scm_sb", dve_g[NO1 - 1], 0, None)
        ctx.wgate = pe_h1[NOH - 1]
        pe_h2, act_c = hpass(b_sb, lambda o: g_sb[:, 1024 * o:1024 * (o + 1)],
                             "h2T", "h2scm_sb", 0, act_b[NOH - 1], [pe_h1[NOH - 1]])
        ctx.wgate = pe_h2[NOH - 1]
        es_b.close()
        es_d = ExitStack()
        dfm = sal(ctx, es_d, f"f_dfm{tg}", [128, 8 * 1024], BF16)

        ld_w3 = [None] * NOC
        pe_w3 = [0] * NOC
        dve_dfm = [0] * NOC
        for oc in range(NOC):
            par = oc % 2

            def _wl(e, oc=oc, par=par):
                if oc >= 2:
                    ctx.w(e, "pe", pe_w3[oc - 2])
                else:
                    ctx.w(e, "pe", getattr(ctx, "wgate", 0))
                src = V(W["w3T"], 128 * oc,
                        [[DM, 128], [128 * DM, 32], [1, 128]])
                dst = wh[par].ap().rearrange("p (k c) -> p k c", k=32, c=128)
                ld_w3[oc] = ctx.dma(e, dst, src, f"d{6 + par}")
            ctx.sec("gpsimd", _wl)

            def _mm(e, oc=oc, par=par):
                if oc == 0:
                    ctx.w(e, "act", act_c[NOH - 1])
                ctx.wp(e, ld_w3[oc])
                if oc >= 2:
                    ctx.w(e, "dve", dve_dfm[oc - 2])
                last = None
                for ki in range(32):
                    lhsT = wh[par][:, ki * 128:(ki + 1) * 128]
                    for h in range(2):
                        last = e.matmul(phw2[oc % 2][:, 512 * h:512 * (h + 1)], lhsT,
                                        V(g_sb, ki * 1024 + 512 * h,
                                          [[NO1 * 1024, 128], [1, 512]]),
                                        start=(ki == 0), stop=(ki == 31),
                                        skip_group_check=True)
                ctx.inc(last, "pe")
            ctx.sec("tensor", _mm)
            pe_w3[oc] = ctx.cnt["pe"]

            def _de(e, oc=oc, tg=tg):
                ctx.w(e, "pe", pe_w3[oc])
                if tg == 0 and oc == 0:
                    ctx.wp(e, scale_tot)
                e.tensor_scalar(dfm[:, 1024 * oc:1024 * (oc + 1)],
                                phw2[oc % 2][:, :],
                                W["s3cm_sb"][:, oc:oc + 1], None, ALU.mult)
            ctx.sec("vector", _de)
            dve_dfm[oc] = ctx.cnt["dve"]

        ctx.wgate = pe_w3[NOC - 1]
        esH.close()
        esP2 = ExitStack()
        ptrF = pal(ctx, esP2, f"f_ptrF{tg}", [128, DM], BF16)
        pe_trF = [0] * 8
        dve_out = [0] * 8
        st_x2 = [None] * 8
        ld_x1 = [None] * 8
        for j in range(8):
            par = j % 2
            row0 = tg * 1024 + 128 * j

            def _ld(e, j=j, par=par, row0=row0):
                if j >= 2:
                    ctx.w(e, "dve", dve_out[j - 2])
                else:
                    ctx.w(e, "dve", ln_done[6 + j])
                    ctx.w(e, "act", sq_act[6 + j])
                ld_x1[j] = ctx.dma(e, xt[par][:, :], x_in[row0:row0 + 128, :],
                                   f"d{par}")
            ctx.sec("sync", _ld)

            def _tr(e, j=j):
                if j == 0:
                    ctx.w(e, "dve", dve_dfm[NOC - 1])
                if j >= 1:
                    ctx.w(e, "dve", dve_out[j - 1])
                last = None
                for oc in range(8):
                    last = e.transpose(
                        ptrF[:, 128 * oc:128 * (oc + 1)],
                        dfm[:, 1024 * oc + 128 * j:1024 * oc + 128 * (j + 1)],
                        ident[:, :])
                ctx.inc(last, "pe")
            ctx.sec("tensor", _tr)
            pe_trF[j] = ctx.cnt["pe"]

            def _ev(e, j=j):
                ctx.w(e, "pe", pe_trF[j])
                if j >= 1:
                    ctx.w(e, "act", lnO.last_sq_act)
                e.tensor_copy(dt_scr[:, :], ptrF[:, :])
            ctx.sec("vector", _ev)

            d = emit_ln(ctx, lnO, dt_scr[:, :], lnF.scr[:, :],
                        [("dve", ctx.cnt["dve"]), ld_x1[j]],
                        res_add_ap=xt[par][:, :], dst2=x2t[par][:, :],
                        norm_waits=([st_x2[j - 2]] if j >= 2 else ()))
            dve_out[j] = d

            def _st(e, j=j, par=par, row0=row0):
                ctx.w(e, "dve", dve_out[j])
                st_x2[j] = ctx.dma(e, x_out[row0:row0 + 128, :], x2t[par][:, :],
                                   f"d{4 + par}")
            ctx.sec("scalar", _st)

        ctx.pending.extend([sv for sv in st_x2 if sv is not None][-2:])
        ctx.barrier(drains=st_x2[-2:])
        esP2.close()
        es_d.close()
        es.close()
    esS.close()


def emit_l1a(ctx, W, x_in, x_out, ident, ones1, uag_in, uag_out):
    nc = ctx.nc
    drains = getattr(ctx, "pending", [])
    ctx.pending = []
    NG = 16
    es = ExitStack()
    y_sb = sal(ctx, es, "s_y", [128, NG * 2048], BF16)
    xt = [sal(ctx, es, f"s_xt{i}", [128, DM], F32) for i in range(2)]
    lnt = [sal(ctx, es, f"s_lnt{i}", [128, DM], BF16) for i in range(2)]
    u_buf = sal(ctx, es, "s_u", [128, 16], F32)
    gcol = sal(ctx, es, "s_gcol", [128, 16], F32)
    cvec = sal(ctx, es, "s_cvec", [128, 6 * 16], F32)
    bp_sb = sal(ctx, es, "s_bp", [128, 16 * 16], F32)
    cpt_sb = sal(ctx, es, "s_cpt", [128, 16 * 16], F32)
    uago = sal(ctx, es, "s_uago", [128, 4 * 16], F32)
    ugb = sal(ctx, es, "s_ugb", [128, 16], BF16)
    p9 = sal(ctx, es, "s_p9", [128, 1], F32)
    t16 = sal(ctx, es, "s_t16", [128, 16], F32)
    boutr = sal(ctx, es, "s_bout", [1, DM], BF16)
    wrow = sal(ctx, es, "s_wrow", [1, DM], BF16)
    wb_sb = sal(ctx, es, "s_wb", [128, DM], BF16)
    lnS = LNState(ctx, es, "s")
    lnSo = LNState(ctx, es, "so")
    BIN, DP, DEC, SEL, UPV, UG = range(6)

    def _pl(e):
        ctx.dma(e, cvec[:, BIN * 16:BIN * 16 + 16], W["bincm"][:, :], "d7")
        ctx.dma(e, cvec[:, DP * 16:DP * 16 + 16], W["Dpcm"][:, :], "d7")
        ctx.dma(e, cvec[:, DEC * 16:DEC * 16 + 16], W["decaycm"][:, :], "d7")
        ctx.dma(e, cvec[:, SEL * 16:SEL * 16 + 4], W["selread"][:, :], "d7")
        src = V(W["Bp"], 0, [[16, 128], [128 * 16, 16], [1, 16]])
        ctx.dma(e, bp_sb.ap().rearrange("p (g j) -> p g j", g=16, j=16), src, "d7")
        src = V(W["CpT"], 0, [[16, 128], [128 * 16, 16], [1, 16]])
        ctx.dma(e, cpt_sb.ap().rearrange("p (g j) -> p g j", g=16, j=16), src, "d7")
        ctx.dma(e, boutr[:, :], W["boutrow"][:, :], "d7")
        e.wait_ge(ctx.sems["d7"], ctx.cnt["d7"])
    ctx.sec("gpsimd", _pl)
    pl_tot = ("d7", ctx.cnt["d7"])

    def _gc(e):
        ctx.wp(e, pl_tot)
        e.memset(p9[:, :], DECAY)
        for g in range(NG):
            e.tensor_mul(t16[:, :], bp_sb[:, 16 * g:16 * (g + 1)],
                         cpt_sb[:, 16 * g:16 * (g + 1)])
            e.reduce_sum(gcol[:, g:g + 1], t16[:, :], AX.X)
    ctx.sec("vector", _gc)
    gcol_done = ctx.cnt["dve"]

    esL = ExitStack()
    lnx2 = sal(ctx, esL, "s_lnx2", [128, 8 * 2048], BF16)
    esA = ExitStack()
    ptr = pal(ctx, esA, "s_ptr", [128, DM], BF16)
    ln_done = [0] * NT
    sq_act = [0] * NT
    pe_tr = [0] * NT
    dve_lnx = [0] * NT
    ld_x = [None] * NT
    for j in range(NT):
        par = j % 2

        def _ld(e, j=j, par=par):
            if j == 0:
                for sv in drains:
                    ctx.wp(e, sv)
            if j >= 2:
                ctx.w(e, "dve", ln_done[j - 2])
                ctx.w(e, "act", sq_act[j - 2])
            ld_x[j] = ctx.dma(e, xt[par][:, :],
                              x_in[128 * j:128 * (j + 1), :], f"d{par}")
        ctx.sec("sync", _ld)

        d = emit_ln(ctx, lnS, xt[par][:, :], lnt[par][:, :], [ld_x[j]],
                    norm_waits=([("pe", pe_tr[j - 2])] if j >= 2 else ()))
        ln_done[j] = d
        sq_act[j] = lnS.last_sq_act

        def _tr(e, j=j, par=par, d=d):
            ctx.w(e, "dve", d)
            if j >= 1:
                ctx.w(e, "dve", dve_lnx[j - 1])
            last = None
            for b in range(8):
                last = e.transpose(ptr[:, 128 * b:128 * (b + 1)],
                                   lnt[par][:, 128 * b:128 * (b + 1)],
                                   ident[:, :])
            ctx.inc(last, "pe")
        ctx.sec("tensor", _tr)
        pe_tr[j] = ctx.cnt["pe"]

        def _ev(e, j=j):
            ctx.w(e, "pe", pe_tr[j])
            dst = V(lnx2, 128 * j, [[8 * 2048, 128], [2048, 8], [1, 128]])
            e.tensor_copy(dst, ptr[:, :])
        ctx.sec("vector", _ev)
        dve_lnx[j] = ctx.cnt["dve"]
    lnx_done = dve_lnx[NT - 1]
    esA.close()

    esB = ExitStack()
    xi_g = [sal(ctx, esB, f"s_xi{i}", [128, 2048], F32) for i in range(2)]
    a_g = sal(ctx, esB, "s_a", [128, 2048], F32)
    t1 = sal(ctx, esB, "s_t1", [128, 2048], F32)
    wbuf = [sal(ctx, esB, f"s_wbuf{i}", [128, 8 * 128], BF16) for i in range(2)]
    pxi = [pal(ctx, esB, f"s_pxi{i}", [128, 2048], F32) for i in range(2)]
    ld_w = [None] * NG
    pe_xi = [0] * NG
    act_xi = [0] * NG
    dve_y = [0] * NG
    for g in range(NG):
        par = g % 2

        def _wl(e, g=g, par=par):
            if g >= 2:
                ctx.w(e, "pe", pe_xi[g - 2])
            else:
                ctx.w(e, "pe", getattr(ctx, "wgate", 0))
            src = V(W["Win"], 128 * g, [[DIN, 128], [128 * DIN, 8], [1, 128]])
            dst = wbuf[par].ap().rearrange("p (k c) -> p k c", k=8, c=128)
            ld_w[g] = ctx.dma(e, dst, src, f"d{6 + par}")
        ctx.sec("gpsimd", _wl)

        def _mm(e, g=g, par=par):
            if g == 0:
                ctx.w(e, "dve", lnx_done)
            ctx.wp(e, ld_w[g])
            if g >= 2:
                ctx.w(e, "act", act_xi[g - 2])
            last = None
            for ki in range(8):
                lhsT = wbuf[par][:, ki * 128:(ki + 1) * 128]
                for h in range(4):
                    last = e.matmul(pxi[g % 2][:, 512 * h:512 * (h + 1)], lhsT,
                                    V(lnx2, ki * 2048 + 512 * h,
                                      [[8 * 2048, 128], [1, 512]]),
                                    start=(ki == 0), stop=(ki == 7),
                                    skip_group_check=True)
            ctx.inc(last, "pe")
        ctx.sec("tensor", _mm)
        pe_xi[g] = ctx.cnt["pe"]

        def _xe(e, g=g):
            ctx.w(e, "pe", pe_xi[g])
            if g == 0:
                ctx.wp(e, pl_tot)
            if g >= 2:
                ctx.w(e, "dve", dve_y[g - 2])
            e.activation(out=xi_g[g % 2][:, :], in_=pxi[g % 2][:, :],
                         func=AF.Silu,
                         bias=cvec[:, BIN * 16 + g:BIN * 16 + g + 1])
        ctx.sec("scalar", _xe)
        act_xi[g] = ctx.cnt["act"]

        def _sy(e, g=g):
            ctx.w(e, "act", act_xi[g])
            d0 = V(p9, 0, [[1, 128], [0, 2048]])
            e.tensor_tensor_scan(a_g[:, :], d0, xi_g[g % 2][:, :], 0.0,
                                 ALU.mult, ALU.add)
            e.tensor_copy(u_buf[:, g:g + 1], a_g[:, 2047:2048])
            e.tensor_scalar(t1[:, :], xi_g[g % 2][:, :],
                            cvec[:, DP * 16 + g:DP * 16 + g + 1], None, ALU.mult)
            e.scalar_tensor_tensor(y_sb[:, 2048 * g:2048 * (g + 1)], a_g[:, :],
                                   gcol[:, g:g + 1], t1[:, :], ALU.mult, ALU.add)
        ctx.sec("vector", _sy)
        dve_y[g] = ctx.cnt["dve"]
    ctx.wgate = pe_xi[NG - 1]
    esB.close()
    esL.close()

    def _ust(e):
        ctx.w(e, "dve", dve_y[NG - 1])
        sv = ctx.dma(e, uag_in[:, :], u_buf[:, :], "d2")
        ctx.wp(e, sv)
        ctx.inc(e.nop(), "act")
    ctx.sec("scalar", _ust)
    act_ust = ctx.cnt["act"]

    def _ag(e):
        ctx.w(e, "act", act_ust)
        ctx.inc(e.collective_compute(
            "AllGather", ALU.bypass,
            replica_groups=[[0, 1, 2, 3], [4, 5, 6, 7]],
            ins=[uag_in[:, :]], outs=[uag_out[:, :]]), "cc")
        e.wait_ge(ctx.sems["cc"], ctx.cnt["cc"])
        src = V(uag_out, 0, [[16, 128], [128 * 16, 4], [1, 16]])
        sv = ctx.dma(e, uago.ap().rearrange("p (r j) -> p r j", r=4, j=16),
                     src, "d7")
        ctx.wp(e, sv)
        ctx.inc(e.engine_nop(), "pool")
    ctx.sec("gpsimd", _ag)
    pool_ag = ctx.cnt["pool"]

    def _sel(e):
        ctx.w(e, "pool", pool_ag)
        selv = V(cvec, SEL * 16, [[6 * 16, 128], [1, 4], [0, 16]])
        e.tensor_mul(uago[:, :], uago[:, :], selv)
        rin = V(uago, 0, [[4 * 16, 128], [1, 16], [16, 4]])
        e.reduce_sum(cvec[:, UPV * 16:UPV * 16 + 16], rin, AX.X)
        e.tensor_mul(cvec[:, UG * 16:UG * 16 + 16],
                     cvec[:, UPV * 16:UPV * 16 + 16], gcol[:, :])
        e.tensor_copy(ugb[:, :], cvec[:, UG * 16:UG * 16 + 16])
    ctx.sec("vector", _sel)
    dve_ug = ctx.cnt["dve"]

    esD = ExitStack()
    wo_sb = sal(ctx, esD, "s_wo", [128, NG * 1024], BF16)
    o_tm = sal(ctx, esD, "s_otm", [128, DM], F32)
    lnn = sal(ctx, esD, "s_lnn", [128, DM], F32)
    x3t = [sal(ctx, esD, f"s_x3{i}", [128, DM], F32) for i in range(2)]
    pso = [pal(ctx, esD, f"s_pso{i}", [128, DM], F32) for i in range(2)]
    pw1 = pal(ctx, esD, "s_pw1", [1, DM], F32)

    def _wol(e):
        ctx.w(e, "pe", ctx.wgate)
        src = V(W["Wout"], 0, [[DM, 128], [128 * DM, NG], [1, DM]])
        sv = ctx.dma(e, wo_sb.ap().rearrange("p (g c) -> p g c", g=NG, c=DM),
                     src, "d6")
        ld = sv
        ctx.pending_wo = ld
    ctx.sec("gpsimd", _wol)
    ld_wo = ctx.pending_wo

    def _wr(e):
        ctx.wp(e, ld_wo)
        ctx.w(e, "dve", dve_ug)
        last = None
        for g in range(NG):
            for h in range(2):
                last = e.matmul(pw1[0:1, 512 * h:512 * (h + 1)],
                                ugb[:, g:g + 1],
                                wo_sb[:, 1024 * g + 512 * h:1024 * g + 512 * (h + 1)],
                                start=(g == 0), stop=(g == NG - 1),
                                skip_group_check=True)
        ctx.inc(last, "pe")
    ctx.sec("tensor", _wr)
    pe_wr = ctx.cnt["pe"]

    def _wre(e):
        ctx.w(e, "pe", pe_wr)
        e.tensor_copy(wrow[0:1, :], pw1[0:1, :])
    ctx.sec("vector", _wre)
    dve_wre = ctx.cnt["dve"]

    def _wrb(e):
        ctx.w(e, "dve", dve_wre)
        last = None
        for h in range(2):
            last = e.matmul(pso[1][:, 512 * h:512 * (h + 1)], ones1[0:1, :],
                            wrow[0:1, 512 * h:512 * (h + 1)],
                            start=True, stop=True, skip_group_check=True)
        ctx.inc(last, "pe")
    ctx.sec("tensor", _wrb)
    pe_wrb = ctx.cnt["pe"]

    def _wbe(e):
        ctx.w(e, "pe", pe_wrb)
        e.tensor_copy(wb_sb[:, :], pso[1][:, :])
    ctx.sec("vector", _wbe)
    dve_wb = ctx.cnt["dve"]

    pe_sl = [0] * NT
    dve_out = [0] * NT
    ld_x2 = [None] * NT
    st_x3 = [None] * NT
    for sl in range(NT):
        par = sl % 2

        def _ld(e, sl=sl, par=par):
            if sl >= 2:
                ctx.w(e, "dve", dve_out[sl - 2])
            else:
                ctx.w(e, "dve", ln_done[NT - 2 + sl])
                ctx.w(e, "act", sq_act[NT - 2 + sl])
            ld_x2[sl] = ctx.dma(e, xt[par][:, :],
                                x_in[128 * sl:128 * (sl + 1), :], f"d{par}")
        ctx.sec("sync", _ld)

        def _mm(e, sl=sl):
            if sl <= 1:
                ctx.w(e, "dve", dve_wb)
            if sl >= 2:
                ctx.w(e, "dve", dve_out[sl - 2])
            last = None
            for g in range(NG):
                lhsT = y_sb[:, 2048 * g + 128 * sl:2048 * g + 128 * (sl + 1)]
                for h in range(2):
                    last = e.matmul(pso[sl % 2][:, 512 * h:512 * (h + 1)], lhsT,
                                    wo_sb[:, 1024 * g + 512 * h:1024 * g + 512 * (h + 1)],
                                    start=(g == 0), stop=False,
                                    skip_group_check=True)
            for h in range(2):
                last = e.matmul(pso[sl % 2][:, 512 * h:512 * (h + 1)], ones1[0:1, :],
                                boutr[0:1, 512 * h:512 * (h + 1)],
                                start=False, stop=(h == 1),
                                skip_group_check=True)
            ctx.inc(last, "pe")
        ctx.sec("tensor", _mm)
        pe_sl[sl] = ctx.cnt["pe"]

        def _oe(e, sl=sl):
            ctx.w(e, "pe", pe_sl[sl])
            ctx.w(e, "act", lnSo.last_sq_act)
            e.scalar_tensor_tensor(o_tm[:, :], wb_sb[:, :],
                                   cvec[:, DEC * 16 + sl:DEC * 16 + sl + 1],
                                   pso[sl % 2][:, :], ALU.mult, ALU.add)
        ctx.sec("vector", _oe)
        dve_oe = ctx.cnt["dve"]

        d = emit_ln(ctx, lnSo, o_tm[:, :], lnn[:, :],
                    [("dve", dve_oe), ld_x2[sl]],
                    res_add_ap=xt[par][:, :], dst2=x3t[par][:, :],
                    norm_waits=([st_x3[sl - 2]] if sl >= 2 else ()))
        dve_out[sl] = d

        def _st(e, sl=sl, par=par):
            ctx.w(e, "dve", dve_out[sl])
            st_x3[sl] = ctx.dma(e, x_out[128 * sl:128 * (sl + 1), :],
                                x3t[par][:, :], f"d{4 + par}")
        ctx.sec("scalar", _st)

    ctx.pending = st_x3[-2:]
    ctx.wgate = pe_sl[NT - 1]
    ctx.barrier(drains=st_x3[-2:])
    esD.close()
    es.close()


def emit_l1f(ctx, W, x_in, y_out, ident, ones1, xp_in, xp_out, c_dr, yt_in,
             yt_out):
    nc = ctx.nc
    drains = getattr(ctx, "pending", [])
    ctx.pending = []
    es = ExitStack()
    xt = [sal(ctx, es, f"m_xt{i}", [128, DM], F32) for i in range(2)]
    lnt = sal(ctx, es, "m_lnt", [128, DM], BF16)
    rS = sal(ctx, es, "m_rS", [128, TC], F32)
    phib = sal(ctx, es, "m_phib", [1, NP], BF16)
    phibc = sal(ctx, es, "m_phibc", [128, 16], F32)
    sel32 = sal(ctx, es, "m_sel32", [128, 32], BF16)
    rep32 = sal(ctx, es, "m_rep32", [32, 128], BF16)
    oav = sal(ctx, es, "m_oav", [128, 128], BF16)
    lnM = LNState(ctx, es, "m")
    lnMo = LNState(ctx, es, "mo")

    def _pl(e):
        ctx.dma(e, phib[:, :], W["phibrow"][:, :], "d7")
        ctx.dma(e, phibc[:, :], W["phibcm"][:, :], "d7")
        ctx.dma(e, sel32[:, :], W["sel32"][:, :], "d7")
        ctx.dma(e, rep32[:, :], W["rep32"][:, :], "d7")
        ctx.dma(e, oav[:, :], W["oav"][:, :], "d7")
        e.wait_ge(ctx.sems["d7"], ctx.cnt["d7"])
    ctx.sec("gpsimd", _pl)
    pl_tot = ("d7", ctx.cnt["d7"])

    es1 = ExitStack()
    phit = sal(ctx, es1, "m_phit", [128, 8 * NP], BF16)
    lnx3fm = sal(ctx, es1, "m_lnx3fm", [128, 8 * TC], BF16)

    def _pwl(e):
        ctx.w(e, "pe", getattr(ctx, "wgate", 0))
        src = V(W["phiT"], 0, [[NP, 128], [128 * NP, 8], [1, NP]])
        ctx.phit_ld = ctx.dma(
            e, phit.ap().rearrange("p (k c) -> p k c", k=8, c=NP), src, "d6")
    ctx.sec("gpsimd", _pwl)
    phit_ld = ctx.phit_ld

    es2 = ExitStack()
    lnx3tm = sal(ctx, es2, "m_lnx3tm", [128, NT * DM], BF16)
    esA = ExitStack()
    ptr = pal(ctx, esA, "m_ptr", [128, DM], BF16)
    ln_done = [0] * NT
    sq_act = [0] * NT
    pe_tr = [0] * NT
    dve_fm = [0] * NT
    ld_x = [None] * NT
    for j in range(NT):
        par = j % 2

        def _ld(e, j=j, par=par):
            if j == 0:
                for sv in drains:
                    ctx.wp(e, sv)
            if j >= 2:
                ctx.w(e, "dve", ln_done[j - 2])
                ctx.w(e, "act", sq_act[j - 2])
            ld_x[j] = ctx.dma(e, xt[par][:, :],
                              x_in[128 * j:128 * (j + 1), :], f"d{par}")
        ctx.sec("sync", _ld)

        d = emit_ln(ctx, lnM, xt[par][:, :], lnx3tm[:, DM * j:DM * (j + 1)],
                    [ld_x[j]], norm_waits=([("pe", pe_tr[j - 2])] if j >= 2 else ()))
        ln_done[j] = d
        sq_act[j] = lnM.last_sq_act

        def _tr(e, j=j, d=d):
            ctx.w(e, "dve", d)
            if j >= 1:
                ctx.w(e, "dve", dve_fm[j - 1])
            last = None
            for b in range(8):
                last = e.transpose(ptr[:, 128 * b:128 * (b + 1)],
                                   lnx3tm[:, DM * j + 128 * b:DM * j + 128 * (b + 1)],
                                   ident[:, :])
            ctx.inc(last, "pe")
        ctx.sec("tensor", _tr)
        pe_tr[j] = ctx.cnt["pe"]

        def _ev(e, j=j):
            ctx.w(e, "pe", pe_tr[j])
            dst = V(lnx3fm, 128 * j, [[8 * TC, 128], [TC, 8], [1, 128]])
            e.tensor_copy(dst, ptr[:, :])
        ctx.sec("vector", _ev)
        dve_fm[j] = ctx.cnt["dve"]
    lnx_done = dve_fm[NT - 1]
    esA.close()

    es3 = ExitStack()
    D_sb = sal(ctx, es3, "m_D", [128, NT * NP], BF16)
    e_d = sal(ctx, es3, "m_ed", [128, NP], BF16)
    sd = sal(ctx, es3, "m_sd", [128, 2 * NEXP], F32)
    plog = pal(ctx, es3, "m_plog", [128, NP], F32)
    pe_lg = [0] * NT
    act_ex = [0] * NT
    dve_D = [0] * NT
    for I in range(NT):
        def _mm(e, I=I):
            if I == 0:
                ctx.w(e, "dve", lnx_done)
                ctx.wp(e, phit_ld)
                ctx.wp(e, pl_tot)
            if I >= 1:
                ctx.w(e, "act", act_ex[I - 1])
            last = None
            for ct in range(8):
                lhsT = V(lnx3fm, ct * TC + 128 * I, [[8 * TC, 128], [1, 128]])
                for h in range(4):
                    last = e.matmul(plog[:, 512 * h:512 * (h + 1)], lhsT,
                                    phit[:, ct * NP + 512 * h:ct * NP + 512 * (h + 1)],
                                    start=(ct == 0), stop=False,
                                    skip_group_check=True)
            for h in range(4):
                last = e.matmul(plog[:, 512 * h:512 * (h + 1)], ones1[0:1, :],
                                phib[0:1, 512 * h:512 * (h + 1)],
                                start=False, stop=(h == 3), skip_group_check=True)
            ctx.inc(last, "pe")
        ctx.sec("tensor", _mm)
        pe_lg[I] = ctx.cnt["pe"]

        def _ex(e, I=I):
            ctx.w(e, "pe", pe_lg[I])
            if I >= 1:
                ctx.w(e, "dve", dve_D[I - 1])
            e.activation(out=e_d[:, :], in_=plog[:, :], func=AF.Exp)
        ctx.sec("scalar", _ex)
        act_ex[I] = ctx.cnt["act"]

        def _D(e, I=I):
            ctx.w(e, "act", act_ex[I])
            rin = V(e_d, 0, [[NP, 128], [PSLOT, NEXP], [1, PSLOT]])
            e.reduce_sum(sd[:, 0:NEXP], rin, AX.X)
            e.reciprocal(sd[:, NEXP:2 * NEXP], sd[:, 0:NEXP])
            rv = V(sd, NEXP, [[2 * NEXP, 128], [1, NEXP], [0, PSLOT]])
            e.tensor_mul(D_sb[:, NP * I:NP * (I + 1)], e_d[:, :], rv)
        ctx.sec("vector", _D)
        dve_D[I] = ctx.cnt["dve"]

    es4 = ExitStack()
    xps = [sal(ctx, es4, f"m_xps{i}", [128, DM], F32) for i in range(2)]
    pxp = [pal(ctx, es4, f"m_pxp{i}", [128, DM], F32) for i in range(2)]
    pe_xp = [0] * 16
    dve_xp = [0] * 16
    st_xp = [None] * 16
    for sn in range(16):
        def _mm(e, sn=sn):
            if sn == 0:
                ctx.w(e, "dve", dve_D[NT - 1])
            if sn >= 2:
                ctx.w(e, "dve", dve_xp[sn - 2])
            last = None
            for I in range(NT):
                lhsT = D_sb[:, NP * I + 128 * sn:NP * I + 128 * (sn + 1)]
                for h in range(2):
                    last = e.matmul(pxp[sn % 2][:, 512 * h:512 * (h + 1)], lhsT,
                                    lnx3tm[:, DM * I + 512 * h:DM * I + 512 * (h + 1)],
                                    start=(I == 0), stop=(I == NT - 1),
                                    skip_group_check=True)
            ctx.inc(last, "pe")
        ctx.sec("tensor", _mm)
        pe_xp[sn] = ctx.cnt["pe"]

        def _ev(e, sn=sn):
            ctx.w(e, "pe", pe_xp[sn])
            if sn >= 2:
                ctx.wp(e, st_xp[sn - 2])
            e.tensor_copy(xps[sn % 2][:, :], pxp[sn % 2][:, :])
        ctx.sec("vector", _ev)
        dve_xp[sn] = ctx.cnt["dve"]

        def _st(e, sn=sn):
            ctx.w(e, "dve", dve_xp[sn])
            st_xp[sn] = ctx.dma(e, xp_in[128 * sn:128 * (sn + 1), :],
                                xps[sn % 2][:, :], f"d{2 + sn % 2}")
        ctx.sec("scalar", _st)
    ctx.wgate = pe_xp[15]

    def _rs(e):
        for sv in st_xp[-2:]:
            ctx.wp(e, sv)
        ctx.inc(e.collective_compute(
            "ReduceScatter", ALU.add, replica_groups=[[0, 1, 2, 3, 4, 5, 6, 7]],
            ins=[xp_in[:, :]], outs=[xp_out[:, :]]), "cc")
    ctx.sec("gpsimd", _rs)
    cc_rs = ctx.cnt["cc"]
    es4.close()
    es3.close()
    es2.close()

    es5 = ExitStack()
    e_fm = sal(ctx, es5, "m_efm", [128, TC], F32)
    e_bf = [sal(ctx, es5, f"m_ebf{i}", [128, TC], BF16) for i in range(2)]
    sC = sal(ctx, es5, "m_sC", [32, TC], BF16)
    pfm = pal(ctx, es5, "m_pfm", [128, TC], F32)
    psC = pal(ctx, es5, "m_psC", [32, TC], F32)
    pe_fm = [0] * 16
    act_fx = [0] * 16
    dve_eb = [0] * 16
    pe_sC = [0] * 16
    st_eb = [None] * 16
    for sn in range(16):
        par = sn % 2

        def _mm(e, sn=sn):
            if sn >= 1:
                ctx.w(e, "act", act_fx[sn - 1])
            last = None
            for ct in range(8):
                lhsT = V(phit, ct * NP + 128 * sn, [[8 * NP, 128], [1, 128]])
                for h in range(4):
                    last = e.matmul(pfm[:, 512 * h:512 * (h + 1)], lhsT,
                                    V(lnx3fm, ct * TC + 512 * h,
                                      [[8 * TC, 128], [1, 512]]),
                                    start=(ct == 0), stop=(ct == 7),
                                    skip_group_check=True)
            ctx.inc(last, "pe")
        ctx.sec("tensor", _mm)
        pe_fm[sn] = ctx.cnt["pe"]

        def _ex(e, sn=sn):
            ctx.w(e, "pe", pe_fm[sn])
            if sn >= 1:
                ctx.w(e, "dve", dve_eb[sn - 1])
            e.activation(out=e_fm[:, :], in_=pfm[:, :], func=AF.Exp,
                         bias=phibc[:, sn:sn + 1])
        ctx.sec("scalar", _ex)
        act_fx[sn] = ctx.cnt["act"]

        def _eb(e, sn=sn, par=par):
            ctx.w(e, "act", act_fx[sn])
            if sn >= 2:
                ctx.wp(e, st_eb[sn - 2])
                ctx.w(e, "pe", pe_sC[sn - 2])
            e.tensor_copy(e_bf[par][:, :], e_fm[:, :])
        ctx.sec("vector", _eb)
        dve_eb[sn] = ctx.cnt["dve"]

        def _st(e, sn=sn, par=par):
            ctx.w(e, "dve", dve_eb[sn])
            if sn < 2:
                ctx.wp(e, st_xp[14 + sn])
            st_eb[sn] = ctx.dma(e, c_dr[128 * sn:128 * (sn + 1), :],
                                e_bf[par][:, :], f"d{2 + par}")
        ctx.sec("scalar", _st)

        def _sc(e, sn=sn, par=par):
            ctx.w(e, "dve", dve_eb[sn])
            last = None
            for h in range(4):
                last = e.matmul(psC[0:32, 512 * h:512 * (h + 1)], sel32[:, :],
                                e_bf[par][:, 512 * h:512 * (h + 1)],
                                start=(sn == 0), stop=(sn == 15),
                                skip_group_check=True)
            ctx.inc(last, "pe")
        ctx.sec("tensor", _sc)
        pe_sC[sn] = ctx.cnt["pe"]

    def _sce(e):
        ctx.w(e, "pe", pe_sC[15])
        e.tensor_copy(sC[0:32, :], psC[0:32, :])
    ctx.sec("vector", _sce)
    dve_sC = ctx.cnt["dve"]

    def _scb(e):
        ctx.w(e, "dve", dve_sC)
        last = None
        for h in range(4):
            last = e.matmul(pfm[:, 512 * h:512 * (h + 1)], rep32[0:32, :],
                            sC[0:32, 512 * h:512 * (h + 1)],
                            start=True, stop=True, skip_group_check=True)
        ctx.inc(last, "pe")
    ctx.sec("tensor", _scb)
    pe_scb = ctx.cnt["pe"]

    def _rse(e):
        ctx.w(e, "pe", pe_scb)
        e.reciprocal(rS[:, :], pfm[:, :])
    ctx.sec("vector", _rse)
    dve_rS = ctx.cnt["dve"]
    ctx.barrier()
    es5.close()
    es1.close()

    es6 = ExitStack()
    xrs = sal(ctx, es6, "m_xrs", [128, DM], F32)
    xcl = sal(ctx, es6, "m_xcl", [128, DM], F32)
    xln = sal(ctx, es6, "m_xln", [128, DM], F32)
    xtfm = sal(ctx, es6, "m_xtfm", [128, 8 * 256], BF16)
    xth = sal(ctx, es6, "m_xth", [128, DM], BF16)
    ptr2 = pal(ctx, es6, "m_ptr2", [128, DM], BF16)
    pe_trx = [0] * 2
    dcl_h = [0] * 2
    for half in range(2):
        def _ld(e, half=half):
            if half == 0:
                e.wait_ge(ctx.sems["cc"], cc_rs)
            else:
                ctx.w(e, "dve", dcl_h[0])
            ld = ctx.dma(e, xrs[:, :], xp_out[128 * half:128 * (half + 1), :],
                         f"d{half}")
            ctx.ld_xrs = ld
        ctx.sec("sync", _ld)
        ld_xrs = ctx.ld_xrs

        def _cl(e, half=half, ld=ld_xrs):
            ctx.wp(e, ld)
            if half == 1:
                ctx.w(e, "pe", pe_trx[0])
                ctx.w(e, "act", lnM.last_sq_act)
            e.tensor_scalar(xcl[:, :], xrs[:, :], -33000.0, 65000.0,
                            ALU.max, ALU.min)
        ctx.sec("vector", _cl)
        d_cl = ctx.cnt["dve"]
        dcl_h[half] = d_cl

        d = emit_ln(ctx, lnM, xcl[:, :], xln[:, :], [("dve", d_cl)])

        def _th(e, half=half, d=d):
            ctx.w(e, "dve", d)
            e.activation(out=xth[:, :], in_=xln[:, :], func=AF.Tanh)
        ctx.sec("scalar", _th)
        a_th = ctx.cnt["act"]

        def _tr(e, half=half, a=a_th):
            ctx.w(e, "act", a)
            last = None
            for b in range(8):
                last = e.transpose(ptr2[:, 128 * b:128 * (b + 1)],
                                   xth[:, 128 * b:128 * (b + 1)], ident[:, :])
            ctx.inc(last, "pe")
        ctx.sec("tensor", _tr)
        pe_trx[half] = ctx.cnt["pe"]

        def _ev(e, half=half):
            ctx.w(e, "pe", pe_trx[half])
            dst = V(xtfm, 128 * half, [[8 * 256, 128], [256, 8], [1, 128]])
            e.tensor_copy(dst, ptr2[:, :])
        ctx.sec("vector", _ev)
    dve_xt = ctx.cnt["dve"]

    ebias = sal(ctx, es6, "m_ebias", [1, 8 * DM], BF16)
    wex = [sal(ctx, es6, f"m_wex{i}", [128, 8 * DM], BF16) for i in range(2)]
    yrow = [sal(ctx, es6, f"m_yrow{i}", [32, DM], BF16) for i in range(2)]
    pY = pal(ctx, es6, "m_pY", [32, DM], F32)

    def _ebl(e):
        ctx.dma(e, ebias[:, :], W["eb_shard"][:, :], "d7")
        e.wait_ge(ctx.sems["d7"], ctx.cnt["d7"])
    ctx.sec("gpsimd", _ebl)
    eb_tot = ("d7", ctx.cnt["d7"])

    ld_we = [None] * 8
    pe_Y = [0] * 8
    dve_Y = [0] * 8
    st_Y = [None] * 8
    for eL in range(8):
        par = eL % 2

        def _wl(e, eL=eL, par=par):
            if eL >= 2:
                ctx.w(e, "pe", pe_Y[eL - 2])
            else:
                ctx.w(e, "pe", ctx.wgate)
            src = V(W["ew_shard"], eL * DM * DM,
                    [[DM, 128], [128 * DM, 8], [1, DM]])
            ld_we[eL] = ctx.dma(
                e, wex[par].ap().rearrange("p (k c) -> p k c", k=8, c=DM),
                src, f"d{6 + par}")
        ctx.sec("gpsimd", _wl)

        def _mm(e, eL=eL, par=par):
            if eL == 0:
                ctx.w(e, "dve", dve_xt)
                ctx.wp(e, eb_tot)
            ctx.wp(e, ld_we[eL])
            if eL >= 1:
                ctx.w(e, "dve", dve_Y[eL - 1])
            last = None
            for ds in range(8):
                lhsT = V(xtfm, ds * 256 + 32 * eL, [[8 * 256, 128], [1, 32]])
                for h in range(2):
                    last = e.matmul(pY[0:32, 512 * h:512 * (h + 1)], lhsT,
                                    wex[par][:, ds * DM + 512 * h:ds * DM + 512 * (h + 1)],
                                    start=(ds == 0), stop=False,
                                    skip_group_check=True)
            for h in range(2):
                last = e.matmul(pY[0:32, 512 * h:512 * (h + 1)], ones1[0:1, 0:32],
                                ebias[0:1, eL * DM + 512 * h:eL * DM + 512 * (h + 1)],
                                start=False, stop=(h == 1), skip_group_check=True)
            ctx.inc(last, "pe")
        ctx.sec("tensor", _mm)
        pe_Y[eL] = ctx.cnt["pe"]

        def _ev(e, eL=eL):
            ctx.w(e, "pe", pe_Y[eL])
            if eL >= 2:
                ctx.wp(e, st_Y[eL - 2])
            e.tensor_copy(yrow[eL % 2][0:32, :], pY[0:32, :])
        ctx.sec("vector", _ev)
        dve_Y[eL] = ctx.cnt["dve"]

        def _st(e, eL=eL):
            ctx.w(e, "dve", dve_Y[eL])
            if eL < 2:
                ctx.wp(e, st_eb[14 + eL])
            st_Y[eL] = ctx.dma(e, yt_in[32 * eL:32 * (eL + 1), :],
                               yrow[eL % 2][0:32, :], f"d{2 + eL % 2}")
        ctx.sec("scalar", _st)

    def _ag(e):
        for sv in st_Y[-2:]:
            ctx.wp(e, sv)
        ctx.inc(e.collective_compute(
            "AllGather", ALU.bypass, replica_groups=[[0, 1, 2, 3, 4, 5, 6, 7]],
            ins=[yt_in[:, :]], outs=[yt_out[:, :]]), "cc")
    ctx.sec("gpsimd", _ag)
    cc_ag = ctx.cnt["cc"]
    es6.close()

    es7 = ExitStack()
    C_all = sal(ctx, es7, "m_C", [128, 16 * TC], BF16)
    ytk = sal(ctx, es7, "m_ytk", [128, NT * DM], BF16)
    ceb = [sal(ctx, es7, f"m_ceb{i}", [128, TC], BF16) for i in range(2)]
    o_tm = sal(ctx, es7, "m_otm", [128, DM], F32)
    lnn = sal(ctx, es7, "m_lnn", [128, DM], F32)
    x4t = [sal(ctx, es7, f"m_x4{i}", [128, DM], F32) for i in range(2)]
    pY2 = [pal(ctx, es7, f"m_pY2{i}", [128, DM], F32) for i in range(2)]

    def _ytl(e):
        e.wait_ge(ctx.sems["cc"], cc_ag)
        src = V(yt_out, 0, [[DM, 128], [128 * DM, NT], [1, DM]])
        ctx.ld_ytk = ctx.dma(
            e, ytk.ap().rearrange("p (g c) -> p g c", g=NT, c=DM), src, "d6")
    ctx.sec("gpsimd", _ytl)
    ld_ytk = ctx.ld_ytk

    ld_eb2 = [None] * 16
    dve_C = [0] * 16
    for sn in range(16):
        par = sn % 2

        def _ld(e, sn=sn, par=par):
            if sn >= 2:
                ctx.w(e, "dve", dve_C[sn - 2])
            else:
                ctx.w(e, "dve", dcl_h[1])
                ctx.wp(e, st_eb[14 + sn])
            ld_eb2[sn] = ctx.dma(e, ceb[par][:, :],
                                 c_dr[128 * sn:128 * (sn + 1), :], f"d{par}")
        ctx.sec("sync", _ld)

        def _C(e, sn=sn, par=par):
            ctx.wp(e, ld_eb2[sn])
            if sn == 0:
                ctx.w(e, "dve", dve_rS)
            e.tensor_mul(C_all[:, TC * sn:TC * (sn + 1)], ceb[par][:, :],
                         rS[:, :])
        ctx.sec("vector", _C)
        dve_C[sn] = ctx.cnt["dve"]

    pe_Y2 = [0] * NT
    dve_o = [0] * NT
    ld_x3 = [None] * NT
    st_x4 = [None] * NT
    for sl in range(NT):
        par = sl % 2

        def _ld(e, sl=sl, par=par):
            if sl >= 2:
                ctx.w(e, "dve", dve_o[sl - 2])
            else:
                ctx.w(e, "dve", ln_done[NT - 2 + sl])
                ctx.w(e, "act", sq_act[NT - 2 + sl])
                e.wait_ge(ctx.sems["cc"], cc_ag)
            ld_x3[sl] = ctx.dma(e, xt[par][:, :],
                                x_in[128 * sl:128 * (sl + 1), :], f"d{2 + par}")
        ctx.sec("sync", _ld)

        def _mm(e, sl=sl):
            if sl == 0:
                ctx.w(e, "dve", dve_C[15])
                ctx.wp(e, ld_ytk)
            if sl >= 2:
                ctx.w(e, "dve", dve_o[sl - 2])
            last = None
            for sn in range(16):
                lhsT = C_all[:, TC * sn + 128 * sl:TC * sn + 128 * (sl + 1)]
                for h in range(2):
                    last = e.matmul(pY2[sl % 2][:, 512 * h:512 * (h + 1)], lhsT,
                                    ytk[:, DM * sn + 512 * h:DM * sn + 512 * (h + 1)],
                                    start=(sn == 0), stop=(sn == 15),
                                    skip_group_check=True)
            ctx.inc(last, "pe")
        ctx.sec("tensor", _mm)
        pe_Y2[sl] = ctx.cnt["pe"]

        def _oe(e, sl=sl):
            ctx.w(e, "pe", pe_Y2[sl])
            ctx.w(e, "act", lnMo.last_sq_act)
            e.tensor_copy(o_tm[:, :], pY2[sl % 2][:, :])
        ctx.sec("vector", _oe)
        dve_oe = ctx.cnt["dve"]

        d = emit_ln(ctx, lnMo, o_tm[:, :], lnn[:, :],
                    [("dve", dve_oe), ld_x3[sl]],
                    res_add_ap=xt[par][:, :], dst2=x4t[par][:, :],
                    norm_waits=([st_x4[sl - 2]] if sl >= 2 else ()))
        dve_o[sl] = d

        def _st(e, sl=sl, par=par):
            ctx.w(e, "dve", dve_o[sl])
            st_x4[sl] = ctx.dma(e, y_out[128 * sl:128 * (sl + 1), :],
                                x4t[par][:, :], f"d{4 + par}")
        ctx.sec("scalar", _st)

    ctx.barrier(drains=st_x4[-2:])
    es7.close()
    es.close()


from concourse.bass_utils import run_bass_kernel_spmd

_CACHE = {}


def kernel(**inputs):
    sh, pc = prep_inputs(inputs)
    if "nc" not in _CACHE:
        _CACHE["nc"] = build(("l0a", "l0f", "l1a", "l1f"))
    nc = _CACHE["nc"]
    declared = set(nc._mk_W.keys()) | {"x_ext"}
    in_maps = [{k: v for k, v in dict(sh, **pc[c]).items() if k in declared}
               for c in range(NCORES)]
    res = run_bass_kernel_spmd(nc, in_maps, core_ids=list(range(NCORES)))
    out = np.concatenate([np.asarray(res.results[c]["y_out"])
                          for c in range(NCORES)], axis=0)
    return out.reshape(2, 8192, DM).astype(np.float32)
